# version 16
# baseline (speedup 1.0000x reference)
import numpy as np
import concourse.bass as bass
import concourse.mybir as mybir

F32 = mybir.dt.float32
BF16 = mybir.dt.bfloat16
FP16 = mybir.dt.float16
F32R = mybir.dt.float32r
FP8 = mybir.dt.float8e4
I32 = mybir.dt.int32
AF = mybir.ActivationFunctionType
ALU = mybir.AluOpType

D = 1024
PC = 8
H = 16
FF = 4096
FC = 32
T = 256
CH = 128
EPS = 1e-5
MAGIC = float(0x5f3759df)

ENGINES = ("sync", "tensor", "vector", "scalar", "gpsimd")
OWN = {"tensor": "sPE", "vector": "sDVE", "scalar": "sACT", "gpsimd": "sPOOL"}
SEMS = ["SX", "SM", "SO", "SPR", "SPR1", "SPR2", "SPR3",
        "sPE", "sDVE", "sACT", "sPOOL"]


class Sched:
    def __init__(self):
        self.q = {e: [] for e in ENGINES}
        self.cnt = {}
        self.hw = {e: {} for e in ENGINES}

    def emit(self, eng, fn, waits=(), inc=None, amt=1):
        real = []
        flat = []
        for w in waits:
            if w is None:
                continue
            if isinstance(w, (list, tuple)) and w and isinstance(w[0], (list, tuple)):
                flat.extend(w)
            elif isinstance(w, (list, tuple)) and len(w) == 2 and isinstance(w[0], str):
                flat.append(w)
            elif isinstance(w, (list, tuple)):
                flat.extend(x for x in w if x is not None)
            else:
                raise ValueError(f"bad wait {w}")
        own = OWN.get(eng)
        if own is not None and eng != "tensor":
            prev = self.cnt.get(own, 0)
            if prev > 0:
                flat.append((own, prev))
        for w in flat:
            if w is None:
                continue
            s, v = w
            if s == own and self.cnt.get(own, 0) >= v and \
                    self.hw[eng].get(s, 0) >= v:
                continue
            if self.hw[eng].get(s, 0) >= v:
                continue
            self.hw[eng][s] = v
            real.append((s, v))
        tok = None
        if inc is not None:
            self.cnt[inc] = self.cnt.get(inc, 0) + amt
            tok = (inc, self.cnt[inc])
        self.q[eng].append((fn, real, inc, amt))
        return tok


def build_nc(ntok):
    nt = ntok // T
    nc = bass.Bass("TRN2", target_bir_lowering=False)

    dxT = nc.dram_tensor("xT", [D, ntok], F32, kind="ExternalInput").ap()
    dxpre = nc.dram_tensor("xpre", [D, CH], F32, kind="ExternalInput").ap()
    dwk = nc.dram_tensor("wkT", [D, D], F32, kind="ExternalInput").ap()
    dwv = nc.dram_tensor("wvT", [D, D], F32, kind="ExternalInput").ap()
    dwr = nc.dram_tensor("wrT", [D, D], F32, kind="ExternalInput").ap()
    dwo = nc.dram_tensor("woT", [D, D], F32, kind="ExternalInput").ap()
    dw1 = nc.dram_tensor("w1T", [D, FF], F32, kind="ExternalInput").ap()
    dw2 = nc.dram_tensor("w2T", [FF, D], F32, kind="ExternalInput").ap()
    dtd = nc.dram_tensor("td", [H, 64], F32, kind="ExternalInput").ap()
    dtf = nc.dram_tensor("tf", [H, 64], F32, kind="ExternalInput").ap()
    db1 = nc.dram_tensor("b1", [FF], F32, kind="ExternalInput").ap()
    db2 = nc.dram_tensor("b2", [D], F32, kind="ExternalInput").ap()
    dmsk = nc.dram_tensor("maskT", [D, ntok], FP8, kind="ExternalInput").ap()
    didn = nc.dram_tensor("idn", [128, 128], BF16, kind="ExternalInput").ap()
    didnf = nc.dram_tensor("idnf", [128, 128], F32, kind="ExternalInput").ap()
    dconc = nc.dram_tensor("onesc", [128, 1], BF16, kind="ExternalInput").ap()
    dconrf = nc.dram_tensor("onesrf", [1, 128], F32, kind="ExternalInput").ap()
    ddif = nc.dram_tensor("dif", [128, 128], F32, kind="ExternalInput").ap()
    drev = nc.dram_tensor("rev", [128, 128], F32, kind="ExternalInput").ap()
    diot = nc.dram_tensor("iot", [128, 128], F32, kind="ExternalInput").ap()
    dc128 = nc.dram_tensor("c128", [128, 1], F32, kind="ExternalInput").ap()
    dout = nc.dram_tensor("outT", [D, ntok], F32, kind="ExternalOutput").ap()

    A = lambda name, w, dt: nc.alloc_sbuf_tensor(name, [128, w], dt).ap()
    XT = A("sXT", 2048, F32)
    XBF = A("sXBF", 2048, BF16)
    KP = A("sKP", 2048, BF16)
    RS = A("sRS", 2048, BF16)
    VT = A("sVT", 1024, BF16)
    KHT = A("sKHT", 1024, BF16)
    TKT = A("sTKT", 1024, BF16)
    RT = A("sRT", 1024, BF16)
    YF = A("sYF", 2048, BF16)
    XR = A("sXR", 2048, F32R)
    X2B = A("sX2B", 2048, BF16)
    X2Q = A("sX2Q", 2048, FP8)
    ONES2R = A("sONES2R", 2, F32R)
    HQ = A("sHQ", 8192, FP8)
    H16 = A("sH16", 256, FP16)
    TMP1 = A("sTMP1", 256, BF16)
    AT = A("sAT", 2048, BF16)
    MSK = A("sMSK", 1024, FP8)
    OUT = A("sOUT", 1024, F32)
    WKs = A("sWK", 8192, BF16)
    WVs = A("sWV", 8192, BF16)
    WRs = A("sWR", 8192, BF16)
    WOs = A("sWO", 8192, BF16)
    W1S = A("sW1S", 32768, FP8)
    W2S = A("sW2S", 32768, FP8)
    TB = A("sTB", 2048, BF16)
    _ovk, _ = nc.bump_sbuf(2048)
    TK = nc.alloc_sbuf_tensor_at("sTK", [128, 1024], BF16, offset=_ovk).ap()
    MSK2 = nc.alloc_sbuf_tensor_at("sMSK2", [128, 1024], FP8, offset=_ovk).ap()
    H16B = nc.alloc_sbuf_tensor_at("sH16B", [128, 256], FP16,
                                   offset=_ovk + 1024).ap()
    TRt = A("sTR", 1024, BF16)
    WC = A("sWC", 8, F32)
    UC = A("sUC", 8, F32)
    LNW = A("sLNW", 8, F32)
    LNWB = A("sLNWB", 16, F32)
    St = A("sS", 1024, BF16)
    B1C = A("sB1C", 32, F32)
    B2C = A("sB2C", 8, F32)
    B1Q = A("sB1Q", 32, FP8)
    B2Q = A("sB2Q", 8, FP8)
    _ov2, _ = nc.bump_sbuf(2048)
    B1R = nc.alloc_sbuf_tensor_at("sB1R", [32, 128], F32, offset=_ov2).ap()
    B2R = nc.alloc_sbuf_tensor_at("sB2R", [8, 128], F32, offset=_ov2 + 512).ap()
    TDR = nc.alloc_sbuf_tensor_at("sTDR", [1, 16], F32, offset=_ov2 + 1024).ap()
    ABRROW = nc.alloc_sbuf_tensor_at("sABRROW", [1, 512], F32, offset=_ov2).ap()
    IDN = A("sIDN", 128, BF16)
    IDNF = A("sIDNF", 128, F32)
    ONESC = A("sONESC", 1, BF16)
    ONESRF = nc.alloc_sbuf_tensor("sONESRF", [1, 128], F32).ap()
    _ovl, _ = nc.bump_sbuf(2048)
    DIF = nc.alloc_sbuf_tensor_at("sDIF", [128, 128], F32, offset=_ovl).ap()
    REV = nc.alloc_sbuf_tensor_at("sREV", [128, 128], F32, offset=_ovl + 512).ap()
    IOT = nc.alloc_sbuf_tensor_at("sIOT", [128, 128], F32, offset=_ovl + 1024).ap()
    XSQ = nc.alloc_sbuf_tensor_at("sXSQ", [128, 512], F32R, offset=_ovl).ap()
    C128 = A("sC128", 1, F32)
    TD8 = A("sTD8", 8, F32)
    STATS = A("sSTATS", 16, F32)
    STATI = A("sSTATI", 4, I32)

    PB = [nc.alloc_psum_tensor(f"pb{i}", [128, 512], F32).ap() for i in range(8)]
    PB7B = PB[7].bitcast(BF16)

    S = Sched()

    _spr_rr = [0]

    def dma(q, dst, src, waits=(), sem="SPR", slow=False):
        if sem == "SPR":
            sem = ["SPR", "SPR1", "SPR2", "SPR3"][_spr_rr[0] % 4]
            _spr_rr[0] += 1
        prev = S.cnt.get(sem, 0)
        def fn(eng, dst=dst, src=src, slow=slow):
            return eng.dma_start(dst, src, allow_slow_non_contiguous=slow)
        return S.emit(q, fn, tuple(waits) + (((sem, prev),) if prev else ()),
                      inc=sem, amt=16)

    def mm(out, lhsT, rhs, start, stop, waits=(), inc=None, skip=False):
        def fn(eng, out=out, lhsT=lhsT, rhs=rhs, start=start, stop=stop, skip=skip):
            return nc.tensor.matmul(out, lhsT, rhs, start=start, stop=stop,
                                    skip_group_check=skip)
        return S.emit("tensor", fn, waits, inc=inc)

    def tr(out, in_, idn, waits=(), inc=None):
        def fn(eng, out=out, in_=in_, idn=idn):
            return nc.tensor.transpose(out, in_, idn)
        return S.emit("tensor", fn, waits, inc=inc)

    def act(out, in_, func, waits=(), inc="sACT", scale=None, bias=None):
        def fn(eng, out=out, in_=in_, func=func, scale=scale, bias=bias):
            kw = {}
            if scale is not None:
                kw["scale"] = scale
            if bias is not None:
                kw["bias"] = bias
            return nc.scalar.activation(out, in_, func, **kw)
        return S.emit("scalar", fn, waits, inc=inc)

    def vcopy(out, in_, waits=(), inc="sDVE", eng="vector"):
        def fn(e, out=out, in_=in_, eng=eng):
            return getattr(nc, eng).tensor_copy(out, in_)
        return S.emit(eng, fn, waits, inc=inc)

    def vtt(out, a, b, op, waits=(), inc="sDVE"):
        def fn(e, out=out, a=a, b=b, op=op):
            return nc.vector.tensor_tensor(out, a, b, op)
        return S.emit("vector", fn, waits, inc=inc)

    def vts(out, in_, s1, s2, op0, op1=None, waits=(), inc="sDVE"):
        def fn(e, out=out, in_=in_, s1=s1, s2=s2, op0=op0, op1=op1):
            if op1 is None:
                return nc.vector.tensor_scalar(out, in_, s1, None, op0)
            return nc.vector.tensor_scalar(out, in_, s1, s2, op0, op1)
        return S.emit("vector", fn, waits, inc=inc)

    def vstt(out, in0, scal, in1, op0, op1, waits=(), inc="sDVE"):
        def fn(e, out=out, in0=in0, scal=scal, in1=in1, op0=op0, op1=op1):
            return nc.vector.scalar_tensor_tensor(out, in0, scal, in1, op0, op1)
        return S.emit("vector", fn, waits, inc=inc)

    tc0 = dma("sync", IDN[:], didn[:])
    dma("sync", IDNF[:], didnf[:])
    dma("sync", ONESC[:], dconc[:])
    dma("sync", ONESRF[:], dconrf[:])
    dma("sync", DIF[:], ddif[:])
    dma("sync", REV[:], drev[:])
    dma("sync", IOT[:], diot[:])
    dma("sync", C128[:], dc128[:])
    dma("sync", TD8[:], dtd.rearrange("(c h2) d -> (h2 d) c", c=8, h2=2), slow=True)
    dma("sync", UC[:], dtf.rearrange("(c h2) d -> (h2 d) c", c=8, h2=2), slow=True)
    dma("sync", TDR[:], dtd[:, 0:1].rearrange("h one -> one h"), slow=True)
    dma("sync", B1R[:], db1.rearrange("(c p) -> c p", p=128))
    tcon = dma("sync", B2R[:], db2.rearrange("(c p) -> c p", p=128))

    HQF = HQ.bitcast(F32)
    stg = [XT, HQF]
    stage4 = [XT[:, 0:1024], XT[:, 1024:2048], HQF[:, 0:1024], HQF[:, 1024:2048]]
    q_free = {}
    wmix_last = None
    for wi, (dsrc, wdst) in enumerate([(dwk, WKs), (dwv, WVs), (dwr, WRs), (dwo, WOs)]):
        for kc in range(PC):
            sl = (wi * PC + kc) % 4
            t_ld = dma("sync", stage4[sl], dsrc[128 * kc:128 * (kc + 1), :],
                       waits=(q_free.get(sl),))
            t_c = vcopy(wdst[:, kc * 1024:(kc + 1) * 1024], stage4[sl], waits=(t_ld,))
            q_free[sl] = t_c
            wmix_last = t_c
    wk_tok = wmix_last

    stage2 = [XT[:, 0:2048], HQF[:, 0:2048]]
    half_free = {0: (q_free.get(0), q_free.get(1)), 1: (q_free.get(2), q_free.get(3))}
    fi = 0
    w1_done, w2_done = [], []
    for kc in range(PC):
        for hh in range(2):
            sl = fi % 2
            t_ld = dma("sync", stage2[sl],
                       dw1[128 * kc:128 * (kc + 1), 2048 * hh:2048 * (hh + 1)],
                       waits=(half_free[sl],))
            dst = W1S[:, kc * 4096 + 2048 * hh: kc * 4096 + 2048 * (hh + 1)]
            eng = ["scalar", "vector", "gpsimd"][fi % 3]
            if eng == "scalar":
                t_c = act(dst, stage2[sl], AF.Identity, waits=(t_ld,))
            else:
                t_c = vcopy(dst, stage2[sl], waits=(t_ld,),
                            inc="sDVE" if eng == "vector" else "sPOOL", eng=eng)
            half_free[sl] = t_c
            w1_done.append(t_c)
            fi += 1
    for j in range(PC):
        srcj = dw2[512 * j:512 * (j + 1), :].rearrange("(j2 p) o -> p j2 o", p=128)
        for hh in range(2):
            sl = fi % 2
            t_ld = dma("sync",
                       stage2[sl].rearrange("p (j2 o) -> p j2 o", o=1024),
                       srcj[:, 2 * hh:2 * hh + 2, :],
                       waits=(half_free[sl],))
            dst = W2S[:, j * 4096 + 2048 * hh: j * 4096 + 2048 * (hh + 1)]
            eng = ["scalar", "vector", "gpsimd"][fi % 3]
            if eng == "scalar":
                t_c = act(dst, stage2[sl], AF.Identity, waits=(t_ld,))
            else:
                t_c = vcopy(dst, stage2[sl], waits=(t_ld,),
                            inc="sDVE" if eng == "vector" else "sPOOL", eng=eng)
            half_free[sl] = t_c
            w2_done.append(t_c)
            fi += 1
    stage_done = dict(half_free)

    t_lnw = act(LNW[:], TD8[:], AF.Ln, waits=(tcon,))
    t_lnr = act(STATS[0:1, 0:16], TDR[:], AF.Ln, waits=(tcon,))
    t_lb = mm(PB[6][:, 0:16], ONESRF[0:1, :], STATS[0:1, 0:16], True, True,
              waits=(t_lnr,), inc="sPE")
    t_lnwb = vcopy(LNWB[:], PB[6][:, 0:16], waits=(t_lb,))
    S.emit("vector", lambda e: nc.vector.memset(STATS[:, 0:2], 1.0), (t_lb,), inc="sDVE")
    vcopy(ONES2R[:], STATS[:, 0:2])
    tb_toks = []
    for h in range(H):
        tb_toks.append(act(TB[:, h * 128:(h + 1) * 128], DIF[:], AF.Exp,
                           scale=LNWB[:, h:h + 1], waits=(t_lnwb,)))
    tk_toks, trt_toks, wc_toks = [], [], []
    for p in range(PC):
        tk_toks.append(act(TK[:, p * 128:(p + 1) * 128], REV[:], AF.Exp,
                           scale=LNW[:, p:p + 1], waits=(t_lnw,)))
        trt_toks.append(act(TRt[:, p * 128:(p + 1) * 128], IOT[:], AF.Exp,
                            scale=LNW[:, p:p + 1], waits=(t_lnw,)))
        wc_toks.append(act(WC[:, p:p + 1], C128[:], AF.Exp,
                           scale=LNW[:, p:p + 1], waits=(t_lnw,)))

    t_tkt = None
    for p in range(PC):
        t_tkt = tr(PB7B[:, p * 128:(p + 1) * 128], TK[:, p * 128:(p + 1) * 128],
                   IDN[:], waits=(tk_toks[p], tc0),
                   inc="sPE" if p == PC - 1 else None)
    t_tkt = vcopy(TKT[:], PB7B[:, 0:1024], waits=(t_tkt,))

    t_b1t = tr(PB[6][0:128, 16:48], B1R[:], IDNF[0:32, 0:32], waits=(tcon, t_lnwb), inc="sPE")
    t_b1e = vcopy(B1C[:], PB[6][:, 16:48], waits=(t_b1t,))
    t_e = vcopy(B1Q[:], B1C[:], waits=(t_b1e,))
    t_b1c = vcopy(B1C[:], B1Q[:], waits=(t_e,))
    t_b2t = tr(PB[6][0:128, 48:56], B2R[:], IDNF[0:8, 0:8], waits=(t_b1e,), inc="sPE")
    t_e = vcopy(B2C[:], PB[6][:, 48:56], waits=(t_b2t,))
    t_e = vcopy(B2Q[:], B2C[:], waits=(t_e,))
    t_b2c = vcopy(B2C[:], B2Q[:], waits=(t_e,))

    regs = [(0, 0), (1, 0), (2, 0), (3, 0), (4, 0), (5, 0)]
    t_pre = dma("sync", OUT[:, 0:1024].rearrange("p (pc t) -> p pc t", t=CH),
                dxpre.rearrange("(pc p) t -> p pc t", p=128))
    t_pxb = act(XBF[:, 0:1024], OUT[:, 0:1024], AF.Identity, waits=(t_pre,))
    reg_free = {}
    pre_kp = []
    for mc in range(PC):
        b, off = regs[mc % 6]
        tok = None
        for kc in range(PC):
            tok = mm(PB[b][:, off:off + 128],
                     WKs[:, kc * 1024 + mc * 128: kc * 1024 + mc * 128 + 128],
                     XBF[:, kc * 128:(kc + 1) * 128],
                     start=(kc == 0), stop=(kc == PC - 1),
                     waits=(t_pxb, wk_tok, reg_free.get(mc % 6)),
                     inc="sPE" if kc == PC - 1 else None)
        t_ev = act(KP[:, mc * 128:(mc + 1) * 128], PB[b][:, off:off + 128],
                   AF.Identity, scale=UC[:, mc:mc + 1], waits=(tok,))
        reg_free[mc % 6] = t_ev
        pre_kp.append(t_ev)
    pre_v = []
    for eh in range(2):
        tok = None
        for kc in range(PC):
            tok = mm(PB[eh][:, 0:512],
                     XBF[:, kc * 128:(kc + 1) * 128],
                     WVs[:, kc * 1024 + eh * 512: kc * 1024 + (eh + 1) * 512],
                     start=(kc == 0), stop=(kc == PC - 1),
                     waits=(reg_free.get(eh),),
                     inc="sPE" if kc == PC - 1 else None)
        pre_v.append(vcopy(VT[:, eh * 512:(eh + 1) * 512], PB[eh][:, 0:512],
                           waits=(tok,)))
    reg_free[0] = pre_v[0]
    reg_free[1] = pre_v[1]
    t_tp = None
    for p in range(PC):
        t_tp = tr(PB7B[:, p * 128:(p + 1) * 128], KP[:, p * 128:(p + 1) * 128],
                  IDN[:], waits=(pre_kp[p], t_tkt if p == 0 else None),
                  inc="sPE" if p == PC - 1 else None)
    t_kht = vtt(KHT[:], PB7B[:, 0:1024], TKT[:], ALU.mult, waits=(t_tp,))
    pre_m = []
    m_last = None
    for p in range(PC):
        mm(PB[4][0:64, p * 64:(p + 1) * 64], KHT[:, p * 128:p * 128 + 64],
           VT[:, 128 * p: 128 * p + 64], True, True,
           waits=(t_kht, pre_v[0], pre_v[1]) + ((t_b2c,) if p == 0 else ()))
        m_last = mm(PB[4][64:128, p * 64:(p + 1) * 64],
                    KHT[:, p * 128 + 64:p * 128 + 128],
                    VT[:, 128 * p + 64: 128 * p + 128], True, True,
                    inc="sPE" if p == PC - 1 else None)
    for p in range(PC):
        pre_m.append(vcopy(St[:, p * 64:(p + 1) * 64],
                           PB[4][:, p * 64:(p + 1) * 64], waits=(m_last,)))

    tok_xfree1 = (stage_done.get(0), stage_done.get(1))
    tok_xfree2 = (stage_done.get(0), stage_done.get(1))
    tok_vt_free = m_last
    tok_kp_done = t_tp
    tok_rs_done = None
    tok_xbf_done = pre_v[1]
    tok_kht_free = None
    tok_rt_free = None
    tok_x2q_done = None
    tok_s_upd = pre_m[-1]
    tok_pb7_free = t_kht
    tok_pb6_free = t_b2c
    tok_abr_free = None
    tok_tmp1_free = None
    tok_stat_prev = None
    tok_kht_ln = None
    at_free = None
    tok_hq_free = None
    prev_t_x2q = None
    prev_store = t_pxb
    g_free = {}
    y_free = {4: pre_m[-1]}
    last_v_ev = pre_v[1]
    gk = 0

    def emit_front_K(i):
        nonlocal tok_kp_done, tok_xbf_done, tok_xfree1, tok_xfree2
        t0 = i * T
        srcA = dxT[0:512, t0:t0 + T].rearrange("(pc p) t -> p pc t", p=128)
        srcB = dxT[512:1024, t0:t0 + T].rearrange("(pc p) t -> p pc t", p=128)
        t_x1 = dma("sync", XT[:, 0:1024].rearrange("p (pc t) -> p pc t", t=T),
                   srcA, waits=(tok_xfree1,) if not isinstance(tok_xfree1, tuple)
                   else tok_xfree1, sem="SX")
        t_xb1 = act(XBF[:, 0:1024], XT[:, 0:1024], AF.Identity,
                    waits=(t_x1, tok_xbf_done))
        t_x2 = dma("sync", XT[:, 1024:2048].rearrange("p (pc t) -> p pc t", t=T),
                   srcB, waits=(tok_xfree2,) if not isinstance(tok_xfree2, tuple)
                   else tok_xfree2, sem="SX")
        t_xb2 = act(XBF[:, 1024:2048], XT[:, 1024:2048], AF.Identity,
                    waits=(t_x2,))
        t_xbf = (t_xb1, t_xb2)
        kp_ev = []
        for mc in range(PC):
            b, off = regs[mc % 6]
            tok = None
            for kc in range(PC):
                tok = mm(PB[b][:, off:off + 256],
                         WKs[:, kc * 1024 + mc * 128: kc * 1024 + mc * 128 + 128],
                         XBF[:, kc * 256:(kc + 1) * 256],
                         start=(kc == 0), stop=(kc == PC - 1),
                         waits=(t_xb1 if kc < 4 else t_xb2,
                                reg_free.get(mc % 6),
                                (g_free.get(b), y_free.get(b)) if kc == 0 else None),
                         inc="sPE" if kc == PC - 1 else None)
            t_ev = act(KP[:, mc * 256:(mc + 1) * 256], PB[b][:, off:off + 256],
                       AF.Identity, scale=UC[:, mc:mc + 1],
                       waits=(tok, tok_kp_done if mc == 0 else None))
            reg_free[mc % 6] = t_ev
            kp_ev.append(t_ev)
        return t_xbf, kp_ev

    def emit_front_R(i, t_xbf):
        nonlocal tok_rs_done
        rs_ev = []
        for mc in range(PC):
            b, off = regs[mc % 6]
            tok = None
            for kc in range(PC):
                tok = mm(PB[b][:, off:off + 256],
                         WRs[:, kc * 1024 + mc * 128: kc * 1024 + mc * 128 + 128],
                         XBF[:, kc * 256:(kc + 1) * 256],
                         start=(kc == 0), stop=(kc == PC - 1),
                         waits=(t_xbf[0 if kc < 4 else 1],
                                reg_free.get(mc % 6)),
                         inc="sPE" if kc == PC - 1 else None)
            t_ev = act(RS[:, mc * 256:(mc + 1) * 256], PB[b][:, off:off + 256],
                       AF.Sigmoid, waits=(tok, tok_rs_done if mc == 0 else None))
            reg_free[mc % 6] = t_ev
            rs_ev.append(t_ev)
        return rs_ev

    def emit_front(i):
        t_xbf, kp_ev = emit_front_K(i)
        rs_ev = emit_front_R(i, t_xbf)
        return t_xbf, kp_ev, rs_ev

    def emit_mask(i, w1, w2):
        msrc = dmsk[:, i * T:(i + 1) * T].rearrange("(pc p) t -> p pc t", p=128)
        ta = dma("sync", MSK[:].rearrange("p (pc t) -> p pc t", t=T),
                 msrc[:, 0:4, :], waits=(w1,), sem="SM")
        tb = dma("sync", MSK2[:].rearrange("p (pc t) -> p pc t", t=T),
                 msrc[:, 4:8, :], waits=(w2,), sem="SM")
        return ta, tb

    def emit_mixer_V(c):
        nonlocal tok_vt_free, last_v_ev
        v_ev_c = None
        for eh in range(2):
            tok = None
            for kc in range(PC):
                tok = mm(PB[eh][:, 0:512],
                         XBF[:, kc * 256 + c * 128: kc * 256 + c * 128 + 128],
                         WVs[:, kc * 1024 + eh * 512: kc * 1024 + (eh + 1) * 512],
                         start=(kc == 0), stop=(kc == PC - 1),
                         waits=(reg_free.get(eh),
                                (g_free.get(eh), y_free.get(eh)) if kc == 0 else None),
                         inc="sPE" if kc == PC - 1 else None)
            t_ev = act(VT[:, eh * 512:(eh + 1) * 512], PB[eh][:, 0:512],
                       AF.Identity,
                       waits=(tok,) + ((tok_vt_free,) if eh == 0 else ()))
            reg_free[eh] = t_ev
            if eh == 1:
                v_ev_c = t_ev
        last_v_ev = v_ev_c
        return v_ev_c

    def emit_mixer_T(c, kp_ev):
        nonlocal tok_pb7_free, tok_kht_free, tok_kp_done
        t_tp = None
        for p in range(PC):
            t_tp = tr(PB7B[:, p * 128:(p + 1) * 128],
                      KP[:, p * 256 + c * 128: p * 256 + c * 128 + 128], IDN[:],
                      waits=(kp_ev[p], tok_pb7_free if p == 0 else None),
                      inc="sPE" if p == PC - 1 else None)
        t_kht = vtt(KHT[:], PB7B[:, 0:1024], TKT[:], ALU.mult,
                    waits=(t_tp, tok_kht_free, tok_kht_ln))
        tok_pb7_free = t_kht
        if c == 1:
            tok_kp_done = t_kht
        return t_kht

    def emit_mixer_A(c, kp_ev):
        v_ev_c = emit_mixer_V(c)
        t_kht = emit_mixer_T(c, kp_ev)
        return v_ev_c, t_kht

    def emit_rt(c, rs_ev):
        nonlocal tok_rt_free, tok_rs_done
        rt_toks = []
        for p in range(PC):
            rt_toks.append(vtt(RT[:, p * 128:(p + 1) * 128],
                               RS[:, p * 256 + c * 128: p * 256 + c * 128 + 128],
                               TRt[:, p * 128:(p + 1) * 128], ALU.mult,
                               waits=(rs_ev[p], trt_toks[p],
                                      tok_rt_free if p == 0 else None)))
        if c == 1:
            tok_rs_done = rt_toks[-1]
        return rt_toks

    def emit_mixer_B(c, kp_ev, rs_ev, v_ev_c, t_kht, rt_toks=None):
        nonlocal tok_s_upd, tok_kht_free, tok_rt_free, tok_vt_free, at_free, gk
        scur = gk % 2
        snxt = 1 - scur
        if rt_toks is None:
            rt_toks = emit_rt(c, rs_ev)
        g_toks = {}
        bank_last = {}
        for wv in range(2):
            for pidx in range(4):
                p = wv * 4 + pidx
                for hh in range(2):
                    bank = (2 + hh) if wv == 0 else hh
                    off = pidx * 128
                    w = [kp_ev[p], rs_ev[p]]
                    if pidx == 0:
                        w += [g_free.get(bank), reg_free.get(bank),
                              y_free.get(bank)]
                    g_toks[(p, hh)] = mm(
                        PB[bank][:, off: off + 128],
                        KP[64 * hh:64 * hh + 64,
                           p * 256 + c * 128: p * 256 + c * 128 + 128],
                        RS[64 * hh:64 * hh + 64,
                           p * 256 + c * 128: p * 256 + c * 128 + 128],
                        True, True, waits=tuple(w), inc="sPE")
                    bank_last[bank] = g_toks[(p, hh)]
        for wv in range(2):
            ybank = 4 + wv
            for pidx in range(4):
                p = wv * 4 + pidx
                off = pidx * 128
                for hh in range(2):
                    mm(PB[ybank][64 * hh:64 * hh + 64, off:off + 128],
                       St[64 * hh:64 * hh + 64,
                          scur * 512 + p * 64: scur * 512 + (p + 1) * 64],
                       RT[64 * hh:64 * hh + 64, p * 128:(p + 1) * 128],
                       start=(pidx == 0), stop=False,
                       waits=(tok_s_upd, rt_toks[p],
                              (y_free.get(ybank), g_free.get(ybank),
                               reg_free.get(ybank))
                              if pidx == 0 and hh == 0 else None),
                       skip=True)
        at_toks = {}
        for wv in range(2):
            for pidx in range(4):
                p = wv * 4 + pidx
                for hh in range(2):
                    bank = (2 + hh) if wv == 0 else hh
                    off = pidx * 128
                    h = 2 * p + hh
                    sl = wv * 8 + 2 * pidx + hh
                    at = vtt(AT[:, sl * 128:(sl + 1) * 128],
                             PB[bank][:, off: off + 128],
                             TB[:, h * 128:(h + 1) * 128], ALU.mult,
                             waits=(bank_last[bank], tb_toks[h],
                                    at_free if sl == 0 else None))
                    at_toks[h] = at
                    g_free[bank] = at
        y_ev = []
        yev_w1 = None
        for wv in range(2):
            ybank = 4 + wv
            ybank_last = None
            for pidx in range(4):
                p = wv * 4 + pidx
                off = pidx * 128
                for hh in range(2):
                    h = 2 * p + hh
                    sl = wv * 8 + 2 * pidx + hh
                    ybank_last = mm(
                        PB[ybank][64 * hh:64 * hh + 64, off:off + 128],
                        VT[:, p * 128 + 64 * hh: p * 128 + 64 * hh + 64],
                        AT[:, sl * 128:(sl + 1) * 128],
                        start=False, stop=True,
                        waits=(v_ev_c, at_toks[h]), inc="sPE", skip=True)
            for pidx in range(4):
                p = wv * 4 + pidx
                off = pidx * 128
                ev = act(YF[:, p * 256 + c * 128: p * 256 + c * 128 + 128],
                         PB[ybank][:, off:off + 128], AF.Identity,
                         waits=(ybank_last,))
                y_free[ybank] = ev
                y_ev.append(ev)
            at_free = ybank_last
            if wv == 0:
                yev_w1 = y_ev[3]
        m_last = None
        for p in range(PC):
            mm(PB[4][0:64, p * 64:(p + 1) * 64], KHT[:, p * 128:p * 128 + 64],
               VT[:, 128 * p: 128 * p + 64], True, True,
               waits=(t_kht, v_ev_c) + ((yev_w1,) if p == 0 else ()))
            m_last = mm(PB[4][64:128, p * 64:(p + 1) * 64],
                        KHT[:, p * 128 + 64:p * 128 + 128],
                        VT[:, 128 * p + 64: 128 * p + 128],
                        True, True, inc="sPE" if p == PC - 1 else None)
        s_upd = []
        for p in range(PC):
            s_upd.append(vstt(St[:, snxt * 512 + p * 64: snxt * 512 + (p + 1) * 64],
                              St[:, scur * 512 + p * 64: scur * 512 + (p + 1) * 64],
                              WC[:, p:p + 1], PB[4][:, p * 64:(p + 1) * 64],
                              ALU.mult, ALU.add, waits=(m_last, wc_toks[p])))
        tok_s_upd = s_upd[-1]
        tok_kht_free = s_upd[-1]
        tok_rt_free = y_ev[-1]
        y_free[4] = s_upd[-1]
        tok_vt_free = m_last
        gk += 1
        return y_ev

    def emit_mixer_chunk(c, kp_ev, rs_ev):
        v_ev_c, t_kht = emit_mixer_A(c, kp_ev)
        return emit_mixer_B(c, kp_ev, rs_ev, v_ev_c, t_kht)

    def emit_attn_mms(y_ev_all, lo, hi):
        toks = []
        for mc in range(lo, hi):
            b, off = regs[mc % 6]
            tok = None
            for kc in range(PC):
                tok = mm(PB[b][:, off:off + 256],
                         WOs[:, kc * 1024 + mc * 128: kc * 1024 + mc * 128 + 128],
                         YF[:, kc * 256:(kc + 1) * 256],
                         start=(kc == 0), stop=(kc == PC - 1),
                         waits=(y_ev_all[kc], y_ev_all[8 + kc],
                                reg_free.get(mc % 6),
                                (g_free.get(b), y_free.get(b)) if kc == 0 else None),
                         inc="sPE" if kc == PC - 1 else None)
            toks.append((mc, tok))
        return toks

    def emit_attn_evacs(i, mm_toks):
        nonlocal tok_xfree1, tok_xfree2
        t_xbf_i = fronts[i][0]
        xr_toks = []
        for mc, tok in mm_toks:
            b, off = regs[mc % 6]
            t_ev = vtt(XR[:, mc * 256:(mc + 1) * 256],
                       XT[:, mc * 256:(mc + 1) * 256],
                       PB[b][:, off:off + 256], ALU.add, waits=(tok,))
            reg_free[mc % 6] = t_ev
            xr_toks.append(t_ev)
            if mc == 3:
                tok_xfree1 = (t_ev, t_xbf_i[0])
            elif mc == PC - 1:
                tok_xfree2 = (t_ev, t_xbf_i[1])
        return xr_toks

    def emit_attn(i, y_ev_all):
        xr_toks = []
        for mc in range(PC):
            xr_toks += emit_attn_evacs(i, emit_attn_mms(y_ev_all, mc, mc + 1))
        return xr_toks

    def ln_stats_make():
        nonlocal tok_pb6_free, tok_stat_prev, tok_kht_ln
        st = {"sq_free": [tok_stat_prev, None], "t": None}
        pb6w = (tok_pb6_free, tok_stat_prev)
        def piece(p, xr_tok):
            first = (p == 0)
            for th in range(2):
                mm(PB[6][:, 2 * th:2 * th + 2],
                   XR[:, p * 256 + th * 128: p * 256 + th * 128 + 128],
                   ONES2R[:, 0:2], start=first and th == 0, stop=(p == PC - 1),
                   waits=(xr_tok,) + (pb6w if first and th == 0 else ()),
                   skip=True)
            sq = XSQ[:, (p % 2) * 256:(p % 2) * 256 + 256]
            t_sq = vtt(sq, XR[:, p * 256:(p + 1) * 256],
                       XR[:, p * 256:(p + 1) * 256], ALU.mult,
                       waits=(xr_tok, st["sq_free"][p % 2],
                              wc_toks[-1] if p < 2 else None))
            t_stat = None
            for th in range(2):
                t_stat = mm(PB[6][:, 4 + 2 * th:6 + 2 * th],
                            sq[:, th * 128:(th + 1) * 128],
                            ONES2R[:, 0:2],
                            start=(p == 0 and th == 0), stop=(p == PC - 1),
                            waits=(t_sq,), inc="sPE", skip=True)
            st["sq_free"][p % 2] = t_stat
            st["t"] = t_stat
            return t_stat
        def fin():
            nonlocal tok_stat_prev, tok_kht_ln
            tok_stat_prev = st["t"]
            tok_kht_ln = st["t"]
            return st["t"]
        return piece, fin

    def ln_stats(xr_toks):
        nonlocal tok_pb6_free, tok_stat_prev, tok_kht_ln
        first = True
        for p in range(PC):
            for th in range(2):
                mm(PB[6][:, 2 * th:2 * th + 2],
                   XR[:, p * 256 + th * 128: p * 256 + th * 128 + 128],
                   ONES2R[:, 0:2], start=first, stop=(p == PC - 1),
                   waits=(xr_toks[p],) + ((tok_pb6_free, tok_stat_prev) if first else ()),
                   skip=True)
                first = False
        t_stat = None
        sq_free = [tok_stat_prev, None]
        for p in range(PC):
            sq = XSQ[:, (p % 2) * 256:(p % 2) * 256 + 256]
            t_sq = vtt(sq, XR[:, p * 256:(p + 1) * 256],
                       XR[:, p * 256:(p + 1) * 256], ALU.mult,
                       waits=(xr_toks[p], sq_free[p % 2],
                              wc_toks[-1] if p < 2 else None))
            for th in range(2):
                t_stat = mm(PB[6][:, 4 + 2 * th:6 + 2 * th],
                            sq[:, th * 128:(th + 1) * 128],
                            ONES2R[:, 0:2],
                            start=(p == 0 and th == 0), stop=(p == PC - 1),
                            waits=(t_sq,), inc="sPE", skip=True)
            sq_free[p % 2] = t_stat
        tok_stat_prev = t_stat
        tok_kht_ln = t_stat
        return t_stat

    def ln_chain(t_stat):
        vts(STATS[:, 0:4], PB[6][:, 0:4], 1.0 / D, None, ALU.mult, waits=(t_stat,))
        vts(STATS[:, 4:8], PB[6][:, 4:8], 1.0 / D, None, ALU.mult)
        vtt(STATS[:, 8:12], STATS[:, 0:4], STATS[:, 0:4], ALU.mult)
        vtt(STATS[:, 8:12], STATS[:, 4:8], STATS[:, 8:12], ALU.subtract)
        vts(STATS[:, 8:12], STATS[:, 8:12], EPS, None, ALU.add)
        vcopy(STATS[:, 12:16], STATS[:, 8:12].bitcast(I32))
        vts(STATS[:, 12:16], STATS[:, 12:16], -0.5, MAGIC, ALU.mult, ALU.add)
        vcopy(STATI[:, 0:4], STATS[:, 12:16])
        Y0 = STATI.bitcast(F32)
        for _ in range(2):
            vtt(STATS[:, 12:16], Y0[:, 0:4], Y0[:, 0:4], ALU.mult)
            vtt(STATS[:, 12:16], STATS[:, 12:16], STATS[:, 8:12], ALU.mult)
            vts(STATS[:, 12:16], STATS[:, 12:16], -0.5, 1.5, ALU.mult, ALU.add)
            vtt(Y0[:, 0:4], Y0[:, 0:4], STATS[:, 12:16], ALU.mult)
        vtt(STATS[:, 4:8], STATS[:, 0:4], Y0[:, 0:4], ALU.mult)
        t_pk = vts(STATS[:, 0:4], STATS[:, 4:8], -1.0, None, ALU.mult)
        return t_pk

    def ln_tr(t_pk):
        t_tr = None
        Y0F = STATI.bitcast(F32)
        srcs = [Y0F[:, 0:1], Y0F[:, 2:3], STATS[:, 0:1], STATS[:, 2:3]]
        for r in range(4):
            t_tr = tr(PB[6][0:1, r * 128:(r + 1) * 128],
                      srcs[r], IDNF[:],
                      waits=(t_pk,), inc="sPE" if r == 3 else None)
        return t_tr

    def ln_bc(t_tr):
        nonlocal tok_abr_free
        t_abr = vcopy(ABRROW[0:1, :], PB[6][0:1, 0:512],
                      waits=(t_tr, tok_abr_free))
        mm(PB[6][:, 256:512], ONESRF[0:1, :], ABRROW[0:1, 0:256], True, True,
           waits=(t_abr,), skip=True)
        t_bc = mm(PB[6][:, 0:256], ONESRF[0:1, :], ABRROW[0:1, 256:512],
                  True, True, inc="sPE", skip=True)
        tok_abr_free = t_bc
        return t_bc

    def ln_bcast(t_pk):
        return ln_bc(ln_tr(t_pk))

    def ln_apply(t_bc, out_writer):
        nonlocal tok_pb6_free, tok_tmp1_free
        out_toks = []
        for p in range(PC):
            vtt(TMP1[:, 0:256], XR[:, p * 256:(p + 1) * 256],
                PB[6][:, 256:512], ALU.mult,
                waits=(t_bc, tok_tmp1_free if p == 0 else None))
            tok_o = out_writer(p)
            tok_tmp1_free = tok_o
            out_toks.append(tok_o)
        tok_pb6_free = out_toks[-1]
        return out_toks

    def ln_bcast_apply(t_pk, out_writer):
        return ln_apply(ln_bcast(t_pk), out_writer)

    fronts = {0: emit_front(0)}
    t_m, t_m2 = emit_mask(0, None, t_tkt)
    y_all = emit_mixer_chunk(0, fronts[0][1], fronts[0][2])
    y_all += emit_mixer_chunk(1, fronts[0][1], fronts[0][2])
    xr_toks = emit_attn(0, y_all)

    for i in range(nt):
        t0 = i * T
        nxt = i + 1 < nt

        t_stat1 = ln_stats(xr_toks)
        if nxt:
            t_xbf_n, kp_n = emit_front_K(i + 1)
        t_pk1 = ln_chain(t_stat1)
        t_tr1 = ln_tr(t_pk1)
        if nxt:
            v0 = emit_mixer_V(0)
        t_bc1 = ln_bc(t_tr1)
        if nxt:
            kht0 = emit_mixer_T(0, kp_n)

        def ln1_writer(p):
            return vtt(X2B[:, p * 256:(p + 1) * 256], TMP1[:, 0:256],
                       PB[6][:, 0:256], ALU.add,
                       waits=((prev_t_x2q,) if p == 0 and prev_t_x2q else ()))
        x2_toks = ln_apply(t_bc1, ln1_writer)
        t_x2q = None
        for p in range(PC):
            t_x2q = act(X2Q[:, p * 256:(p + 1) * 256],
                        X2B[:, p * 256:(p + 1) * 256], AF.Identity,
                        waits=(x2_toks[p],) + ((tok_x2q_done,) if p == 0 else ()))
        prev_t_x2q = t_x2q
        if nxt:
            rs_n = emit_front_R(i + 1, t_xbf_n)
            y_next = emit_mixer_B(0, kp_n, rs_n, v0, kht0)

        hq_toks = []
        h16_free = {0: None, 1: None}
        for mc in range(FC):
            b, off = regs[mc % 6]
            tok = None
            for kc in range(PC):
                tok = mm(PB[b][:, off:off + 256],
                         W1S[:, kc * 4096 + mc * 128: kc * 4096 + mc * 128 + 128],
                         X2Q[:, kc * 256:(kc + 1) * 256],
                         start=(kc == 0), stop=(kc == PC - 1),
                         waits=(t_x2q, reg_free.get(mc % 6),
                                (g_free.get(b), y_free.get(b)) if kc == 0 else None)
                         + (tuple(w1_done) if i == 0 and mc == 0 else ()),
                         inc="sPE" if kc == PC - 1 else None)
            hb = (H16 if mc % 2 == 0 else H16B)[:, 0:256]
            t_h = act(hb, PB[b][:, off:off + 256], AF.Identity,
                      bias=B1C[:, mc:mc + 1],
                      waits=(tok, h16_free[mc % 2],
                             t_b1c if i == 0 and mc == 0 else None))
            t_q = vts(HQ[:, mc * 256:(mc + 1) * 256], hb, 0.0, None, ALU.max,
                      waits=(t_h, tok_hq_free if mc == 0 else None)
                      + (tuple(w2_done) if i == 0 and mc == 0 else ()))
            h16_free[mc % 2] = t_q
            reg_free[mc % 6] = t_q
            hq_toks.append(t_q)

        if nxt:
            rt1 = emit_rt(1, rs_n)

        st_piece, st_fin = ln_stats_make()
        r2_toks = []
        for mc in range(PC):
            b, off = regs[mc % 6]
            tok = None
            for kc in range(FC):
                tok = mm(PB[b][:, off:off + 256],
                         W2S[:, kc * 1024 + mc * 128: kc * 1024 + mc * 128 + 128],
                         HQ[:, kc * 256:(kc + 1) * 256],
                         start=(kc == 0), stop=(kc == FC - 1),
                         waits=(hq_toks[kc], reg_free.get(mc % 6),
                                (g_free.get(b), y_free.get(b)) if kc == 0 else None),
                         inc="sPE" if kc == FC - 1 else None)
            hb = (H16 if mc % 2 == 0 else H16B)[:, 0:256]
            t_h = act(hb, PB[b][:, off:off + 256], AF.Identity,
                      bias=B2C[:, mc:mc + 1], waits=(tok, h16_free[mc % 2]))
            mbuf = MSK if mc < 4 else MSK2
            t_mul = vtt(TMP1[:, 0:256], hb,
                        mbuf[:, (mc % 4) * 256:((mc % 4) + 1) * 256],
                        ALU.mult, waits=(t_h, t_m if mc < 4 else t_m2,
                                         tok_tmp1_free))
            t_r2 = vtt(XR[:, mc * 256:(mc + 1) * 256], TMP1[:, 0:256],
                       X2B[:, mc * 256:(mc + 1) * 256], ALU.add,
                       waits=(t_mul,))
            tok_tmp1_free = t_r2
            h16_free[mc % 2] = t_mul
            reg_free[mc % 6] = t_r2
            r2_toks.append(t_r2)
            st_piece(mc, t_r2)
            if mc == 3 and nxt:
                msrc_n = dmsk[:, (i + 1) * T:(i + 2) * T].rearrange(
                    "(pc p) t -> p pc t", p=128)
                t_mA = dma("sync", MSK[:].rearrange("p (pc t) -> p pc t", t=T),
                           msrc_n[:, 0:4, :], waits=(t_mul,), sem="SM")
            if mc == PC - 1 and nxt:
                msrc_n = dmsk[:, (i + 1) * T:(i + 2) * T].rearrange(
                    "(pc p) t -> p pc t", p=128)
                t_mB = dma("sync", MSK2[:].rearrange("p (pc t) -> p pc t", t=T),
                           msrc_n[:, 4:8, :], waits=(t_mul,), sem="SM")
        tok_hq_free = r2_toks[-1]
        tok_x2q_done = r2_toks[-1]
        if nxt:
            t_m, t_m2 = t_mA, t_mB

        t_stat2 = st_fin()
        t_pk2 = ln_chain(t_stat2)
        if nxt:
            mA1 = emit_mixer_A(1, kp_n)
            y_next = y_next + emit_mixer_B(1, kp_n, rs_n, *mA1, rt_toks=rt1)
        t_tr2 = ln_tr(t_pk2)
        if nxt:
            fronts[i + 1] = (t_xbf_n, kp_n, rs_n)
            attn_p1 = emit_attn_mms(y_next, 0, 2)
        t_bc2 = ln_bc(t_tr2)

        ln2_state = {}

        def ln2_writer(p):
            nonlocal prev_store
            if p == 4:
                dst1 = dout[0:512, t0:t0 + T].rearrange("(pc p) t -> p pc t", p=128)
                ln2_state["st1"] = dma("gpsimd", dst1,
                                       OUT[:].rearrange("p (pc t) -> p pc t", t=T),
                                       waits=(ln2_state[3],), sem="SO")
            w = ()
            if p == 0:
                w = (prev_store,)
            elif p == 4:
                w = (ln2_state["st1"],)
            tok = vtt(OUT[:, (p % 4) * 256:((p % 4) + 1) * 256], TMP1[:, 0:256],
                      PB[6][:, 0:256], ALU.add, waits=w)
            ln2_state[p] = tok
            return tok
        o_toks = ln_apply(t_bc2, ln2_writer)
        if nxt:
            xr_toks = emit_attn_evacs(i + 1, attn_p1)
            xr_toks = xr_toks + emit_attn_evacs(
                i + 1, emit_attn_mms(y_next, 2, PC))
        dst2 = dout[512:1024, t0:t0 + T].rearrange("(pc p) t -> p pc t", p=128)
        prev_store = dma("gpsimd", dst2,
                         OUT[:].rearrange("p (pc t) -> p pc t", t=T),
                         waits=(o_toks[-1],), sem="SO")

    return nc, S


def realize(nc, S):
    import contextlib
    semh = {}
    ctx = contextlib.ExitStack()
    for s in SEMS:
        semh[s] = ctx.enter_context(nc.semaphore(s))
    block = ctx.enter_context(nc.Block())
    so_total = S.cnt.get("SO", 0)

    def make(engname):
        def body(engobj):
            for fn, waits, inc, amt in S.q[engname]:
                for (s, v) in waits:
                    engobj.wait_ge(semh[s], v)
                inst = fn(engobj)
                if inc is not None and inst is not None:
                    inst.then_inc(semh[inc], amt)
            if engname == "gpsimd":
                engobj.wait_ge(semh["SO"], so_total)
        return body

    for e in ENGINES:
        getattr(block, e)(make(e))
    ctx.close()
    return nc


def host_constants(ntok):
    import ml_dtypes
    bfd = ml_dtypes.bfloat16
    f8 = ml_dtypes.float8_e4m3
    s = np.arange(128)
    t = np.arange(128)
    dif = (t[None, :] - s[:, None]).astype(np.float32)
    dif[t[None, :] < s[:, None]] = 1e5
    rev = np.broadcast_to((127.0 - t).astype(np.float32), (128, 128)).copy()
    iot = np.broadcast_to((t + 1.0).astype(np.float32), (128, 128)).copy()
    return {
        "idn": np.eye(128, dtype=bfd),
        "idnf": np.eye(128, dtype=np.float32),
        "onesc": np.ones((128, 1), dtype=bfd),
        "onesrf": np.ones((1, 128), dtype=np.float32),
        "dif": dif,
        "rev": rev,
        "iot": iot,
        "c128": np.full((128, 1), 128.0, dtype=np.float32),
    }


NCORE = 8
NTOK = 4096


def kernel(L0_attn_x, L0_attn_Wk, L0_attn_Wv, L0_attn_Wr, L0_attn_Wo,
           L0_attn_time_decay, L0_attn_time_first,
           L0_ffn_w1, L0_ffn_b1, L0_ffn_w2, L0_ffn_b2):
    import ml_dtypes
    import jax
    from concourse.bass_utils import run_bass_kernel_spmd
    f8d = ml_dtypes.float8_e4m3

    x = np.asarray(L0_attn_x, dtype=np.float32)
    B, L, Dm = x.shape
    assert (B, L, Dm) == (4, 8192, 1024)

    cpu = jax.devices("cpu")[0]
    with jax.default_device(cpu):
        keep = np.asarray(jax.random.bernoulli(
            jax.random.key(42, impl="threefry2x32"), 0.8, (B, L, Dm)))
    maskv = np.where(keep, np.float32(1.25), np.float32(0.0))

    wkT = np.ascontiguousarray(np.asarray(L0_attn_Wk, np.float32).T)
    wvT = np.ascontiguousarray(np.asarray(L0_attn_Wv, np.float32).T)
    wrT = np.ascontiguousarray(np.asarray(L0_attn_Wr, np.float32).T)
    woT = np.ascontiguousarray(np.asarray(L0_attn_Wo, np.float32).T)
    w1T = np.ascontiguousarray(np.asarray(L0_ffn_w1, np.float32).T)
    w2T = np.ascontiguousarray(np.asarray(L0_ffn_w2, np.float32).T)
    td = np.asarray(L0_attn_time_decay, np.float32)
    tf = np.asarray(L0_attn_time_first, np.float32)
    b1 = np.asarray(L0_ffn_b1, np.float32)
    b2 = np.asarray(L0_ffn_b2, np.float32)
    con = host_constants(NTOK)
    zpre = np.zeros((Dm, CH), dtype=np.float32)

    in_maps = []
    for core in range(NCORE):
        b, hf = core // 2, core % 2
        sl = x[b, hf * NTOK:(hf + 1) * NTOK, :]
        xT = np.ascontiguousarray(sl.T)
        xpre = np.ascontiguousarray(x[b, NTOK - CH:NTOK, :].T) if hf == 1 else zpre
        mT = np.ascontiguousarray(maskv[b, hf * NTOK:(hf + 1) * NTOK, :].T).astype(f8d)
        in_maps.append(dict(
            xT=xT, xpre=xpre, wkT=wkT, wvT=wvT, wrT=wrT, woT=woT,
            w1T=w1T, w2T=w2T, td=td, tf=tf, b1=b1, b2=b2, maskT=mT, **con))

    nc, S = build_nc(NTOK)
    realize(nc, S)
    res = run_bass_kernel_spmd(nc, in_maps, core_ids=list(range(NCORE)))
    out = np.empty((B, L, Dm), dtype=np.float32)
    for core in range(NCORE):
        b, hf = core // 2, core % 2
        out[b, hf * NTOK:(hf + 1) * NTOK, :] = res.results[core]["outT"].T
    return out


# version 17
# speedup vs baseline: 1.1352x; 1.1352x over previous
import numpy as np
import concourse.bass as bass
import concourse.mybir as mybir

F32 = mybir.dt.float32
BF16 = mybir.dt.bfloat16
FP16 = mybir.dt.float16
F32R = mybir.dt.float32r
FP8 = mybir.dt.float8e4
I32 = mybir.dt.int32
AF = mybir.ActivationFunctionType
ALU = mybir.AluOpType

D = 1024
PC = 8
H = 16
FF = 4096
FC = 32
T = 256
CH = 128
EPS = 1e-5
MAGIC = float(0x5f3759df)

ENGINES = ("sync", "tensor", "vector", "scalar", "gpsimd")
OWN = {"tensor": "sPE", "vector": "sDVE", "scalar": "sACT", "gpsimd": "sPOOL"}
SEMS = ["SX", "SM", "SO", "SPR", "SPR1", "SPR2", "SPR3",
        "sPE", "sDVE", "sACT", "sPOOL"]


class Sched:
    def __init__(self):
        self.q = {e: [] for e in ENGINES}
        self.cnt = {}
        self.hw = {e: {} for e in ENGINES}

    def emit(self, eng, fn, waits=(), inc=None, amt=1):
        real = []
        flat = []
        for w in waits:
            if w is None:
                continue
            if isinstance(w, (list, tuple)) and w and isinstance(w[0], (list, tuple)):
                flat.extend(w)
            elif isinstance(w, (list, tuple)) and len(w) == 2 and isinstance(w[0], str):
                flat.append(w)
            elif isinstance(w, (list, tuple)):
                flat.extend(x for x in w if x is not None)
            else:
                raise ValueError(f"bad wait {w}")
        own = OWN.get(eng)
        if own is not None and eng != "tensor":
            prev = self.cnt.get(own, 0)
            if prev > 0:
                flat.append((own, prev))
        for w in flat:
            if w is None:
                continue
            s, v = w
            if s == own and self.cnt.get(own, 0) >= v and \
                    self.hw[eng].get(s, 0) >= v:
                continue
            if self.hw[eng].get(s, 0) >= v:
                continue
            self.hw[eng][s] = v
            real.append((s, v))
        tok = None
        if inc is not None:
            self.cnt[inc] = self.cnt.get(inc, 0) + amt
            tok = (inc, self.cnt[inc])
        self.q[eng].append((fn, real, inc, amt))
        return tok


def build_nc(ntok):
    nt = ntok // T
    nc = bass.Bass("TRN2", target_bir_lowering=False)

    dxT = nc.dram_tensor("xT", [D, ntok], F32, kind="ExternalInput").ap()
    dxpre = nc.dram_tensor("xpre", [D, CH], F32, kind="ExternalInput").ap()
    dwk = nc.dram_tensor("wkT", [D, D], F32, kind="ExternalInput").ap()
    dwv = nc.dram_tensor("wvT", [D, D], F32, kind="ExternalInput").ap()
    dwr = nc.dram_tensor("wrT", [D, D], F32, kind="ExternalInput").ap()
    dwo = nc.dram_tensor("woT", [D, D], F32, kind="ExternalInput").ap()
    dw1 = nc.dram_tensor("w1T", [D, FF], F32, kind="ExternalInput").ap()
    dw2 = nc.dram_tensor("w2T", [FF, D], F32, kind="ExternalInput").ap()
    dtd = nc.dram_tensor("td", [H, 64], F32, kind="ExternalInput").ap()
    dtf = nc.dram_tensor("tf", [H, 64], F32, kind="ExternalInput").ap()
    db1 = nc.dram_tensor("b1", [FF], F32, kind="ExternalInput").ap()
    db2 = nc.dram_tensor("b2", [D], F32, kind="ExternalInput").ap()
    dmsk = nc.dram_tensor("maskT", [D, ntok], FP8, kind="ExternalInput").ap()
    didn = nc.dram_tensor("idn", [128, 128], BF16, kind="ExternalInput").ap()
    didnf = nc.dram_tensor("idnf", [128, 128], F32, kind="ExternalInput").ap()
    dconc = nc.dram_tensor("onesc", [128, 1], BF16, kind="ExternalInput").ap()
    dconrf = nc.dram_tensor("onesrf", [1, 128], F32, kind="ExternalInput").ap()
    ddif = nc.dram_tensor("dif", [128, 128], F32, kind="ExternalInput").ap()
    drev = nc.dram_tensor("rev", [128, 128], F32, kind="ExternalInput").ap()
    diot = nc.dram_tensor("iot", [128, 128], F32, kind="ExternalInput").ap()
    dc128 = nc.dram_tensor("c128", [128, 1], F32, kind="ExternalInput").ap()
    dout = nc.dram_tensor("outT", [D, ntok], F32, kind="ExternalOutput").ap()

    A = lambda name, w, dt: nc.alloc_sbuf_tensor(name, [128, w], dt).ap()
    XT = A("sXT", 2048, F32)
    XBF = A("sXBF", 2048, BF16)
    KP = A("sKP", 2048, BF16)
    RS = A("sRS", 2048, BF16)
    VT = A("sVT", 1024, BF16)
    KHT = A("sKHT", 1024, BF16)
    TKT = A("sTKT", 1024, BF16)
    RT = A("sRT", 1024, BF16)
    YF = A("sYF", 2048, BF16)
    XR = A("sXR", 2048, F32R)
    X2B = A("sX2B", 2048, BF16)
    X2Q = A("sX2Q", 2048, FP8)
    ONES2R = A("sONES2R", 2, F32R)
    HQ = A("sHQ", 8192, FP8)
    H16 = A("sH16", 256, FP16)
    TMP1 = A("sTMP1", 256, BF16)
    AT = A("sAT", 2048, BF16)
    MSK = A("sMSK", 1024, FP8)
    OUT = A("sOUT", 1024, F32)
    WKs = A("sWK", 8192, BF16)
    WVs = A("sWV", 8192, BF16)
    WRs = A("sWR", 8192, BF16)
    WOs = A("sWO", 8192, BF16)
    W1S = A("sW1S", 32768, FP8)
    W2S = A("sW2S", 32768, FP8)
    TB = A("sTB", 2048, BF16)
    _ovk, _ = nc.bump_sbuf(2048)
    TK = nc.alloc_sbuf_tensor_at("sTK", [128, 1024], BF16, offset=_ovk).ap()
    MSK2 = nc.alloc_sbuf_tensor_at("sMSK2", [128, 1024], FP8, offset=_ovk).ap()
    H16B = nc.alloc_sbuf_tensor_at("sH16B", [128, 256], FP16,
                                   offset=_ovk + 1024).ap()
    TRt = A("sTR", 1024, BF16)
    WC = A("sWC", 8, F32)
    UC = A("sUC", 8, F32)
    LNW = A("sLNW", 8, F32)
    LNWB = A("sLNWB", 16, F32)
    St = A("sS", 1024, BF16)
    B1C = A("sB1C", 32, F32)
    B2C = A("sB2C", 8, F32)
    B1Q = A("sB1Q", 32, FP8)
    B2Q = A("sB2Q", 8, FP8)
    _ov2, _ = nc.bump_sbuf(2048)
    B1R = nc.alloc_sbuf_tensor_at("sB1R", [32, 128], F32, offset=_ov2).ap()
    B2R = nc.alloc_sbuf_tensor_at("sB2R", [8, 128], F32, offset=_ov2 + 512).ap()
    TDR = nc.alloc_sbuf_tensor_at("sTDR", [1, 16], F32, offset=_ov2 + 1024).ap()
    ABRROW = nc.alloc_sbuf_tensor_at("sABRROW", [1, 512], F32, offset=_ov2).ap()
    IDN = A("sIDN", 128, BF16)
    IDNF = A("sIDNF", 128, F32)
    ONESC = A("sONESC", 1, BF16)
    ONESRF = nc.alloc_sbuf_tensor("sONESRF", [1, 128], F32).ap()
    _ovl, _ = nc.bump_sbuf(2048)
    DIF = nc.alloc_sbuf_tensor_at("sDIF", [128, 128], F32, offset=_ovl).ap()
    REV = nc.alloc_sbuf_tensor_at("sREV", [128, 128], F32, offset=_ovl + 512).ap()
    IOT = nc.alloc_sbuf_tensor_at("sIOT", [128, 128], F32, offset=_ovl + 1024).ap()
    XSQ = nc.alloc_sbuf_tensor_at("sXSQ", [128, 512], F32R, offset=_ovl).ap()
    C128 = A("sC128", 1, F32)
    TD8 = A("sTD8", 8, F32)
    STATS = A("sSTATS", 16, F32)
    STATI = A("sSTATI", 4, I32)

    PB = [nc.alloc_psum_tensor(f"pb{i}", [128, 512], F32).ap() for i in range(8)]
    PB7B = PB[7].bitcast(BF16)

    S = Sched()

    _spr_rr = [0]

    def dma(q, dst, src, waits=(), sem="SPR", slow=False):
        if sem == "SPR":
            sem = ["SPR", "SPR1", "SPR2", "SPR3"][_spr_rr[0] % 4]
            _spr_rr[0] += 1
        prev = S.cnt.get(sem, 0)
        def fn(eng, dst=dst, src=src, slow=slow):
            return eng.dma_start(dst, src, allow_slow_non_contiguous=slow)
        return S.emit(q, fn, tuple(waits) + (((sem, prev),) if prev else ()),
                      inc=sem, amt=16)

    def mm(out, lhsT, rhs, start, stop, waits=(), inc=None, skip=False):
        def fn(eng, out=out, lhsT=lhsT, rhs=rhs, start=start, stop=stop, skip=skip):
            return nc.tensor.matmul(out, lhsT, rhs, start=start, stop=stop,
                                    skip_group_check=skip)
        return S.emit("tensor", fn, waits, inc=inc)

    def tr(out, in_, idn, waits=(), inc=None):
        def fn(eng, out=out, in_=in_, idn=idn):
            return nc.tensor.transpose(out, in_, idn)
        return S.emit("tensor", fn, waits, inc=inc)

    def act(out, in_, func, waits=(), inc="sACT", scale=None, bias=None):
        def fn(eng, out=out, in_=in_, func=func, scale=scale, bias=bias):
            kw = {}
            if scale is not None:
                kw["scale"] = scale
            if bias is not None:
                kw["bias"] = bias
            return nc.scalar.activation(out, in_, func, **kw)
        return S.emit("scalar", fn, waits, inc=inc)

    def vcopy(out, in_, waits=(), inc="sDVE", eng="vector"):
        def fn(e, out=out, in_=in_, eng=eng):
            return getattr(nc, eng).tensor_copy(out, in_)
        return S.emit(eng, fn, waits, inc=inc)

    def vtt(out, a, b, op, waits=(), inc="sDVE"):
        def fn(e, out=out, a=a, b=b, op=op):
            return nc.vector.tensor_tensor(out, a, b, op)
        return S.emit("vector", fn, waits, inc=inc)

    def vts(out, in_, s1, s2, op0, op1=None, waits=(), inc="sDVE"):
        def fn(e, out=out, in_=in_, s1=s1, s2=s2, op0=op0, op1=op1):
            if op1 is None:
                return nc.vector.tensor_scalar(out, in_, s1, None, op0)
            return nc.vector.tensor_scalar(out, in_, s1, s2, op0, op1)
        return S.emit("vector", fn, waits, inc=inc)

    def vstt(out, in0, scal, in1, op0, op1, waits=(), inc="sDVE"):
        def fn(e, out=out, in0=in0, scal=scal, in1=in1, op0=op0, op1=op1):
            return nc.vector.scalar_tensor_tensor(out, in0, scal, in1, op0, op1)
        return S.emit("vector", fn, waits, inc=inc)

    tc0 = dma("sync", IDN[:], didn[:])
    dma("sync", IDNF[:], didnf[:])
    dma("sync", ONESC[:], dconc[:])
    dma("sync", ONESRF[:], dconrf[:])
    dma("sync", DIF[:], ddif[:])
    dma("sync", REV[:], drev[:])
    dma("sync", IOT[:], diot[:])
    dma("sync", C128[:], dc128[:])
    dma("sync", TD8[:], dtd.rearrange("(c h2) d -> (h2 d) c", c=8, h2=2), slow=True)
    dma("sync", UC[:], dtf.rearrange("(c h2) d -> (h2 d) c", c=8, h2=2), slow=True)
    dma("sync", TDR[:], dtd[:, 0:1].rearrange("h one -> one h"), slow=True)
    dma("sync", B1R[:], db1.rearrange("(c p) -> c p", p=128))
    tcon = dma("sync", B2R[:], db2.rearrange("(c p) -> c p", p=128))

    HQF = HQ.bitcast(F32)
    stg = [XT, HQF]
    stage4 = [XT[:, 0:1024], XT[:, 1024:2048], HQF[:, 0:1024], HQF[:, 1024:2048]]
    q_free = {}
    wmix_last = None
    for wi, (dsrc, wdst) in enumerate([(dwk, WKs), (dwv, WVs), (dwr, WRs), (dwo, WOs)]):
        for kc in range(PC):
            sl = (wi * PC + kc) % 4
            t_ld = dma("sync", stage4[sl], dsrc[128 * kc:128 * (kc + 1), :],
                       waits=(q_free.get(sl),))
            t_c = vcopy(wdst[:, kc * 1024:(kc + 1) * 1024], stage4[sl], waits=(t_ld,))
            q_free[sl] = t_c
            wmix_last = t_c
    wk_tok = wmix_last

    stage2 = [XT[:, 0:2048], HQF[:, 0:2048]]
    half_free = {0: (q_free.get(0), q_free.get(1)), 1: (q_free.get(2), q_free.get(3))}
    fi = 0
    w1_done, w2_done = [], []
    for kc in range(PC):
        for hh in range(2):
            sl = fi % 2
            t_ld = dma("sync", stage2[sl],
                       dw1[128 * kc:128 * (kc + 1), 2048 * hh:2048 * (hh + 1)],
                       waits=(half_free[sl],))
            dst = W1S[:, kc * 4096 + 2048 * hh: kc * 4096 + 2048 * (hh + 1)]
            eng = ["scalar", "vector", "gpsimd"][fi % 3]
            if eng == "scalar":
                t_c = act(dst, stage2[sl], AF.Identity, waits=(t_ld,))
            else:
                t_c = vcopy(dst, stage2[sl], waits=(t_ld,),
                            inc="sDVE" if eng == "vector" else "sPOOL", eng=eng)
            half_free[sl] = t_c
            w1_done.append(t_c)
            fi += 1
    for j in range(PC):
        srcj = dw2[512 * j:512 * (j + 1), :].rearrange("(j2 p) o -> p j2 o", p=128)
        for hh in range(2):
            sl = fi % 2
            t_ld = dma("sync",
                       stage2[sl].rearrange("p (j2 o) -> p j2 o", o=1024),
                       srcj[:, 2 * hh:2 * hh + 2, :],
                       waits=(half_free[sl],))
            dst = W2S[:, j * 4096 + 2048 * hh: j * 4096 + 2048 * (hh + 1)]
            eng = ["scalar", "vector", "gpsimd"][fi % 3]
            if eng == "scalar":
                t_c = act(dst, stage2[sl], AF.Identity, waits=(t_ld,))
            else:
                t_c = vcopy(dst, stage2[sl], waits=(t_ld,),
                            inc="sDVE" if eng == "vector" else "sPOOL", eng=eng)
            half_free[sl] = t_c
            w2_done.append(t_c)
            fi += 1
    stage_done = dict(half_free)

    t_lnw = act(LNW[:], TD8[:], AF.Ln, waits=(tcon,))
    t_lnr = act(STATS[0:1, 0:16], TDR[:], AF.Ln, waits=(tcon,))
    t_lb = mm(PB[6][:, 0:16], ONESRF[0:1, :], STATS[0:1, 0:16], True, True,
              waits=(t_lnr,), inc="sPE")
    t_lnwb = vcopy(LNWB[:], PB[6][:, 0:16], waits=(t_lb,))
    S.emit("vector", lambda e: nc.vector.memset(STATS[:, 0:2], 1.0), (t_lb,), inc="sDVE")
    vcopy(ONES2R[:], STATS[:, 0:2])
    tb_toks = []
    for h in range(H):
        tb_toks.append(act(TB[:, h * 128:(h + 1) * 128], DIF[:], AF.Exp,
                           scale=LNWB[:, h:h + 1], waits=(t_lnwb,)))
    tk_toks, trt_toks, wc_toks = [], [], []
    for p in range(PC):
        tk_toks.append(act(TK[:, p * 128:(p + 1) * 128], REV[:], AF.Exp,
                           scale=LNW[:, p:p + 1], waits=(t_lnw,)))
        trt_toks.append(act(TRt[:, p * 128:(p + 1) * 128], IOT[:], AF.Exp,
                            scale=LNW[:, p:p + 1], waits=(t_lnw,)))
        wc_toks.append(act(WC[:, p:p + 1], C128[:], AF.Exp,
                           scale=LNW[:, p:p + 1], waits=(t_lnw,)))

    t_tkt = None
    for p in range(PC):
        t_tkt = tr(PB7B[:, p * 128:(p + 1) * 128], TK[:, p * 128:(p + 1) * 128],
                   IDN[:], waits=(tk_toks[p], tc0),
                   inc="sPE" if p == PC - 1 else None)
    t_tkt = vcopy(TKT[:], PB7B[:, 0:1024], waits=(t_tkt,))

    t_b1t = tr(PB[6][0:128, 16:48], B1R[:], IDNF[0:32, 0:32], waits=(tcon, t_lnwb), inc="sPE")
    t_b1e = vcopy(B1C[:], PB[6][:, 16:48], waits=(t_b1t,))
    t_e = vcopy(B1Q[:], B1C[:], waits=(t_b1e,))
    t_b1c = vcopy(B1C[:], B1Q[:], waits=(t_e,))
    t_b2t = tr(PB[6][0:128, 48:56], B2R[:], IDNF[0:8, 0:8], waits=(t_b1e,), inc="sPE")
    t_e = vcopy(B2C[:], PB[6][:, 48:56], waits=(t_b2t,))
    t_e = vcopy(B2Q[:], B2C[:], waits=(t_e,))
    t_b2c = vcopy(B2C[:], B2Q[:], waits=(t_e,))

    regs = [(0, 0), (1, 0), (2, 0), (3, 0), (4, 0), (5, 0)]
    t_pre = dma("sync", OUT[:, 0:1024].rearrange("p (pc t) -> p pc t", t=CH),
                dxpre.rearrange("(pc p) t -> p pc t", p=128))
    t_pxb = act(XBF[:, 0:1024], OUT[:, 0:1024], AF.Identity, waits=(t_pre,))
    reg_free = {}
    pre_kp = []
    for mc in range(PC):
        b, off = regs[mc % 6]
        tok = None
        for kc in range(PC):
            tok = mm(PB[b][:, off:off + 128],
                     WKs[:, kc * 1024 + mc * 128: kc * 1024 + mc * 128 + 128],
                     XBF[:, kc * 128:(kc + 1) * 128],
                     start=(kc == 0), stop=(kc == PC - 1),
                     waits=(t_pxb, wk_tok, reg_free.get(mc % 6)),
                     inc="sPE" if kc == PC - 1 else None)
        t_ev = act(KP[:, mc * 128:(mc + 1) * 128], PB[b][:, off:off + 128],
                   AF.Identity, scale=UC[:, mc:mc + 1], waits=(tok,))
        reg_free[mc % 6] = t_ev
        pre_kp.append(t_ev)
    pre_v = []
    for eh in range(2):
        tok = None
        for kc in range(PC):
            tok = mm(PB[eh][:, 0:512],
                     XBF[:, kc * 128:(kc + 1) * 128],
                     WVs[:, kc * 1024 + eh * 512: kc * 1024 + (eh + 1) * 512],
                     start=(kc == 0), stop=(kc == PC - 1),
                     waits=(reg_free.get(eh),),
                     inc="sPE" if kc == PC - 1 else None)
        pre_v.append(vcopy(VT[:, eh * 512:(eh + 1) * 512], PB[eh][:, 0:512],
                           waits=(tok,)))
    reg_free[0] = pre_v[0]
    reg_free[1] = pre_v[1]
    t_tp = None
    for p in range(PC):
        t_tp = tr(PB7B[:, p * 128:(p + 1) * 128], KP[:, p * 128:(p + 1) * 128],
                  IDN[:], waits=(pre_kp[p], t_tkt if p == 0 else None),
                  inc="sPE" if p == PC - 1 else None)
    t_kht = vtt(KHT[:], PB7B[:, 0:1024], TKT[:], ALU.mult, waits=(t_tp,))
    pre_m = []
    m_last = None
    for p in range(PC):
        mm(PB[4][0:64, p * 64:(p + 1) * 64], KHT[:, p * 128:p * 128 + 64],
           VT[:, 128 * p: 128 * p + 64], True, True,
           waits=(t_kht, pre_v[0], pre_v[1]) + ((t_b2c,) if p == 0 else ()))
        m_last = mm(PB[4][64:128, p * 64:(p + 1) * 64],
                    KHT[:, p * 128 + 64:p * 128 + 128],
                    VT[:, 128 * p + 64: 128 * p + 128], True, True,
                    inc="sPE" if p == PC - 1 else None)
    for p in range(PC):
        pre_m.append(vcopy(St[:, p * 64:(p + 1) * 64],
                           PB[4][:, p * 64:(p + 1) * 64], waits=(m_last,)))

    tok_xfree1 = (stage_done.get(0), stage_done.get(1))
    tok_xfree2 = (stage_done.get(0), stage_done.get(1))
    tok_vt_free = m_last
    tok_kp_done = t_tp
    tok_rs_done = None
    tok_xbf_done = pre_v[1]
    tok_kht_free = None
    tok_rt_free = None
    tok_x2q_done = None
    tok_s_upd = pre_m[-1]
    tok_pb7_free = t_kht
    tok_pb6_free = t_b2c
    tok_abr_free = None
    tok_tmp1_free = None
    tok_stat_prev = None
    tok_kht_ln = None
    at_free = None
    tok_hq_free = None
    prev_t_x2q = None
    prev_store = t_pxb
    g_free = {}
    y_free = {4: pre_m[-1]}
    last_v_ev = pre_v[1]
    gk = 0

    def emit_front_K(i):
        nonlocal tok_kp_done, tok_xbf_done, tok_xfree1, tok_xfree2
        t0 = i * T
        srcA = dxT[0:512, t0:t0 + T].rearrange("(pc p) t -> p pc t", p=128)
        srcB = dxT[512:1024, t0:t0 + T].rearrange("(pc p) t -> p pc t", p=128)
        t_x1 = dma("sync", XT[:, 0:1024].rearrange("p (pc t) -> p pc t", t=T),
                   srcA, waits=(tok_xfree1,) if not isinstance(tok_xfree1, tuple)
                   else tok_xfree1, sem="SX")
        t_xb1 = act(XBF[:, 0:1024], XT[:, 0:1024], AF.Identity,
                    waits=(t_x1, tok_xbf_done))
        t_x2 = dma("sync", XT[:, 1024:2048].rearrange("p (pc t) -> p pc t", t=T),
                   srcB, waits=(tok_xfree2,) if not isinstance(tok_xfree2, tuple)
                   else tok_xfree2, sem="SX")
        t_xb2 = act(XBF[:, 1024:2048], XT[:, 1024:2048], AF.Identity,
                    waits=(t_x2,))
        t_xbf = (t_xb1, t_xb2)
        kp_ev = []
        for mc in range(PC):
            b, off = regs[mc % 6]
            tok = None
            for kc in range(PC):
                tok = mm(PB[b][:, off:off + 256],
                         WKs[:, kc * 1024 + mc * 128: kc * 1024 + mc * 128 + 128],
                         XBF[:, kc * 256:(kc + 1) * 256],
                         start=(kc == 0), stop=(kc == PC - 1),
                         waits=(t_xb1 if kc < 4 else t_xb2,
                                reg_free.get(mc % 6),
                                (g_free.get(b), y_free.get(b)) if kc == 0 else None),
                         inc="sPE" if kc == PC - 1 else None)
            t_ev = act(KP[:, mc * 256:(mc + 1) * 256], PB[b][:, off:off + 256],
                       AF.Identity, scale=UC[:, mc:mc + 1],
                       waits=(tok, tok_kp_done if mc == 0 else None))
            reg_free[mc % 6] = t_ev
            kp_ev.append(t_ev)
        return t_xbf, kp_ev

    def emit_front_R(i, t_xbf):
        nonlocal tok_rs_done
        rs_ev = []
        for mc in range(PC):
            b, off = regs[mc % 6]
            tok = None
            for kc in range(PC):
                tok = mm(PB[b][:, off:off + 256],
                         WRs[:, kc * 1024 + mc * 128: kc * 1024 + mc * 128 + 128],
                         XBF[:, kc * 256:(kc + 1) * 256],
                         start=(kc == 0), stop=(kc == PC - 1),
                         waits=(t_xbf[0 if kc < 4 else 1],
                                reg_free.get(mc % 6)),
                         inc="sPE" if kc == PC - 1 else None)
            t_ev = act(RS[:, mc * 256:(mc + 1) * 256], PB[b][:, off:off + 256],
                       AF.Sigmoid, waits=(tok, tok_rs_done if mc == 0 else None))
            reg_free[mc % 6] = t_ev
            rs_ev.append(t_ev)
        return rs_ev

    def emit_front(i):
        t_xbf, kp_ev = emit_front_K(i)
        rs_ev = emit_front_R(i, t_xbf)
        return t_xbf, kp_ev, rs_ev

    def emit_mask(i, w1, w2):
        msrc = dmsk[:, i * T:(i + 1) * T].rearrange("(pc p) t -> p pc t", p=128)
        ta = dma("sync", MSK[:].rearrange("p (pc t) -> p pc t", t=T),
                 msrc[:, 0:4, :], waits=(w1,), sem="SM")
        tb = dma("sync", MSK2[:].rearrange("p (pc t) -> p pc t", t=T),
                 msrc[:, 4:8, :], waits=(w2,), sem="SM")
        return ta, tb

    def emit_mixer_V(c):
        nonlocal tok_vt_free, last_v_ev
        v_ev_c = None
        for eh in range(2):
            tok = None
            for kc in range(PC):
                tok = mm(PB[eh][:, 0:512],
                         XBF[:, kc * 256 + c * 128: kc * 256 + c * 128 + 128],
                         WVs[:, kc * 1024 + eh * 512: kc * 1024 + (eh + 1) * 512],
                         start=(kc == 0), stop=(kc == PC - 1),
                         waits=(reg_free.get(eh),
                                (g_free.get(eh), y_free.get(eh)) if kc == 0 else None),
                         inc="sPE" if kc == PC - 1 else None)
            t_ev = act(VT[:, eh * 512:(eh + 1) * 512], PB[eh][:, 0:512],
                       AF.Identity,
                       waits=(tok,) + ((tok_vt_free,) if eh == 0 else ()))
            reg_free[eh] = t_ev
            if eh == 1:
                v_ev_c = t_ev
        last_v_ev = v_ev_c
        return v_ev_c

    def emit_mixer_T(c, kp_ev):
        nonlocal tok_pb7_free, tok_kht_free, tok_kp_done
        t_tp = None
        for p in range(PC):
            t_tp = tr(PB7B[:, p * 128:(p + 1) * 128],
                      KP[:, p * 256 + c * 128: p * 256 + c * 128 + 128], IDN[:],
                      waits=(kp_ev[p], tok_pb7_free if p == 0 else None),
                      inc="sPE" if p == PC - 1 else None)
        t_kht = vtt(KHT[:], PB7B[:, 0:1024], TKT[:], ALU.mult,
                    waits=(t_tp, tok_kht_free, tok_kht_ln))
        tok_pb7_free = t_kht
        if c == 1:
            tok_kp_done = t_kht
        return t_kht

    def emit_mixer_A(c, kp_ev):
        v_ev_c = emit_mixer_V(c)
        t_kht = emit_mixer_T(c, kp_ev)
        return v_ev_c, t_kht

    def emit_rt(c, rs_ev):
        nonlocal tok_rt_free, tok_rs_done
        rt_toks = []
        for p in range(PC):
            rt_toks.append(vtt(RT[:, p * 128:(p + 1) * 128],
                               RS[:, p * 256 + c * 128: p * 256 + c * 128 + 128],
                               TRt[:, p * 128:(p + 1) * 128], ALU.mult,
                               waits=(rs_ev[p], trt_toks[p],
                                      tok_rt_free if p == 0 else None)))
        if c == 1:
            tok_rs_done = rt_toks[-1]
        return rt_toks

    def emit_mixer_B(c, kp_ev, rs_ev, v_ev_c, t_kht, rt_toks=None):
        nonlocal tok_s_upd, tok_kht_free, tok_rt_free, tok_vt_free, at_free, gk
        scur = gk % 2
        snxt = 1 - scur
        if rt_toks is None:
            rt_toks = emit_rt(c, rs_ev)
        g_toks = {}
        bank_last = {}
        for wv in range(2):
            for pidx in range(4):
                p = wv * 4 + pidx
                for hh in range(2):
                    bank = (2 + hh) if wv == 0 else hh
                    off = pidx * 128
                    w = [kp_ev[p], rs_ev[p]]
                    if pidx == 0:
                        w += [g_free.get(bank), reg_free.get(bank),
                              y_free.get(bank)]
                    g_toks[(p, hh)] = mm(
                        PB[bank][:, off: off + 128],
                        KP[64 * hh:64 * hh + 64,
                           p * 256 + c * 128: p * 256 + c * 128 + 128],
                        RS[64 * hh:64 * hh + 64,
                           p * 256 + c * 128: p * 256 + c * 128 + 128],
                        True, True, waits=tuple(w), inc="sPE")
                    bank_last[bank] = g_toks[(p, hh)]
        for wv in range(2):
            ybank = 4 + wv
            for pidx in range(4):
                p = wv * 4 + pidx
                off = pidx * 128
                for hh in range(2):
                    mm(PB[ybank][64 * hh:64 * hh + 64, off:off + 128],
                       St[64 * hh:64 * hh + 64,
                          scur * 512 + p * 64: scur * 512 + (p + 1) * 64],
                       RT[64 * hh:64 * hh + 64, p * 128:(p + 1) * 128],
                       start=(pidx == 0), stop=False,
                       waits=(tok_s_upd, rt_toks[p],
                              (y_free.get(ybank), g_free.get(ybank),
                               reg_free.get(ybank))
                              if pidx == 0 and hh == 0 else None),
                       skip=True)
        at_toks = {}
        for wv in range(2):
            for pidx in range(4):
                p = wv * 4 + pidx
                for hh in range(2):
                    bank = (2 + hh) if wv == 0 else hh
                    off = pidx * 128
                    h = 2 * p + hh
                    sl = wv * 8 + 2 * pidx + hh
                    at = vtt(AT[:, sl * 128:(sl + 1) * 128],
                             PB[bank][:, off: off + 128],
                             TB[:, h * 128:(h + 1) * 128], ALU.mult,
                             waits=(bank_last[bank], tb_toks[h],
                                    at_free if sl == 0 else None))
                    at_toks[h] = at
                    g_free[bank] = at
        y_ev = []
        yev_w1 = None
        for wv in range(2):
            ybank = 4 + wv
            ybank_last = None
            for pidx in range(4):
                p = wv * 4 + pidx
                off = pidx * 128
                for hh in range(2):
                    h = 2 * p + hh
                    sl = wv * 8 + 2 * pidx + hh
                    ybank_last = mm(
                        PB[ybank][64 * hh:64 * hh + 64, off:off + 128],
                        VT[:, p * 128 + 64 * hh: p * 128 + 64 * hh + 64],
                        AT[:, sl * 128:(sl + 1) * 128],
                        start=False, stop=True,
                        waits=(v_ev_c, at_toks[h]), inc="sPE", skip=True)
            for pidx in range(4):
                p = wv * 4 + pidx
                off = pidx * 128
                ev = act(YF[:, p * 256 + c * 128: p * 256 + c * 128 + 128],
                         PB[ybank][:, off:off + 128], AF.Identity,
                         waits=(ybank_last,))
                y_free[ybank] = ev
                y_ev.append(ev)
            at_free = ybank_last
            if wv == 0:
                yev_w1 = y_ev[3]
        m_last = None
        for p in range(PC):
            mm(PB[4][0:64, p * 64:(p + 1) * 64], KHT[:, p * 128:p * 128 + 64],
               VT[:, 128 * p: 128 * p + 64], True, True,
               waits=(t_kht, v_ev_c) + ((yev_w1,) if p == 0 else ()))
            m_last = mm(PB[4][64:128, p * 64:(p + 1) * 64],
                        KHT[:, p * 128 + 64:p * 128 + 128],
                        VT[:, 128 * p + 64: 128 * p + 128],
                        True, True, inc="sPE" if p == PC - 1 else None)
        s_upd = []
        for p in range(PC):
            s_upd.append(vstt(St[:, snxt * 512 + p * 64: snxt * 512 + (p + 1) * 64],
                              St[:, scur * 512 + p * 64: scur * 512 + (p + 1) * 64],
                              WC[:, p:p + 1], PB[4][:, p * 64:(p + 1) * 64],
                              ALU.mult, ALU.add, waits=(m_last, wc_toks[p])))
        tok_s_upd = s_upd[-1]
        tok_kht_free = s_upd[-1]
        tok_rt_free = y_ev[-1]
        y_free[4] = s_upd[-1]
        tok_vt_free = m_last
        gk += 1
        return y_ev

    def emit_mixer_chunk(c, kp_ev, rs_ev):
        v_ev_c, t_kht = emit_mixer_A(c, kp_ev)
        return emit_mixer_B(c, kp_ev, rs_ev, v_ev_c, t_kht)

    def emit_attn_mms(y_ev_all, lo, hi):
        toks = []
        for mc in range(lo, hi):
            b, off = regs[mc % 6]
            tok = None
            for kc in range(PC):
                tok = mm(PB[b][:, off:off + 256],
                         WOs[:, kc * 1024 + mc * 128: kc * 1024 + mc * 128 + 128],
                         YF[:, kc * 256:(kc + 1) * 256],
                         start=(kc == 0), stop=(kc == PC - 1),
                         waits=(y_ev_all[kc], y_ev_all[8 + kc],
                                reg_free.get(mc % 6),
                                (g_free.get(b), y_free.get(b)) if kc == 0 else None),
                         inc="sPE" if kc == PC - 1 else None)
            toks.append((mc, tok))
        return toks

    def emit_attn_evacs(i, mm_toks):
        nonlocal tok_xfree1, tok_xfree2
        t_xbf_i = fronts[i][0]
        xr_toks = []
        for mc, tok in mm_toks:
            b, off = regs[mc % 6]
            t_ev = vtt(XR[:, mc * 256:(mc + 1) * 256],
                       XT[:, mc * 256:(mc + 1) * 256],
                       PB[b][:, off:off + 256], ALU.add, waits=(tok,))
            reg_free[mc % 6] = t_ev
            xr_toks.append(t_ev)
            if mc == 3:
                tok_xfree1 = (t_ev, t_xbf_i[0])
            elif mc == PC - 1:
                tok_xfree2 = (t_ev, t_xbf_i[1])
        return xr_toks

    def emit_attn(i, y_ev_all):
        xr_toks = []
        for mc in range(PC):
            xr_toks += emit_attn_evacs(i, emit_attn_mms(y_ev_all, mc, mc + 1))
        return xr_toks

    def ln_stats_make():
        nonlocal tok_pb6_free, tok_stat_prev, tok_kht_ln
        st = {"sq_free": [tok_stat_prev, None], "t": None}
        pb6w = (tok_pb6_free, tok_stat_prev)
        def piece(p, xr_tok):
            first = (p == 0)
            for th in range(2):
                mm(PB[6][:, 2 * th:2 * th + 2],
                   XR[:, p * 256 + th * 128: p * 256 + th * 128 + 128],
                   ONES2R[:, 0:2], start=first and th == 0, stop=(p == PC - 1),
                   waits=(xr_tok,) + (pb6w if first and th == 0 else ()),
                   skip=True)
            sq = XSQ[:, (p % 2) * 256:(p % 2) * 256 + 256]
            t_sq = vtt(sq, XR[:, p * 256:(p + 1) * 256],
                       XR[:, p * 256:(p + 1) * 256], ALU.mult,
                       waits=(xr_tok, st["sq_free"][p % 2],
                              wc_toks[-1] if p < 2 else None))
            t_stat = None
            for th in range(2):
                t_stat = mm(PB[6][:, 4 + 2 * th:6 + 2 * th],
                            sq[:, th * 128:(th + 1) * 128],
                            ONES2R[:, 0:2],
                            start=(p == 0 and th == 0), stop=(p == PC - 1),
                            waits=(t_sq,), inc="sPE", skip=True)
            st["sq_free"][p % 2] = t_stat
            st["t"] = t_stat
            return t_stat
        def fin():
            nonlocal tok_stat_prev, tok_kht_ln
            tok_stat_prev = st["t"]
            tok_kht_ln = st["t"]
            return st["t"]
        return piece, fin

    def ln_stats(xr_toks):
        nonlocal tok_pb6_free, tok_stat_prev, tok_kht_ln
        first = True
        for p in range(PC):
            for th in range(2):
                mm(PB[6][:, 2 * th:2 * th + 2],
                   XR[:, p * 256 + th * 128: p * 256 + th * 128 + 128],
                   ONES2R[:, 0:2], start=first, stop=(p == PC - 1),
                   waits=(xr_toks[p],) + ((tok_pb6_free, tok_stat_prev) if first else ()),
                   skip=True)
                first = False
        t_stat = None
        sq_free = [tok_stat_prev, None]
        for p in range(PC):
            sq = XSQ[:, (p % 2) * 256:(p % 2) * 256 + 256]
            t_sq = vtt(sq, XR[:, p * 256:(p + 1) * 256],
                       XR[:, p * 256:(p + 1) * 256], ALU.mult,
                       waits=(xr_toks[p], sq_free[p % 2],
                              wc_toks[-1] if p < 2 else None))
            for th in range(2):
                t_stat = mm(PB[6][:, 4 + 2 * th:6 + 2 * th],
                            sq[:, th * 128:(th + 1) * 128],
                            ONES2R[:, 0:2],
                            start=(p == 0 and th == 0), stop=(p == PC - 1),
                            waits=(t_sq,), inc="sPE", skip=True)
            sq_free[p % 2] = t_stat
        tok_stat_prev = t_stat
        tok_kht_ln = t_stat
        return t_stat

    def ln_chain(t_stat):
        vts(STATS[:, 0:4], PB[6][:, 0:4], 1.0 / D, None, ALU.mult, waits=(t_stat,))
        vts(STATS[:, 4:8], PB[6][:, 4:8], 1.0 / D, None, ALU.mult)
        vtt(STATS[:, 8:12], STATS[:, 0:4], STATS[:, 0:4], ALU.mult)
        vtt(STATS[:, 8:12], STATS[:, 4:8], STATS[:, 8:12], ALU.subtract)
        vts(STATS[:, 8:12], STATS[:, 8:12], EPS, None, ALU.add)
        vcopy(STATS[:, 12:16], STATS[:, 8:12].bitcast(I32))
        vts(STATS[:, 12:16], STATS[:, 12:16], -0.5, MAGIC, ALU.mult, ALU.add)
        vcopy(STATI[:, 0:4], STATS[:, 12:16])
        Y0 = STATI.bitcast(F32)
        for _ in range(2):
            vtt(STATS[:, 12:16], Y0[:, 0:4], Y0[:, 0:4], ALU.mult)
            vtt(STATS[:, 12:16], STATS[:, 12:16], STATS[:, 8:12], ALU.mult)
            vts(STATS[:, 12:16], STATS[:, 12:16], -0.5, 1.5, ALU.mult, ALU.add)
            vtt(Y0[:, 0:4], Y0[:, 0:4], STATS[:, 12:16], ALU.mult)
        vtt(STATS[:, 4:8], STATS[:, 0:4], Y0[:, 0:4], ALU.mult)
        t_pk = vts(STATS[:, 0:4], STATS[:, 4:8], -1.0, None, ALU.mult)
        return t_pk

    def ln_tr(t_pk):
        t_tr = None
        Y0F = STATI.bitcast(F32)
        srcs = [Y0F[:, 0:1], Y0F[:, 2:3], STATS[:, 0:1], STATS[:, 2:3]]
        for r in range(4):
            t_tr = tr(PB[6][0:1, r * 128:(r + 1) * 128],
                      srcs[r], IDNF[:],
                      waits=(t_pk,), inc="sPE" if r == 3 else None)
        return t_tr

    def ln_bc(t_tr):
        nonlocal tok_abr_free
        t_abr = vcopy(ABRROW[0:1, :], PB[6][0:1, 0:512],
                      waits=(t_tr, tok_abr_free))
        mm(PB[6][:, 256:512], ONESRF[0:1, :], ABRROW[0:1, 0:256], True, True,
           waits=(t_abr,), skip=True)
        t_bc = mm(PB[6][:, 0:256], ONESRF[0:1, :], ABRROW[0:1, 256:512],
                  True, True, inc="sPE", skip=True)
        tok_abr_free = t_bc
        return t_bc

    def ln_bcast(t_pk):
        return ln_bc(ln_tr(t_pk))

    def ln_apply(t_bc, out_writer):
        nonlocal tok_pb6_free, tok_tmp1_free
        out_toks = []
        for p in range(PC):
            vtt(TMP1[:, 0:256], XR[:, p * 256:(p + 1) * 256],
                PB[6][:, 256:512], ALU.mult,
                waits=(t_bc, tok_tmp1_free if p == 0 else None))
            tok_o = out_writer(p)
            tok_tmp1_free = tok_o
            out_toks.append(tok_o)
        tok_pb6_free = out_toks[-1]
        return out_toks

    def ln_bcast_apply(t_pk, out_writer):
        return ln_apply(ln_bcast(t_pk), out_writer)

    fronts = {0: emit_front(0)}
    t_m, t_m2 = emit_mask(0, None, t_tkt)
    y_all = emit_mixer_chunk(0, fronts[0][1], fronts[0][2])
    y_all += emit_mixer_chunk(1, fronts[0][1], fronts[0][2])
    xr_toks = emit_attn(0, y_all)

    for i in range(nt):
        t0 = i * T
        nxt = i + 1 < nt

        t_stat1 = ln_stats(xr_toks)
        if nxt:
            t_xbf_n, kp_n = emit_front_K(i + 1)
        t_pk1 = ln_chain(t_stat1)
        t_tr1 = ln_tr(t_pk1)
        if nxt:
            v0 = emit_mixer_V(0)
        t_bc1 = ln_bc(t_tr1)
        if nxt:
            kht0 = emit_mixer_T(0, kp_n)

        def ln1_writer(p):
            return vtt(X2B[:, p * 256:(p + 1) * 256], TMP1[:, 0:256],
                       PB[6][:, 0:256], ALU.add,
                       waits=((prev_t_x2q,) if p == 0 and prev_t_x2q else ()))
        x2_toks = ln_apply(t_bc1, ln1_writer)
        t_x2q = None
        for p in range(PC):
            t_x2q = act(X2Q[:, p * 256:(p + 1) * 256],
                        X2B[:, p * 256:(p + 1) * 256], AF.Identity,
                        waits=(x2_toks[p],) + ((tok_x2q_done,) if p == 0 else ()))
        prev_t_x2q = t_x2q
        if nxt:
            rs_n = emit_front_R(i + 1, t_xbf_n)
            y_next = emit_mixer_B(0, kp_n, rs_n, v0, kht0)

        hq_toks = []
        h16_free = {0: None, 1: None}
        for mc in range(FC):
            b, off = regs[mc % 6]
            tok = None
            for kc in range(PC):
                tok = mm(PB[b][:, off:off + 256],
                         W1S[:, kc * 4096 + mc * 128: kc * 4096 + mc * 128 + 128],
                         X2Q[:, kc * 256:(kc + 1) * 256],
                         start=(kc == 0), stop=(kc == PC - 1),
                         waits=(t_x2q, reg_free.get(mc % 6),
                                (g_free.get(b), y_free.get(b)) if kc == 0 else None)
                         + (tuple(w1_done) if i == 0 and mc == 0 else ()),
                         inc="sPE" if kc == PC - 1 else None)
            hb = (H16 if mc % 2 == 0 else H16B)[:, 0:256]
            t_h = act(hb, PB[b][:, off:off + 256], AF.Identity,
                      bias=B1C[:, mc:mc + 1],
                      waits=(tok, h16_free[mc % 2],
                             t_b1c if i == 0 and mc == 0 else None))
            t_q = vts(HQ[:, mc * 256:(mc + 1) * 256], hb, 0.0, None, ALU.max,
                      waits=(t_h, tok_hq_free if mc == 0 else None)
                      + (tuple(w2_done) if i == 0 and mc == 0 else ()))
            h16_free[mc % 2] = t_q
            reg_free[mc % 6] = t_q
            hq_toks.append(t_q)

        if nxt:
            rt1 = emit_rt(1, rs_n)

        r2_toks = []
        for mc in range(PC):
            b, off = regs[mc % 6]
            tok = None
            for kc in range(FC):
                tok = mm(PB[b][:, off:off + 256],
                         W2S[:, kc * 1024 + mc * 128: kc * 1024 + mc * 128 + 128],
                         HQ[:, kc * 256:(kc + 1) * 256],
                         start=(kc == 0), stop=(kc == FC - 1),
                         waits=(hq_toks[kc], reg_free.get(mc % 6),
                                (g_free.get(b), y_free.get(b)) if kc == 0 else None),
                         inc="sPE" if kc == FC - 1 else None)
            hb = (H16 if mc % 2 == 0 else H16B)[:, 0:256]
            t_h = act(hb, PB[b][:, off:off + 256], AF.Identity,
                      bias=B2C[:, mc:mc + 1], waits=(tok, h16_free[mc % 2]))
            mbuf = MSK if mc < 4 else MSK2
            t_mul = vtt(TMP1[:, 0:256], hb,
                        mbuf[:, (mc % 4) * 256:((mc % 4) + 1) * 256],
                        ALU.mult, waits=(t_h, t_m if mc < 4 else t_m2,
                                         tok_tmp1_free))
            t_r2 = vtt(XR[:, mc * 256:(mc + 1) * 256], TMP1[:, 0:256],
                       X2B[:, mc * 256:(mc + 1) * 256], ALU.add,
                       waits=(t_mul,))
            tok_tmp1_free = t_r2
            h16_free[mc % 2] = t_mul
            reg_free[mc % 6] = t_r2
            r2_toks.append(t_r2)
            if mc == 3 and nxt:
                msrc_n = dmsk[:, (i + 1) * T:(i + 2) * T].rearrange(
                    "(pc p) t -> p pc t", p=128)
                t_mA = dma("sync", MSK[:].rearrange("p (pc t) -> p pc t", t=T),
                           msrc_n[:, 0:4, :], waits=(t_mul,), sem="SM")
            if mc == PC - 1 and nxt:
                msrc_n = dmsk[:, (i + 1) * T:(i + 2) * T].rearrange(
                    "(pc p) t -> p pc t", p=128)
                t_mB = dma("sync", MSK2[:].rearrange("p (pc t) -> p pc t", t=T),
                           msrc_n[:, 4:8, :], waits=(t_mul,), sem="SM")
        tok_hq_free = r2_toks[-1]
        tok_x2q_done = r2_toks[-1]
        if nxt:
            t_m, t_m2 = t_mA, t_mB

        t_stat2 = ln_stats(r2_toks)
        t_pk2 = ln_chain(t_stat2)
        if nxt:
            mA1 = emit_mixer_A(1, kp_n)
            y_next = y_next + emit_mixer_B(1, kp_n, rs_n, *mA1, rt_toks=rt1)
        t_tr2 = ln_tr(t_pk2)
        if nxt:
            fronts[i + 1] = (t_xbf_n, kp_n, rs_n)
            attn_p1 = emit_attn_mms(y_next, 0, 2)
        t_bc2 = ln_bc(t_tr2)

        ln2_state = {}

        def ln2_writer(p):
            nonlocal prev_store
            if p == 4:
                dst1 = dout[0:512, t0:t0 + T].rearrange("(pc p) t -> p pc t", p=128)
                ln2_state["st1"] = dma("gpsimd", dst1,
                                       OUT[:].rearrange("p (pc t) -> p pc t", t=T),
                                       waits=(ln2_state[3],), sem="SO")
            w = ()
            if p == 0:
                w = (prev_store,)
            elif p == 4:
                w = (ln2_state["st1"],)
            tok = vtt(OUT[:, (p % 4) * 256:((p % 4) + 1) * 256], TMP1[:, 0:256],
                      PB[6][:, 0:256], ALU.add, waits=w)
            ln2_state[p] = tok
            return tok
        o_toks = ln_apply(t_bc2, ln2_writer)
        if nxt:
            xr_toks = emit_attn_evacs(i + 1, attn_p1)
            xr_toks = xr_toks + emit_attn_evacs(
                i + 1, emit_attn_mms(y_next, 2, PC))
        dst2 = dout[512:1024, t0:t0 + T].rearrange("(pc p) t -> p pc t", p=128)
        prev_store = dma("gpsimd", dst2,
                         OUT[:].rearrange("p (pc t) -> p pc t", t=T),
                         waits=(o_toks[-1],), sem="SO")

    return nc, S


def realize(nc, S):
    import contextlib
    semh = {}
    ctx = contextlib.ExitStack()
    for s in SEMS:
        semh[s] = ctx.enter_context(nc.semaphore(s))
    block = ctx.enter_context(nc.Block())
    so_total = S.cnt.get("SO", 0)

    def make(engname):
        def body(engobj):
            for fn, waits, inc, amt in S.q[engname]:
                for (s, v) in waits:
                    engobj.wait_ge(semh[s], v)
                inst = fn(engobj)
                if inc is not None and inst is not None:
                    inst.then_inc(semh[inc], amt)
            if engname == "gpsimd":
                engobj.wait_ge(semh["SO"], so_total)
        return body

    for e in ENGINES:
        getattr(block, e)(make(e))
    ctx.close()
    return nc


def host_constants(ntok):
    import ml_dtypes
    bfd = ml_dtypes.bfloat16
    f8 = ml_dtypes.float8_e4m3
    s = np.arange(128)
    t = np.arange(128)
    dif = (t[None, :] - s[:, None]).astype(np.float32)
    dif[t[None, :] < s[:, None]] = 1e5
    rev = np.broadcast_to((127.0 - t).astype(np.float32), (128, 128)).copy()
    iot = np.broadcast_to((t + 1.0).astype(np.float32), (128, 128)).copy()
    return {
        "idn": np.eye(128, dtype=bfd),
        "idnf": np.eye(128, dtype=np.float32),
        "onesc": np.ones((128, 1), dtype=bfd),
        "onesrf": np.ones((1, 128), dtype=np.float32),
        "dif": dif,
        "rev": rev,
        "iot": iot,
        "c128": np.full((128, 1), 128.0, dtype=np.float32),
    }


NCORE = 8
NTOK = 4096


def kernel(L0_attn_x, L0_attn_Wk, L0_attn_Wv, L0_attn_Wr, L0_attn_Wo,
           L0_attn_time_decay, L0_attn_time_first,
           L0_ffn_w1, L0_ffn_b1, L0_ffn_w2, L0_ffn_b2):
    import ml_dtypes
    import jax
    from concourse.bass_utils import run_bass_kernel_spmd
    f8d = ml_dtypes.float8_e4m3

    x = np.asarray(L0_attn_x, dtype=np.float32)
    B, L, Dm = x.shape
    assert (B, L, Dm) == (4, 8192, 1024)

    cpu = jax.devices("cpu")[0]
    with jax.default_device(cpu):
        keep = np.asarray(jax.random.bernoulli(
            jax.random.key(42, impl="threefry2x32"), 0.8, (B, L, Dm)))
    maskv = np.where(keep, np.float32(1.25), np.float32(0.0))

    wkT = np.ascontiguousarray(np.asarray(L0_attn_Wk, np.float32).T)
    wvT = np.ascontiguousarray(np.asarray(L0_attn_Wv, np.float32).T)
    wrT = np.ascontiguousarray(np.asarray(L0_attn_Wr, np.float32).T)
    woT = np.ascontiguousarray(np.asarray(L0_attn_Wo, np.float32).T)
    w1T = np.ascontiguousarray(np.asarray(L0_ffn_w1, np.float32).T)
    w2T = np.ascontiguousarray(np.asarray(L0_ffn_w2, np.float32).T)
    td = np.asarray(L0_attn_time_decay, np.float32)
    tf = np.asarray(L0_attn_time_first, np.float32)
    b1 = np.asarray(L0_ffn_b1, np.float32)
    b2 = np.asarray(L0_ffn_b2, np.float32)
    con = host_constants(NTOK)
    zpre = np.zeros((Dm, CH), dtype=np.float32)

    in_maps = []
    for core in range(NCORE):
        b, hf = core // 2, core % 2
        sl = x[b, hf * NTOK:(hf + 1) * NTOK, :]
        xT = np.ascontiguousarray(sl.T)
        xpre = np.ascontiguousarray(x[b, NTOK - CH:NTOK, :].T) if hf == 1 else zpre
        mT = np.ascontiguousarray(maskv[b, hf * NTOK:(hf + 1) * NTOK, :].T).astype(f8d)
        in_maps.append(dict(
            xT=xT, xpre=xpre, wkT=wkT, wvT=wvT, wrT=wrT, woT=woT,
            w1T=w1T, w2T=w2T, td=td, tf=tf, b1=b1, b2=b2, maskT=mT, **con))

    nc, S = build_nc(NTOK)
    realize(nc, S)
    res = run_bass_kernel_spmd(nc, in_maps, core_ids=list(range(NCORE)))
    out = np.empty((B, L, Dm), dtype=np.float32)
    for core in range(NCORE):
        b, hf = core // 2, core % 2
        out[b, hf * NTOK:(hf + 1) * NTOK, :] = res.results[core]["outT"].T
    return out


# version 18
# speedup vs baseline: 1.1422x; 1.0061x over previous
import numpy as np
import concourse.bass as bass
import concourse.mybir as mybir

F32 = mybir.dt.float32
BF16 = mybir.dt.bfloat16
FP16 = mybir.dt.float16
F32R = mybir.dt.float32r
FP8 = mybir.dt.float8e4
I32 = mybir.dt.int32
AF = mybir.ActivationFunctionType
ALU = mybir.AluOpType

D = 1024
PC = 8
H = 16
FF = 4096
FC = 32
T = 256
CH = 128
EPS = 1e-5
MAGIC = float(0x5f3759df)

ENGINES = ("sync", "tensor", "vector", "scalar", "gpsimd")
OWN = {"tensor": "sPE", "vector": "sDVE", "scalar": "sACT", "gpsimd": "sPOOL"}
SEMS = ["SX", "SM", "SO", "SPR", "SPR1", "SPR2", "SPR3",
        "sPE", "sDVE", "sACT", "sPOOL"]


class Sched:
    def __init__(self):
        self.q = {e: [] for e in ENGINES}
        self.cnt = {}
        self.hw = {e: {} for e in ENGINES}

    def emit(self, eng, fn, waits=(), inc=None, amt=1):
        real = []
        flat = []
        for w in waits:
            if w is None:
                continue
            if isinstance(w, (list, tuple)) and w and isinstance(w[0], (list, tuple)):
                flat.extend(w)
            elif isinstance(w, (list, tuple)) and len(w) == 2 and isinstance(w[0], str):
                flat.append(w)
            elif isinstance(w, (list, tuple)):
                flat.extend(x for x in w if x is not None)
            else:
                raise ValueError(f"bad wait {w}")
        own = OWN.get(eng)
        if own is not None and eng != "tensor":
            prev = self.cnt.get(own, 0)
            if prev > 0:
                flat.append((own, prev))
        for w in flat:
            if w is None:
                continue
            s, v = w
            if s == own and self.cnt.get(own, 0) >= v and \
                    self.hw[eng].get(s, 0) >= v:
                continue
            if self.hw[eng].get(s, 0) >= v:
                continue
            self.hw[eng][s] = v
            real.append((s, v))
        tok = None
        if inc is not None:
            self.cnt[inc] = self.cnt.get(inc, 0) + amt
            tok = (inc, self.cnt[inc])
        self.q[eng].append((fn, real, inc, amt))
        return tok


def build_nc(ntok):
    nt = ntok // T
    nc = bass.Bass("TRN2", target_bir_lowering=False)

    dxT = nc.dram_tensor("xT", [D, ntok], F32, kind="ExternalInput").ap()
    dxpre = nc.dram_tensor("xpre", [D, CH], F32, kind="ExternalInput").ap()
    dwk = nc.dram_tensor("wkT", [D, D], F32, kind="ExternalInput").ap()
    dwv = nc.dram_tensor("wvT", [D, D], F32, kind="ExternalInput").ap()
    dwr = nc.dram_tensor("wrT", [D, D], F32, kind="ExternalInput").ap()
    dwo = nc.dram_tensor("woT", [D, D], F32, kind="ExternalInput").ap()
    dw1 = nc.dram_tensor("w1T", [D, FF], F32, kind="ExternalInput").ap()
    dw2 = nc.dram_tensor("w2T", [FF, D], F32, kind="ExternalInput").ap()
    dtd = nc.dram_tensor("td", [H, 64], F32, kind="ExternalInput").ap()
    dtf = nc.dram_tensor("tf", [H, 64], F32, kind="ExternalInput").ap()
    db1 = nc.dram_tensor("b1", [FF], F32, kind="ExternalInput").ap()
    db2 = nc.dram_tensor("b2", [D], F32, kind="ExternalInput").ap()
    dmsk = nc.dram_tensor("maskT", [D, ntok], FP8, kind="ExternalInput").ap()
    didn = nc.dram_tensor("idn", [128, 128], BF16, kind="ExternalInput").ap()
    didnf = nc.dram_tensor("idnf", [128, 128], F32, kind="ExternalInput").ap()
    dconc = nc.dram_tensor("onesc", [128, 1], BF16, kind="ExternalInput").ap()
    dconrf = nc.dram_tensor("onesrf", [1, 128], F32, kind="ExternalInput").ap()
    ddif = nc.dram_tensor("dif", [128, 128], F32, kind="ExternalInput").ap()
    drev = nc.dram_tensor("rev", [128, 128], F32, kind="ExternalInput").ap()
    diot = nc.dram_tensor("iot", [128, 128], F32, kind="ExternalInput").ap()
    dc128 = nc.dram_tensor("c128", [128, 1], F32, kind="ExternalInput").ap()
    dout = nc.dram_tensor("outT", [D, ntok], F32, kind="ExternalOutput").ap()

    A = lambda name, w, dt: nc.alloc_sbuf_tensor(name, [128, w], dt).ap()
    XT = A("sXT", 2048, F32)
    XBF = A("sXBF", 2048, BF16)
    KP = A("sKP", 2048, BF16)
    RS = A("sRS", 2048, BF16)
    VT = A("sVT", 1024, BF16)
    KHT = A("sKHT", 1024, BF16)
    TKT = A("sTKT", 1024, BF16)
    RT = A("sRT", 1024, BF16)
    YF = A("sYF", 2048, BF16)
    XR = A("sXR", 2048, F32R)
    X2B = A("sX2B", 2048, BF16)
    X2Q = A("sX2Q", 2048, FP8)
    ONES2R = A("sONES2R", 2, F32R)
    HQ = A("sHQ", 8192, FP8)
    H16 = A("sH16", 256, FP16)
    TMP1 = A("sTMP1", 256, BF16)
    AT = A("sAT", 2048, BF16)
    MSK = A("sMSK", 1024, FP8)
    OUT = A("sOUT", 1024, F32)
    WKs = A("sWK", 8192, BF16)
    WVs = A("sWV", 8192, BF16)
    WRs = A("sWR", 8192, BF16)
    WOs = A("sWO", 8192, BF16)
    W1S = A("sW1S", 32768, FP8)
    W2S = A("sW2S", 32768, FP8)
    TB = A("sTB", 2048, BF16)
    _ovk, _ = nc.bump_sbuf(2048)
    TK = nc.alloc_sbuf_tensor_at("sTK", [128, 1024], BF16, offset=_ovk).ap()
    MSK2 = nc.alloc_sbuf_tensor_at("sMSK2", [128, 1024], FP8, offset=_ovk).ap()
    H16B = nc.alloc_sbuf_tensor_at("sH16B", [128, 256], FP16,
                                   offset=_ovk + 1024).ap()
    TRt = A("sTR", 1024, BF16)
    WC = A("sWC", 8, F32)
    UC = A("sUC", 8, F32)
    LNW = A("sLNW", 8, F32)
    LNWB = A("sLNWB", 16, F32)
    St = A("sS", 1024, BF16)
    B1C = A("sB1C", 32, F32)
    B2C = A("sB2C", 8, F32)
    B1Q = A("sB1Q", 32, FP8)
    B2Q = A("sB2Q", 8, FP8)
    _ov2, _ = nc.bump_sbuf(2048)
    B1R = nc.alloc_sbuf_tensor_at("sB1R", [32, 128], F32, offset=_ov2).ap()
    B2R = nc.alloc_sbuf_tensor_at("sB2R", [8, 128], F32, offset=_ov2 + 512).ap()
    TDR = nc.alloc_sbuf_tensor_at("sTDR", [1, 16], F32, offset=_ov2 + 1024).ap()
    ABRROW = nc.alloc_sbuf_tensor_at("sABRROW", [1, 512], F32, offset=_ov2).ap()
    IDN = A("sIDN", 128, BF16)
    IDNF = A("sIDNF", 128, F32)
    ONESC = A("sONESC", 1, BF16)
    ONESRF = nc.alloc_sbuf_tensor("sONESRF", [1, 128], F32).ap()
    _ovl, _ = nc.bump_sbuf(2048)
    DIF = nc.alloc_sbuf_tensor_at("sDIF", [128, 128], F32, offset=_ovl).ap()
    REV = nc.alloc_sbuf_tensor_at("sREV", [128, 128], F32, offset=_ovl + 512).ap()
    IOT = nc.alloc_sbuf_tensor_at("sIOT", [128, 128], F32, offset=_ovl + 1024).ap()
    XSQ = nc.alloc_sbuf_tensor_at("sXSQ", [128, 512], F32R, offset=_ovl).ap()
    C128 = A("sC128", 1, F32)
    TD8 = A("sTD8", 8, F32)
    STATS = A("sSTATS", 16, F32)
    STATI = A("sSTATI", 4, I32)

    PB = [nc.alloc_psum_tensor(f"pb{i}", [128, 512], F32).ap() for i in range(8)]
    PB7B = PB[7].bitcast(BF16)

    S = Sched()

    _spr_rr = [0]

    def dma(q, dst, src, waits=(), sem="SPR", slow=False):
        if sem == "SPR":
            sem = ["SPR", "SPR1", "SPR2", "SPR3"][_spr_rr[0] % 4]
            _spr_rr[0] += 1
        prev = S.cnt.get(sem, 0)
        def fn(eng, dst=dst, src=src, slow=slow):
            return eng.dma_start(dst, src, allow_slow_non_contiguous=slow)
        return S.emit(q, fn, tuple(waits) + (((sem, prev),) if prev else ()),
                      inc=sem, amt=16)

    def mm(out, lhsT, rhs, start, stop, waits=(), inc=None, skip=False):
        def fn(eng, out=out, lhsT=lhsT, rhs=rhs, start=start, stop=stop, skip=skip):
            return nc.tensor.matmul(out, lhsT, rhs, start=start, stop=stop,
                                    skip_group_check=skip)
        return S.emit("tensor", fn, waits, inc=inc)

    def tr(out, in_, idn, waits=(), inc=None):
        def fn(eng, out=out, in_=in_, idn=idn):
            return nc.tensor.transpose(out, in_, idn)
        return S.emit("tensor", fn, waits, inc=inc)

    def act(out, in_, func, waits=(), inc="sACT", scale=None, bias=None):
        def fn(eng, out=out, in_=in_, func=func, scale=scale, bias=bias):
            kw = {}
            if scale is not None:
                kw["scale"] = scale
            if bias is not None:
                kw["bias"] = bias
            return nc.scalar.activation(out, in_, func, **kw)
        return S.emit("scalar", fn, waits, inc=inc)

    def vcopy(out, in_, waits=(), inc="sDVE", eng="vector"):
        def fn(e, out=out, in_=in_, eng=eng):
            return getattr(nc, eng).tensor_copy(out, in_)
        return S.emit(eng, fn, waits, inc=inc)

    def vtt(out, a, b, op, waits=(), inc="sDVE"):
        def fn(e, out=out, a=a, b=b, op=op):
            return nc.vector.tensor_tensor(out, a, b, op)
        return S.emit("vector", fn, waits, inc=inc)

    def vts(out, in_, s1, s2, op0, op1=None, waits=(), inc="sDVE"):
        def fn(e, out=out, in_=in_, s1=s1, s2=s2, op0=op0, op1=op1):
            if op1 is None:
                return nc.vector.tensor_scalar(out, in_, s1, None, op0)
            return nc.vector.tensor_scalar(out, in_, s1, s2, op0, op1)
        return S.emit("vector", fn, waits, inc=inc)

    def vstt(out, in0, scal, in1, op0, op1, waits=(), inc="sDVE"):
        def fn(e, out=out, in0=in0, scal=scal, in1=in1, op0=op0, op1=op1):
            return nc.vector.scalar_tensor_tensor(out, in0, scal, in1, op0, op1)
        return S.emit("vector", fn, waits, inc=inc)

    tc0 = dma("sync", IDN[:], didn[:])
    dma("sync", IDNF[:], didnf[:])
    dma("sync", ONESC[:], dconc[:])
    dma("sync", ONESRF[:], dconrf[:])
    dma("sync", DIF[:], ddif[:])
    dma("sync", REV[:], drev[:])
    dma("sync", IOT[:], diot[:])
    dma("sync", C128[:], dc128[:])
    dma("sync", TD8[:], dtd.rearrange("(c h2) d -> (h2 d) c", c=8, h2=2), slow=True)
    dma("sync", UC[:], dtf.rearrange("(c h2) d -> (h2 d) c", c=8, h2=2), slow=True)
    dma("sync", TDR[:], dtd[:, 0:1].rearrange("h one -> one h"), slow=True)
    dma("sync", B1R[:], db1.rearrange("(c p) -> c p", p=128))
    tcon = dma("sync", B2R[:], db2.rearrange("(c p) -> c p", p=128))

    HQF = HQ.bitcast(F32)
    stg = [XT, HQF]
    stage4 = [XT[:, 0:1024], XT[:, 1024:2048], HQF[:, 0:1024], HQF[:, 1024:2048]]
    q_free = {}
    wmix_last = None
    for wi, (dsrc, wdst) in enumerate([(dwk, WKs), (dwv, WVs), (dwr, WRs), (dwo, WOs)]):
        for kc in range(PC):
            sl = (wi * PC + kc) % 4
            t_ld = dma("sync", stage4[sl], dsrc[128 * kc:128 * (kc + 1), :],
                       waits=(q_free.get(sl),))
            t_c = vcopy(wdst[:, kc * 1024:(kc + 1) * 1024], stage4[sl], waits=(t_ld,))
            q_free[sl] = t_c
            wmix_last = t_c
    wk_tok = wmix_last

    stage2 = [XT[:, 0:2048], HQF[:, 0:2048]]
    half_free = {0: (q_free.get(0), q_free.get(1)), 1: (q_free.get(2), q_free.get(3))}
    fi = 0
    w1_done, w2_done = [], []
    for kc in range(PC):
        for hh in range(2):
            sl = fi % 2
            t_ld = dma("sync", stage2[sl],
                       dw1[128 * kc:128 * (kc + 1), 2048 * hh:2048 * (hh + 1)],
                       waits=(half_free[sl],))
            dst = W1S[:, kc * 4096 + 2048 * hh: kc * 4096 + 2048 * (hh + 1)]
            eng = ["scalar", "vector", "gpsimd"][fi % 3]
            if eng == "scalar":
                t_c = act(dst, stage2[sl], AF.Identity, waits=(t_ld,))
            else:
                t_c = vcopy(dst, stage2[sl], waits=(t_ld,),
                            inc="sDVE" if eng == "vector" else "sPOOL", eng=eng)
            half_free[sl] = t_c
            w1_done.append(t_c)
            fi += 1
    for j in range(PC):
        srcj = dw2[512 * j:512 * (j + 1), :].rearrange("(j2 p) o -> p j2 o", p=128)
        for hh in range(2):
            sl = fi % 2
            t_ld = dma("sync",
                       stage2[sl].rearrange("p (j2 o) -> p j2 o", o=1024),
                       srcj[:, 2 * hh:2 * hh + 2, :],
                       waits=(half_free[sl],))
            dst = W2S[:, j * 4096 + 2048 * hh: j * 4096 + 2048 * (hh + 1)]
            eng = ["scalar", "vector", "gpsimd"][fi % 3]
            if eng == "scalar":
                t_c = act(dst, stage2[sl], AF.Identity, waits=(t_ld,))
            else:
                t_c = vcopy(dst, stage2[sl], waits=(t_ld,),
                            inc="sDVE" if eng == "vector" else "sPOOL", eng=eng)
            half_free[sl] = t_c
            w2_done.append(t_c)
            fi += 1
    stage_done = dict(half_free)

    t_lnw = act(LNW[:], TD8[:], AF.Ln, waits=(tcon,))
    t_lnr = act(STATS[0:1, 0:16], TDR[:], AF.Ln, waits=(tcon,))
    t_lb = mm(PB[6][:, 0:16], ONESRF[0:1, :], STATS[0:1, 0:16], True, True,
              waits=(t_lnr,), inc="sPE")
    t_lnwb = vcopy(LNWB[:], PB[6][:, 0:16], waits=(t_lb,))
    S.emit("vector", lambda e: nc.vector.memset(STATS[:, 0:2], 1.0), (t_lb,), inc="sDVE")
    vcopy(ONES2R[:], STATS[:, 0:2])
    tb_toks = []
    for h in range(H):
        tb_toks.append(act(TB[:, h * 128:(h + 1) * 128], DIF[:], AF.Exp,
                           scale=LNWB[:, h:h + 1], waits=(t_lnwb,)))
    tk_toks, trt_toks, wc_toks = [], [], []
    for p in range(PC):
        tk_toks.append(act(TK[:, p * 128:(p + 1) * 128], REV[:], AF.Exp,
                           scale=LNW[:, p:p + 1], waits=(t_lnw,)))
        trt_toks.append(act(TRt[:, p * 128:(p + 1) * 128], IOT[:], AF.Exp,
                            scale=LNW[:, p:p + 1], waits=(t_lnw,)))
        wc_toks.append(act(WC[:, p:p + 1], C128[:], AF.Exp,
                           scale=LNW[:, p:p + 1], waits=(t_lnw,)))

    t_tkt = None
    for p in range(PC):
        t_tkt = tr(PB7B[:, p * 128:(p + 1) * 128], TK[:, p * 128:(p + 1) * 128],
                   IDN[:], waits=(tk_toks[p], tc0),
                   inc="sPE" if p == PC - 1 else None)
    t_tkt = vcopy(TKT[:], PB7B[:, 0:1024], waits=(t_tkt,))

    t_b1t = tr(PB[6][0:128, 16:48], B1R[:], IDNF[0:32, 0:32], waits=(tcon, t_lnwb), inc="sPE")
    t_b1e = vcopy(B1C[:], PB[6][:, 16:48], waits=(t_b1t,))
    t_e = vcopy(B1Q[:], B1C[:], waits=(t_b1e,))
    t_b1c = vcopy(B1C[:], B1Q[:], waits=(t_e,))
    t_b2t = tr(PB[6][0:128, 48:56], B2R[:], IDNF[0:8, 0:8], waits=(t_b1e,), inc="sPE")
    t_e = vcopy(B2C[:], PB[6][:, 48:56], waits=(t_b2t,))
    t_e = vcopy(B2Q[:], B2C[:], waits=(t_e,))
    t_b2c = vcopy(B2C[:], B2Q[:], waits=(t_e,))

    regs = [(0, 0), (1, 0), (2, 0), (3, 0), (4, 0), (5, 0)]
    t_pre = dma("sync", OUT[:, 0:1024].rearrange("p (pc t) -> p pc t", t=CH),
                dxpre.rearrange("(pc p) t -> p pc t", p=128))
    t_pxb = act(XBF[:, 0:1024], OUT[:, 0:1024], AF.Identity, waits=(t_pre,))
    reg_free = {}
    pre_kp = []
    for mc in range(PC):
        b, off = regs[mc % 6]
        tok = None
        for kc in range(PC):
            tok = mm(PB[b][:, off:off + 128],
                     WKs[:, kc * 1024 + mc * 128: kc * 1024 + mc * 128 + 128],
                     XBF[:, kc * 128:(kc + 1) * 128],
                     start=(kc == 0), stop=(kc == PC - 1),
                     waits=(t_pxb, wk_tok, reg_free.get(mc % 6)),
                     inc="sPE" if kc == PC - 1 else None)
        t_ev = act(KP[:, mc * 128:(mc + 1) * 128], PB[b][:, off:off + 128],
                   AF.Identity, scale=UC[:, mc:mc + 1], waits=(tok,))
        reg_free[mc % 6] = t_ev
        pre_kp.append(t_ev)
    pre_v = []
    for eh in range(2):
        tok = None
        for kc in range(PC):
            tok = mm(PB[eh][:, 0:512],
                     XBF[:, kc * 128:(kc + 1) * 128],
                     WVs[:, kc * 1024 + eh * 512: kc * 1024 + (eh + 1) * 512],
                     start=(kc == 0), stop=(kc == PC - 1),
                     waits=(reg_free.get(eh),),
                     inc="sPE" if kc == PC - 1 else None)
        pre_v.append(vcopy(VT[:, eh * 512:(eh + 1) * 512], PB[eh][:, 0:512],
                           waits=(tok,)))
    reg_free[0] = pre_v[0]
    reg_free[1] = pre_v[1]
    t_tp = None
    for p in range(PC):
        t_tp = tr(PB7B[:, p * 128:(p + 1) * 128], KP[:, p * 128:(p + 1) * 128],
                  IDN[:], waits=(pre_kp[p], t_tkt if p == 0 else None),
                  inc="sPE" if p == PC - 1 else None)
    t_kht = vtt(KHT[:], PB7B[:, 0:1024], TKT[:], ALU.mult, waits=(t_tp,))
    pre_m = []
    m_last = None
    for p in range(PC):
        mm(PB[4][0:64, p * 64:(p + 1) * 64], KHT[:, p * 128:p * 128 + 64],
           VT[:, 128 * p: 128 * p + 64], True, True,
           waits=(t_kht, pre_v[0], pre_v[1]) + ((t_b2c,) if p == 0 else ()))
        m_last = mm(PB[4][64:128, p * 64:(p + 1) * 64],
                    KHT[:, p * 128 + 64:p * 128 + 128],
                    VT[:, 128 * p + 64: 128 * p + 128], True, True,
                    inc="sPE" if p == PC - 1 else None)
    for p in range(PC):
        pre_m.append(vcopy(St[:, p * 64:(p + 1) * 64],
                           PB[4][:, p * 64:(p + 1) * 64], waits=(m_last,)))

    tok_xfree1 = (stage_done.get(0), stage_done.get(1))
    tok_xfree2 = (stage_done.get(0), stage_done.get(1))
    tok_vt_free = m_last
    tok_kp_done = t_tp
    tok_rs_done = None
    tok_xbf_done = pre_v[1]
    tok_kht_free = None
    tok_rt_free = None
    tok_x2q_done = None
    tok_s_upd = pre_m[-1]
    tok_pb7_free = t_kht
    tok_pb6_free = t_b2c
    tok_abr_free = None
    tok_tmp1_free = None
    tok_stat_prev = None
    tok_kht_ln = None
    at_free = None
    tok_hq_free = None
    prev_t_x2q = None
    prev_store = t_pxb
    g_free = {}
    y_free = {4: pre_m[-1]}
    last_v_ev = pre_v[1]
    gk = 0

    def emit_front_K(i):
        nonlocal tok_kp_done, tok_xbf_done, tok_xfree1, tok_xfree2
        t0 = i * T
        srcA = dxT[0:512, t0:t0 + T].rearrange("(pc p) t -> p pc t", p=128)
        srcB = dxT[512:1024, t0:t0 + T].rearrange("(pc p) t -> p pc t", p=128)
        t_x1 = dma("sync", XT[:, 0:1024].rearrange("p (pc t) -> p pc t", t=T),
                   srcA, waits=(tok_xfree1,) if not isinstance(tok_xfree1, tuple)
                   else tok_xfree1, sem="SX")
        t_xb1 = act(XBF[:, 0:1024], XT[:, 0:1024], AF.Identity,
                    waits=(t_x1, tok_xbf_done))
        t_x2 = dma("sync", XT[:, 1024:2048].rearrange("p (pc t) -> p pc t", t=T),
                   srcB, waits=(tok_xfree2,) if not isinstance(tok_xfree2, tuple)
                   else tok_xfree2, sem="SX")
        t_xb2 = act(XBF[:, 1024:2048], XT[:, 1024:2048], AF.Identity,
                    waits=(t_x2,))
        t_xbf = (t_xb1, t_xb2)
        kp_ev = []
        for mc in range(PC):
            b, off = regs[mc % 6]
            tok = None
            for kc in range(PC):
                tok = mm(PB[b][:, off:off + 256],
                         WKs[:, kc * 1024 + mc * 128: kc * 1024 + mc * 128 + 128],
                         XBF[:, kc * 256:(kc + 1) * 256],
                         start=(kc == 0), stop=(kc == PC - 1),
                         waits=(t_xb1 if kc < 4 else t_xb2,
                                reg_free.get(mc % 6),
                                (g_free.get(b), y_free.get(b)) if kc == 0 else None),
                         inc="sPE" if kc == PC - 1 else None)
            t_ev = act(KP[:, mc * 256:(mc + 1) * 256], PB[b][:, off:off + 256],
                       AF.Identity, scale=UC[:, mc:mc + 1],
                       waits=(tok, tok_kp_done if mc == 0 else None))
            reg_free[mc % 6] = t_ev
            kp_ev.append(t_ev)
        return t_xbf, kp_ev

    def emit_front_R(i, t_xbf):
        nonlocal tok_rs_done
        rs_ev = []
        for mc in range(PC):
            b, off = regs[mc % 6]
            tok = None
            for kc in range(PC):
                tok = mm(PB[b][:, off:off + 256],
                         WRs[:, kc * 1024 + mc * 128: kc * 1024 + mc * 128 + 128],
                         XBF[:, kc * 256:(kc + 1) * 256],
                         start=(kc == 0), stop=(kc == PC - 1),
                         waits=(t_xbf[0 if kc < 4 else 1],
                                reg_free.get(mc % 6)),
                         inc="sPE" if kc == PC - 1 else None)
            t_ev = act(RS[:, mc * 256:(mc + 1) * 256], PB[b][:, off:off + 256],
                       AF.Sigmoid, waits=(tok, tok_rs_done if mc == 0 else None))
            reg_free[mc % 6] = t_ev
            rs_ev.append(t_ev)
        return rs_ev

    def emit_front(i):
        t_xbf, kp_ev = emit_front_K(i)
        rs_ev = emit_front_R(i, t_xbf)
        return t_xbf, kp_ev, rs_ev

    def emit_mask(i, w1, w2):
        msrc = dmsk[:, i * T:(i + 1) * T].rearrange("(pc p) t -> p pc t", p=128)
        ta = dma("sync", MSK[:].rearrange("p (pc t) -> p pc t", t=T),
                 msrc[:, 0:4, :], waits=(w1,), sem="SM")
        tb = dma("sync", MSK2[:].rearrange("p (pc t) -> p pc t", t=T),
                 msrc[:, 4:8, :], waits=(w2,), sem="SM")
        return ta, tb

    def emit_mixer_V(c):
        nonlocal tok_vt_free, last_v_ev
        v_ev_c = None
        for eh in range(2):
            tok = None
            for kc in range(PC):
                tok = mm(PB[eh][:, 0:512],
                         XBF[:, kc * 256 + c * 128: kc * 256 + c * 128 + 128],
                         WVs[:, kc * 1024 + eh * 512: kc * 1024 + (eh + 1) * 512],
                         start=(kc == 0), stop=(kc == PC - 1),
                         waits=(reg_free.get(eh),
                                (g_free.get(eh), y_free.get(eh)) if kc == 0 else None),
                         inc="sPE" if kc == PC - 1 else None)
            t_ev = act(VT[:, eh * 512:(eh + 1) * 512], PB[eh][:, 0:512],
                       AF.Identity,
                       waits=(tok,) + ((tok_vt_free,) if eh == 0 else ()))
            reg_free[eh] = t_ev
            if eh == 1:
                v_ev_c = t_ev
        last_v_ev = v_ev_c
        return v_ev_c

    def emit_mixer_T(c, kp_ev):
        nonlocal tok_pb7_free, tok_kht_free, tok_kp_done
        t_tp = None
        for p in range(PC):
            t_tp = tr(PB7B[:, p * 128:(p + 1) * 128],
                      KP[:, p * 256 + c * 128: p * 256 + c * 128 + 128], IDN[:],
                      waits=(kp_ev[p], tok_pb7_free if p == 0 else None),
                      inc="sPE" if p == PC - 1 else None)
        t_kht = vtt(KHT[:], PB7B[:, 0:1024], TKT[:], ALU.mult,
                    waits=(t_tp, tok_kht_free, tok_kht_ln))
        tok_pb7_free = t_kht
        if c == 1:
            tok_kp_done = t_kht
        return t_kht

    def emit_mixer_A(c, kp_ev):
        v_ev_c = emit_mixer_V(c)
        t_kht = emit_mixer_T(c, kp_ev)
        return v_ev_c, t_kht

    def emit_rt(c, rs_ev):
        nonlocal tok_rt_free, tok_rs_done
        rt_toks = []
        for p in range(PC):
            rt_toks.append(vtt(RT[:, p * 128:(p + 1) * 128],
                               RS[:, p * 256 + c * 128: p * 256 + c * 128 + 128],
                               TRt[:, p * 128:(p + 1) * 128], ALU.mult,
                               waits=(rs_ev[p], trt_toks[p],
                                      tok_rt_free if p == 0 else None)))
        if c == 1:
            tok_rs_done = rt_toks[-1]
        return rt_toks

    def emit_mixer_B(c, kp_ev, rs_ev, v_ev_c, t_kht, rt_toks=None):
        nonlocal tok_s_upd, tok_kht_free, tok_rt_free, tok_vt_free, at_free, gk
        scur = gk % 2
        snxt = 1 - scur
        if rt_toks is None:
            rt_toks = emit_rt(c, rs_ev)
        g_toks = {}
        bank_last = {}
        for wv in range(2):
            for pidx in range(4):
                p = wv * 4 + pidx
                for hh in range(2):
                    bank = (2 + hh) if wv == 0 else hh
                    off = pidx * 128
                    w = [kp_ev[p], rs_ev[p]]
                    if pidx == 0:
                        w += [g_free.get(bank), reg_free.get(bank),
                              y_free.get(bank)]
                    g_toks[(p, hh)] = mm(
                        PB[bank][:, off: off + 128],
                        KP[64 * hh:64 * hh + 64,
                           p * 256 + c * 128: p * 256 + c * 128 + 128],
                        RS[64 * hh:64 * hh + 64,
                           p * 256 + c * 128: p * 256 + c * 128 + 128],
                        True, True, waits=tuple(w), inc="sPE")
                    bank_last[bank] = g_toks[(p, hh)]
        at_toks = {}
        for wv in range(2):
            for pidx in range(4):
                p = wv * 4 + pidx
                for hh in range(2):
                    bank = (2 + hh) if wv == 0 else hh
                    off = pidx * 128
                    h = 2 * p + hh
                    sl = wv * 8 + 2 * pidx + hh
                    at = vtt(AT[:, sl * 128:(sl + 1) * 128],
                             PB[bank][:, off: off + 128],
                             TB[:, h * 128:(h + 1) * 128], ALU.mult,
                             waits=(bank_last[bank], tb_toks[h],
                                    at_free if sl == 0 else None))
                    at_toks[h] = at
                    g_free[bank] = at
        y_ev = []
        yev_w1 = None
        for wv in range(2):
            ybank = 4 + wv
            ybank_last = None
            for pidx in range(4):
                p = wv * 4 + pidx
                off = pidx * 128
                for hh in range(2):
                    h = 2 * p + hh
                    sl = wv * 8 + 2 * pidx + hh
                    mm(PB[ybank][64 * hh:64 * hh + 64, off:off + 128],
                       St[64 * hh:64 * hh + 64,
                          scur * 512 + p * 64: scur * 512 + (p + 1) * 64],
                       RT[64 * hh:64 * hh + 64, p * 128:(p + 1) * 128],
                       start=(pidx == 0), stop=False,
                       waits=(tok_s_upd, rt_toks[p],
                              (y_free.get(ybank), g_free.get(ybank),
                               reg_free.get(ybank))
                              if pidx == 0 and hh == 0 else None),
                       skip=True)
                    ybank_last = mm(
                        PB[ybank][64 * hh:64 * hh + 64, off:off + 128],
                        VT[:, p * 128 + 64 * hh: p * 128 + 64 * hh + 64],
                        AT[:, sl * 128:(sl + 1) * 128],
                        start=False, stop=True,
                        waits=(v_ev_c, at_toks[h]), inc="sPE", skip=True)
            for pidx in range(4):
                p = wv * 4 + pidx
                off = pidx * 128
                ev = act(YF[:, p * 256 + c * 128: p * 256 + c * 128 + 128],
                         PB[ybank][:, off:off + 128], AF.Identity,
                         waits=(ybank_last,))
                y_free[ybank] = ev
                y_ev.append(ev)
            at_free = ybank_last
            if wv == 0:
                yev_w1 = y_ev[3]
        m_last = None
        for p in range(PC):
            mm(PB[4][0:64, p * 64:(p + 1) * 64], KHT[:, p * 128:p * 128 + 64],
               VT[:, 128 * p: 128 * p + 64], True, True,
               waits=(t_kht, v_ev_c) + ((yev_w1,) if p == 0 else ()))
            m_last = mm(PB[4][64:128, p * 64:(p + 1) * 64],
                        KHT[:, p * 128 + 64:p * 128 + 128],
                        VT[:, 128 * p + 64: 128 * p + 128],
                        True, True, inc="sPE" if p == PC - 1 else None)
        s_upd = []
        for p in range(PC):
            s_upd.append(vstt(St[:, snxt * 512 + p * 64: snxt * 512 + (p + 1) * 64],
                              St[:, scur * 512 + p * 64: scur * 512 + (p + 1) * 64],
                              WC[:, p:p + 1], PB[4][:, p * 64:(p + 1) * 64],
                              ALU.mult, ALU.add, waits=(m_last, wc_toks[p])))
        tok_s_upd = s_upd[-1]
        tok_kht_free = s_upd[-1]
        tok_rt_free = y_ev[-1]
        y_free[4] = s_upd[-1]
        tok_vt_free = m_last
        gk += 1
        return y_ev

    def emit_mixer_chunk(c, kp_ev, rs_ev):
        v_ev_c, t_kht = emit_mixer_A(c, kp_ev)
        return emit_mixer_B(c, kp_ev, rs_ev, v_ev_c, t_kht)

    def emit_attn_mms(y_ev_all, lo, hi):
        toks = []
        for mc in range(lo, hi):
            b, off = regs[mc % 6]
            tok = None
            for kc in range(PC):
                tok = mm(PB[b][:, off:off + 256],
                         WOs[:, kc * 1024 + mc * 128: kc * 1024 + mc * 128 + 128],
                         YF[:, kc * 256:(kc + 1) * 256],
                         start=(kc == 0), stop=(kc == PC - 1),
                         waits=(y_ev_all[kc], y_ev_all[8 + kc],
                                reg_free.get(mc % 6),
                                (g_free.get(b), y_free.get(b)) if kc == 0 else None),
                         inc="sPE" if kc == PC - 1 else None)
            toks.append((mc, tok))
        return toks

    def emit_attn_evacs(i, mm_toks):
        nonlocal tok_xfree1, tok_xfree2
        t_xbf_i = fronts[i][0]
        xr_toks = []
        for mc, tok in mm_toks:
            b, off = regs[mc % 6]
            t_ev = vtt(XR[:, mc * 256:(mc + 1) * 256],
                       XT[:, mc * 256:(mc + 1) * 256],
                       PB[b][:, off:off + 256], ALU.add, waits=(tok,))
            reg_free[mc % 6] = t_ev
            xr_toks.append(t_ev)
            if mc == 3:
                tok_xfree1 = (t_ev, t_xbf_i[0])
            elif mc == PC - 1:
                tok_xfree2 = (t_ev, t_xbf_i[1])
        return xr_toks

    def emit_attn(i, y_ev_all):
        xr_toks = []
        for mc in range(PC):
            xr_toks += emit_attn_evacs(i, emit_attn_mms(y_ev_all, mc, mc + 1))
        return xr_toks

    def ln_stats_make():
        nonlocal tok_pb6_free, tok_stat_prev, tok_kht_ln
        st = {"sq_free": [tok_stat_prev, None], "t": None}
        pb6w = (tok_pb6_free, tok_stat_prev)
        def piece(p, xr_tok):
            first = (p == 0)
            for th in range(2):
                mm(PB[6][:, 2 * th:2 * th + 2],
                   XR[:, p * 256 + th * 128: p * 256 + th * 128 + 128],
                   ONES2R[:, 0:2], start=first and th == 0, stop=(p == PC - 1),
                   waits=(xr_tok,) + (pb6w if first and th == 0 else ()),
                   skip=True)
            sq = XSQ[:, (p % 2) * 256:(p % 2) * 256 + 256]
            t_sq = vtt(sq, XR[:, p * 256:(p + 1) * 256],
                       XR[:, p * 256:(p + 1) * 256], ALU.mult,
                       waits=(xr_tok, st["sq_free"][p % 2],
                              wc_toks[-1] if p < 2 else None))
            t_stat = None
            for th in range(2):
                t_stat = mm(PB[6][:, 4 + 2 * th:6 + 2 * th],
                            sq[:, th * 128:(th + 1) * 128],
                            ONES2R[:, 0:2],
                            start=(p == 0 and th == 0), stop=(p == PC - 1),
                            waits=(t_sq,), inc="sPE", skip=True)
            st["sq_free"][p % 2] = t_stat
            st["t"] = t_stat
            return t_stat
        def fin():
            nonlocal tok_stat_prev, tok_kht_ln
            tok_stat_prev = st["t"]
            tok_kht_ln = st["t"]
            return st["t"]
        return piece, fin

    def ln_stats(xr_toks):
        nonlocal tok_pb6_free, tok_stat_prev, tok_kht_ln
        first = True
        for p in range(PC):
            for th in range(2):
                mm(PB[6][:, 2 * th:2 * th + 2],
                   XR[:, p * 256 + th * 128: p * 256 + th * 128 + 128],
                   ONES2R[:, 0:2], start=first, stop=(p == PC - 1),
                   waits=(xr_toks[p],) + ((tok_pb6_free, tok_stat_prev) if first else ()),
                   skip=True)
                first = False
        t_stat = None
        sq_free = [tok_stat_prev, None]
        for p in range(PC):
            sq = XSQ[:, (p % 2) * 256:(p % 2) * 256 + 256]
            t_sq = vtt(sq, XR[:, p * 256:(p + 1) * 256],
                       XR[:, p * 256:(p + 1) * 256], ALU.mult,
                       waits=(xr_toks[p], sq_free[p % 2],
                              wc_toks[-1] if p < 2 else None))
            for th in range(2):
                t_stat = mm(PB[6][:, 4 + 2 * th:6 + 2 * th],
                            sq[:, th * 128:(th + 1) * 128],
                            ONES2R[:, 0:2],
                            start=(p == 0 and th == 0), stop=(p == PC - 1),
                            waits=(t_sq,), inc="sPE", skip=True)
            sq_free[p % 2] = t_stat
        tok_stat_prev = t_stat
        tok_kht_ln = t_stat
        return t_stat

    def ln_chain(t_stat):
        vts(STATS[:, 0:4], PB[6][:, 0:4], 1.0 / D, None, ALU.mult, waits=(t_stat,))
        vts(STATS[:, 4:8], PB[6][:, 4:8], 1.0 / D, None, ALU.mult)
        vtt(STATS[:, 8:12], STATS[:, 0:4], STATS[:, 0:4], ALU.mult)
        vtt(STATS[:, 8:12], STATS[:, 4:8], STATS[:, 8:12], ALU.subtract)
        vts(STATS[:, 8:12], STATS[:, 8:12], EPS, None, ALU.add)
        vcopy(STATS[:, 12:16], STATS[:, 8:12].bitcast(I32))
        vts(STATS[:, 12:16], STATS[:, 12:16], -0.5, MAGIC, ALU.mult, ALU.add)
        vcopy(STATI[:, 0:4], STATS[:, 12:16])
        Y0 = STATI.bitcast(F32)
        for _ in range(2):
            vtt(STATS[:, 12:16], Y0[:, 0:4], Y0[:, 0:4], ALU.mult)
            vtt(STATS[:, 12:16], STATS[:, 12:16], STATS[:, 8:12], ALU.mult)
            vts(STATS[:, 12:16], STATS[:, 12:16], -0.5, 1.5, ALU.mult, ALU.add)
            vtt(Y0[:, 0:4], Y0[:, 0:4], STATS[:, 12:16], ALU.mult)
        vtt(STATS[:, 4:8], STATS[:, 0:4], Y0[:, 0:4], ALU.mult)
        t_pk = vts(STATS[:, 0:4], STATS[:, 4:8], -1.0, None, ALU.mult)
        return t_pk

    def ln_tr(t_pk):
        t_tr = None
        Y0F = STATI.bitcast(F32)
        srcs = [Y0F[:, 0:1], Y0F[:, 2:3], STATS[:, 0:1], STATS[:, 2:3]]
        for r in range(4):
            t_tr = tr(PB[6][0:1, r * 128:(r + 1) * 128],
                      srcs[r], IDNF[:],
                      waits=(t_pk,), inc="sPE" if r == 3 else None)
        return t_tr

    def ln_bc(t_tr):
        nonlocal tok_abr_free
        t_abr = vcopy(ABRROW[0:1, :], PB[6][0:1, 0:512],
                      waits=(t_tr, tok_abr_free))
        mm(PB[6][:, 256:512], ONESRF[0:1, :], ABRROW[0:1, 0:256], True, True,
           waits=(t_abr,), skip=True)
        t_bc = mm(PB[6][:, 0:256], ONESRF[0:1, :], ABRROW[0:1, 256:512],
                  True, True, inc="sPE", skip=True)
        tok_abr_free = t_bc
        return t_bc

    def ln_bcast(t_pk):
        return ln_bc(ln_tr(t_pk))

    def ln_apply(t_bc, out_writer):
        nonlocal tok_pb6_free, tok_tmp1_free
        out_toks = []
        for p in range(PC):
            vtt(TMP1[:, 0:256], XR[:, p * 256:(p + 1) * 256],
                PB[6][:, 256:512], ALU.mult,
                waits=(t_bc, tok_tmp1_free if p == 0 else None))
            tok_o = out_writer(p)
            tok_tmp1_free = tok_o
            out_toks.append(tok_o)
        tok_pb6_free = out_toks[-1]
        return out_toks

    def ln_bcast_apply(t_pk, out_writer):
        return ln_apply(ln_bcast(t_pk), out_writer)

    fronts = {0: emit_front(0)}
    t_m, t_m2 = emit_mask(0, None, t_tkt)
    y_all = emit_mixer_chunk(0, fronts[0][1], fronts[0][2])
    y_all += emit_mixer_chunk(1, fronts[0][1], fronts[0][2])
    xr_toks = emit_attn(0, y_all)

    for i in range(nt):
        t0 = i * T
        nxt = i + 1 < nt

        t_stat1 = ln_stats(xr_toks)
        if nxt:
            t_xbf_n, kp_n = emit_front_K(i + 1)
        t_pk1 = ln_chain(t_stat1)
        t_tr1 = ln_tr(t_pk1)
        if nxt:
            v0 = emit_mixer_V(0)
        t_bc1 = ln_bc(t_tr1)
        if nxt:
            kht0 = emit_mixer_T(0, kp_n)

        def ln1_writer(p):
            return vtt(X2B[:, p * 256:(p + 1) * 256], TMP1[:, 0:256],
                       PB[6][:, 0:256], ALU.add,
                       waits=((prev_t_x2q,) if p == 0 and prev_t_x2q else ()))
        x2_toks = ln_apply(t_bc1, ln1_writer)
        t_x2q = None
        for p in range(PC):
            t_x2q = act(X2Q[:, p * 256:(p + 1) * 256],
                        X2B[:, p * 256:(p + 1) * 256], AF.Identity,
                        waits=(x2_toks[p],) + ((tok_x2q_done,) if p == 0 else ()))
        prev_t_x2q = t_x2q
        if nxt:
            rs_n = emit_front_R(i + 1, t_xbf_n)
            y_next = emit_mixer_B(0, kp_n, rs_n, v0, kht0)

        hq_toks = []
        h16_free = {0: None, 1: None}
        for mc in range(FC):
            b, off = regs[mc % 6]
            tok = None
            for kc in range(PC):
                tok = mm(PB[b][:, off:off + 256],
                         W1S[:, kc * 4096 + mc * 128: kc * 4096 + mc * 128 + 128],
                         X2Q[:, kc * 256:(kc + 1) * 256],
                         start=(kc == 0), stop=(kc == PC - 1),
                         waits=(t_x2q, reg_free.get(mc % 6),
                                (g_free.get(b), y_free.get(b)) if kc == 0 else None)
                         + (tuple(w1_done) if i == 0 and mc == 0 else ()),
                         inc="sPE" if kc == PC - 1 else None)
            hb = (H16 if mc % 2 == 0 else H16B)[:, 0:256]
            t_h = act(hb, PB[b][:, off:off + 256], AF.Identity,
                      bias=B1C[:, mc:mc + 1],
                      waits=(tok, h16_free[mc % 2],
                             t_b1c if i == 0 and mc == 0 else None))
            t_q = vts(HQ[:, mc * 256:(mc + 1) * 256], hb, 0.0, None, ALU.max,
                      waits=(t_h, tok_hq_free if mc == 0 else None)
                      + (tuple(w2_done) if i == 0 and mc == 0 else ()))
            h16_free[mc % 2] = t_q
            reg_free[mc % 6] = t_q
            hq_toks.append(t_q)

        if nxt:
            rt1 = emit_rt(1, rs_n)

        r2_toks = []
        for mc in range(PC):
            b, off = regs[mc % 6]
            tok = None
            for kc in range(FC):
                tok = mm(PB[b][:, off:off + 256],
                         W2S[:, kc * 1024 + mc * 128: kc * 1024 + mc * 128 + 128],
                         HQ[:, kc * 256:(kc + 1) * 256],
                         start=(kc == 0), stop=(kc == FC - 1),
                         waits=(hq_toks[kc], reg_free.get(mc % 6),
                                (g_free.get(b), y_free.get(b)) if kc == 0 else None),
                         inc="sPE" if kc == FC - 1 else None)
            hb = (H16 if mc % 2 == 0 else H16B)[:, 0:256]
            t_h = act(hb, PB[b][:, off:off + 256], AF.Identity,
                      bias=B2C[:, mc:mc + 1], waits=(tok, h16_free[mc % 2]))
            mbuf = MSK if mc < 4 else MSK2
            t_mul = vtt(TMP1[:, 0:256], hb,
                        mbuf[:, (mc % 4) * 256:((mc % 4) + 1) * 256],
                        ALU.mult, waits=(t_h, t_m if mc < 4 else t_m2,
                                         tok_tmp1_free))
            t_r2 = vtt(XR[:, mc * 256:(mc + 1) * 256], TMP1[:, 0:256],
                       X2B[:, mc * 256:(mc + 1) * 256], ALU.add,
                       waits=(t_mul,))
            tok_tmp1_free = t_r2
            h16_free[mc % 2] = t_mul
            reg_free[mc % 6] = t_r2
            r2_toks.append(t_r2)
            if mc == 3 and nxt:
                msrc_n = dmsk[:, (i + 1) * T:(i + 2) * T].rearrange(
                    "(pc p) t -> p pc t", p=128)
                t_mA = dma("sync", MSK[:].rearrange("p (pc t) -> p pc t", t=T),
                           msrc_n[:, 0:4, :], waits=(t_mul,), sem="SM")
            if mc == PC - 1 and nxt:
                msrc_n = dmsk[:, (i + 1) * T:(i + 2) * T].rearrange(
                    "(pc p) t -> p pc t", p=128)
                t_mB = dma("sync", MSK2[:].rearrange("p (pc t) -> p pc t", t=T),
                           msrc_n[:, 4:8, :], waits=(t_mul,), sem="SM")
        tok_hq_free = r2_toks[-1]
        tok_x2q_done = r2_toks[-1]
        if nxt:
            t_m, t_m2 = t_mA, t_mB

        t_stat2 = ln_stats(r2_toks)
        t_pk2 = ln_chain(t_stat2)
        if nxt:
            mA1 = emit_mixer_A(1, kp_n)
            y_next = y_next + emit_mixer_B(1, kp_n, rs_n, *mA1, rt_toks=rt1)
        t_tr2 = ln_tr(t_pk2)
        if nxt:
            fronts[i + 1] = (t_xbf_n, kp_n, rs_n)
            attn_p1 = emit_attn_mms(y_next, 0, 2)
        t_bc2 = ln_bc(t_tr2)

        ln2_state = {}

        def ln2_writer(p):
            nonlocal prev_store
            if p == 4:
                dst1 = dout[0:512, t0:t0 + T].rearrange("(pc p) t -> p pc t", p=128)
                ln2_state["st1"] = dma("gpsimd", dst1,
                                       OUT[:].rearrange("p (pc t) -> p pc t", t=T),
                                       waits=(ln2_state[3],), sem="SO")
            w = ()
            if p == 0:
                w = (prev_store,)
            elif p == 4:
                w = (ln2_state["st1"],)
            tok = vtt(OUT[:, (p % 4) * 256:((p % 4) + 1) * 256], TMP1[:, 0:256],
                      PB[6][:, 0:256], ALU.add, waits=w)
            ln2_state[p] = tok
            return tok
        o_toks = ln_apply(t_bc2, ln2_writer)
        if nxt:
            xr_toks = emit_attn_evacs(i + 1, attn_p1)
            xr_toks = xr_toks + emit_attn_evacs(
                i + 1, emit_attn_mms(y_next, 2, PC))
        dst2 = dout[512:1024, t0:t0 + T].rearrange("(pc p) t -> p pc t", p=128)
        prev_store = dma("gpsimd", dst2,
                         OUT[:].rearrange("p (pc t) -> p pc t", t=T),
                         waits=(o_toks[-1],), sem="SO")

    return nc, S


def realize(nc, S):
    import contextlib
    semh = {}
    ctx = contextlib.ExitStack()
    for s in SEMS:
        semh[s] = ctx.enter_context(nc.semaphore(s))
    block = ctx.enter_context(nc.Block())
    so_total = S.cnt.get("SO", 0)

    def make(engname):
        def body(engobj):
            for fn, waits, inc, amt in S.q[engname]:
                for (s, v) in waits:
                    engobj.wait_ge(semh[s], v)
                inst = fn(engobj)
                if inc is not None and inst is not None:
                    inst.then_inc(semh[inc], amt)
            if engname == "gpsimd":
                engobj.wait_ge(semh["SO"], so_total)
        return body

    for e in ENGINES:
        getattr(block, e)(make(e))
    ctx.close()
    return nc


def host_constants(ntok):
    import ml_dtypes
    bfd = ml_dtypes.bfloat16
    f8 = ml_dtypes.float8_e4m3
    s = np.arange(128)
    t = np.arange(128)
    dif = (t[None, :] - s[:, None]).astype(np.float32)
    dif[t[None, :] < s[:, None]] = 1e5
    rev = np.broadcast_to((127.0 - t).astype(np.float32), (128, 128)).copy()
    iot = np.broadcast_to((t + 1.0).astype(np.float32), (128, 128)).copy()
    return {
        "idn": np.eye(128, dtype=bfd),
        "idnf": np.eye(128, dtype=np.float32),
        "onesc": np.ones((128, 1), dtype=bfd),
        "onesrf": np.ones((1, 128), dtype=np.float32),
        "dif": dif,
        "rev": rev,
        "iot": iot,
        "c128": np.full((128, 1), 128.0, dtype=np.float32),
    }


NCORE = 8
NTOK = 4096


def kernel(L0_attn_x, L0_attn_Wk, L0_attn_Wv, L0_attn_Wr, L0_attn_Wo,
           L0_attn_time_decay, L0_attn_time_first,
           L0_ffn_w1, L0_ffn_b1, L0_ffn_w2, L0_ffn_b2):
    import ml_dtypes
    import jax
    from concourse.bass_utils import run_bass_kernel_spmd
    f8d = ml_dtypes.float8_e4m3

    x = np.asarray(L0_attn_x, dtype=np.float32)
    B, L, Dm = x.shape
    assert (B, L, Dm) == (4, 8192, 1024)

    cpu = jax.devices("cpu")[0]
    with jax.default_device(cpu):
        keep = np.asarray(jax.random.bernoulli(
            jax.random.key(42, impl="threefry2x32"), 0.8, (B, L, Dm)))
    maskv = np.where(keep, np.float32(1.25), np.float32(0.0))

    wkT = np.ascontiguousarray(np.asarray(L0_attn_Wk, np.float32).T)
    wvT = np.ascontiguousarray(np.asarray(L0_attn_Wv, np.float32).T)
    wrT = np.ascontiguousarray(np.asarray(L0_attn_Wr, np.float32).T)
    woT = np.ascontiguousarray(np.asarray(L0_attn_Wo, np.float32).T)
    w1T = np.ascontiguousarray(np.asarray(L0_ffn_w1, np.float32).T)
    w2T = np.ascontiguousarray(np.asarray(L0_ffn_w2, np.float32).T)
    td = np.asarray(L0_attn_time_decay, np.float32)
    tf = np.asarray(L0_attn_time_first, np.float32)
    b1 = np.asarray(L0_ffn_b1, np.float32)
    b2 = np.asarray(L0_ffn_b2, np.float32)
    con = host_constants(NTOK)
    zpre = np.zeros((Dm, CH), dtype=np.float32)

    in_maps = []
    for core in range(NCORE):
        b, hf = core // 2, core % 2
        sl = x[b, hf * NTOK:(hf + 1) * NTOK, :]
        xT = np.ascontiguousarray(sl.T)
        xpre = np.ascontiguousarray(x[b, NTOK - CH:NTOK, :].T) if hf == 1 else zpre
        mT = np.ascontiguousarray(maskv[b, hf * NTOK:(hf + 1) * NTOK, :].T).astype(f8d)
        in_maps.append(dict(
            xT=xT, xpre=xpre, wkT=wkT, wvT=wvT, wrT=wrT, woT=woT,
            w1T=w1T, w2T=w2T, td=td, tf=tf, b1=b1, b2=b2, maskT=mT, **con))

    nc, S = build_nc(NTOK)
    realize(nc, S)
    res = run_bass_kernel_spmd(nc, in_maps, core_ids=list(range(NCORE)))
    out = np.empty((B, L, Dm), dtype=np.float32)
    for core in range(NCORE):
        b, hf = core // 2, core % 2
        out[b, hf * NTOK:(hf + 1) * NTOK, :] = res.results[core]["outT"].T
    return out


# version 19
# speedup vs baseline: 1.1477x; 1.0048x over previous
import numpy as np
import concourse.bass as bass
import concourse.mybir as mybir

F32 = mybir.dt.float32
BF16 = mybir.dt.bfloat16
FP16 = mybir.dt.float16
F32R = mybir.dt.float32r
FP8 = mybir.dt.float8e4
I32 = mybir.dt.int32
AF = mybir.ActivationFunctionType
ALU = mybir.AluOpType

D = 1024
PC = 8
H = 16
FF = 4096
FC = 32
T = 256
CH = 128
EPS = 1e-5
MAGIC = float(0x5f3759df)

ENGINES = ("sync", "tensor", "vector", "scalar", "gpsimd")
OWN = {"tensor": "sPE", "vector": "sDVE", "scalar": "sACT", "gpsimd": "sPOOL"}
SEMS = ["SX", "SM", "SO", "SPR", "SPR1", "SPR2", "SPR3",
        "sPE", "sDVE", "sACT", "sPOOL"]


class Sched:
    def __init__(self):
        self.q = {e: [] for e in ENGINES}
        self.cnt = {}
        self.hw = {e: {} for e in ENGINES}

    def emit(self, eng, fn, waits=(), inc=None, amt=1):
        real = []
        flat = []
        for w in waits:
            if w is None:
                continue
            if isinstance(w, (list, tuple)) and w and isinstance(w[0], (list, tuple)):
                flat.extend(w)
            elif isinstance(w, (list, tuple)) and len(w) == 2 and isinstance(w[0], str):
                flat.append(w)
            elif isinstance(w, (list, tuple)):
                flat.extend(x for x in w if x is not None)
            else:
                raise ValueError(f"bad wait {w}")
        own = OWN.get(eng)
        if own is not None and eng != "tensor":
            prev = self.cnt.get(own, 0)
            if prev > 0:
                flat.append((own, prev))
        for w in flat:
            if w is None:
                continue
            s, v = w
            if s == own and self.cnt.get(own, 0) >= v and \
                    self.hw[eng].get(s, 0) >= v:
                continue
            if self.hw[eng].get(s, 0) >= v:
                continue
            self.hw[eng][s] = v
            real.append((s, v))
        tok = None
        if inc is not None:
            self.cnt[inc] = self.cnt.get(inc, 0) + amt
            tok = (inc, self.cnt[inc])
        self.q[eng].append((fn, real, inc, amt))
        return tok


def build_nc(ntok):
    nt = ntok // T
    nc = bass.Bass("TRN2", target_bir_lowering=False)

    dxT = nc.dram_tensor("xT", [D, ntok], F32, kind="ExternalInput").ap()
    dxpre = nc.dram_tensor("xpre", [D, CH], F32, kind="ExternalInput").ap()
    dwk = nc.dram_tensor("wkT", [D, D], F32, kind="ExternalInput").ap()
    dwv = nc.dram_tensor("wvT", [D, D], F32, kind="ExternalInput").ap()
    dwr = nc.dram_tensor("wrT", [D, D], F32, kind="ExternalInput").ap()
    dwo = nc.dram_tensor("woT", [D, D], F32, kind="ExternalInput").ap()
    dw1 = nc.dram_tensor("w1T", [D, FF], F32, kind="ExternalInput").ap()
    dw2 = nc.dram_tensor("w2T", [FF, D], F32, kind="ExternalInput").ap()
    dtd = nc.dram_tensor("td", [H, 64], F32, kind="ExternalInput").ap()
    dtf = nc.dram_tensor("tf", [H, 64], F32, kind="ExternalInput").ap()
    db1 = nc.dram_tensor("b1", [FF], F32, kind="ExternalInput").ap()
    db2 = nc.dram_tensor("b2", [D], F32, kind="ExternalInput").ap()
    dmsk = nc.dram_tensor("maskT", [D, ntok], FP8, kind="ExternalInput").ap()
    didn = nc.dram_tensor("idn", [128, 128], BF16, kind="ExternalInput").ap()
    didnf = nc.dram_tensor("idnf", [128, 128], F32, kind="ExternalInput").ap()
    dconc = nc.dram_tensor("onesc", [128, 1], BF16, kind="ExternalInput").ap()
    dconrf = nc.dram_tensor("onesrf", [1, 128], F32, kind="ExternalInput").ap()
    ddif = nc.dram_tensor("dif", [128, 128], F32, kind="ExternalInput").ap()
    drev = nc.dram_tensor("rev", [128, 128], F32, kind="ExternalInput").ap()
    diot = nc.dram_tensor("iot", [128, 128], F32, kind="ExternalInput").ap()
    dc128 = nc.dram_tensor("c128", [128, 1], F32, kind="ExternalInput").ap()
    dout = nc.dram_tensor("outT", [D, ntok], F32, kind="ExternalOutput").ap()

    A = lambda name, w, dt: nc.alloc_sbuf_tensor(name, [128, w], dt).ap()
    XT = A("sXT", 2048, F32)
    XBF = A("sXBF", 2048, BF16)
    KP = A("sKP", 2048, BF16)
    RS = A("sRS", 2048, BF16)
    VT = A("sVT", 1024, BF16)
    KHT = A("sKHT", 1024, BF16)
    TKT = A("sTKT", 1024, BF16)
    RT = A("sRT", 1024, BF16)
    YF = A("sYF", 2048, BF16)
    XR = A("sXR", 2048, F32R)
    X2B = A("sX2B", 2048, BF16)
    X2Q = A("sX2Q", 2048, FP8)
    ONES2R = A("sONES2R", 2, F32R)
    HQ = A("sHQ", 8192, FP8)
    H16 = A("sH16", 256, FP16)
    TMP1 = A("sTMP1", 256, BF16)
    AT = A("sAT", 2048, BF16)
    MSK = A("sMSK", 1024, FP8)
    OUT = A("sOUT", 1024, F32)
    WKs = A("sWK", 8192, BF16)
    WVs = A("sWV", 8192, BF16)
    WRs = A("sWR", 8192, BF16)
    WOs = A("sWO", 8192, BF16)
    W1S = A("sW1S", 32768, FP8)
    W2S = A("sW2S", 32768, FP8)
    TB = A("sTB", 2048, BF16)
    _ovk, _ = nc.bump_sbuf(2048)
    TK = nc.alloc_sbuf_tensor_at("sTK", [128, 1024], BF16, offset=_ovk).ap()
    MSK2 = nc.alloc_sbuf_tensor_at("sMSK2", [128, 1024], FP8, offset=_ovk).ap()
    H16B = nc.alloc_sbuf_tensor_at("sH16B", [128, 256], FP16,
                                   offset=_ovk + 1024).ap()
    TRt = A("sTR", 1024, BF16)
    WC = A("sWC", 8, F32)
    UC = A("sUC", 8, F32)
    LNW = A("sLNW", 8, F32)
    LNWB = A("sLNWB", 16, F32)
    St = A("sS", 1024, BF16)
    B1C = A("sB1C", 32, F32)
    B2C = A("sB2C", 8, F32)
    B1Q = A("sB1Q", 32, FP8)
    B2Q = A("sB2Q", 8, FP8)
    _ov2, _ = nc.bump_sbuf(2048)
    B1R = nc.alloc_sbuf_tensor_at("sB1R", [32, 128], F32, offset=_ov2).ap()
    B2R = nc.alloc_sbuf_tensor_at("sB2R", [8, 128], F32, offset=_ov2 + 512).ap()
    TDR = nc.alloc_sbuf_tensor_at("sTDR", [1, 16], F32, offset=_ov2 + 1024).ap()
    ABRROW = nc.alloc_sbuf_tensor_at("sABRROW", [1, 512], F32, offset=_ov2).ap()
    IDN = A("sIDN", 128, BF16)
    IDNF = A("sIDNF", 128, F32)
    ONESC = A("sONESC", 1, BF16)
    ONESRF = nc.alloc_sbuf_tensor("sONESRF", [1, 128], F32).ap()
    _ovl, _ = nc.bump_sbuf(2048)
    DIF = nc.alloc_sbuf_tensor_at("sDIF", [128, 128], F32, offset=_ovl).ap()
    REV = nc.alloc_sbuf_tensor_at("sREV", [128, 128], F32, offset=_ovl + 512).ap()
    IOT = nc.alloc_sbuf_tensor_at("sIOT", [128, 128], F32, offset=_ovl + 1024).ap()
    XSQ = nc.alloc_sbuf_tensor_at("sXSQ", [128, 512], F32R, offset=_ovl).ap()
    C128 = A("sC128", 1, F32)
    TD8 = A("sTD8", 8, F32)
    STATS = A("sSTATS", 16, F32)
    STATI = A("sSTATI", 4, I32)

    PB = [nc.alloc_psum_tensor(f"pb{i}", [128, 512], F32).ap() for i in range(8)]
    PB7B = PB[7].bitcast(BF16)

    S = Sched()

    _spr_rr = [0]

    def dma(q, dst, src, waits=(), sem="SPR", slow=False):
        if sem == "SPR":
            sem = ["SPR", "SPR1", "SPR2", "SPR3"][_spr_rr[0] % 4]
            _spr_rr[0] += 1
        prev = S.cnt.get(sem, 0)
        def fn(eng, dst=dst, src=src, slow=slow):
            return eng.dma_start(dst, src, allow_slow_non_contiguous=slow)
        return S.emit(q, fn, tuple(waits) + (((sem, prev),) if prev else ()),
                      inc=sem, amt=16)

    def mm(out, lhsT, rhs, start, stop, waits=(), inc=None, skip=False):
        def fn(eng, out=out, lhsT=lhsT, rhs=rhs, start=start, stop=stop, skip=skip):
            return nc.tensor.matmul(out, lhsT, rhs, start=start, stop=stop,
                                    skip_group_check=skip)
        return S.emit("tensor", fn, waits, inc=inc)

    def tr(out, in_, idn, waits=(), inc=None):
        def fn(eng, out=out, in_=in_, idn=idn):
            return nc.tensor.transpose(out, in_, idn)
        return S.emit("tensor", fn, waits, inc=inc)

    def act(out, in_, func, waits=(), inc="sACT", scale=None, bias=None):
        def fn(eng, out=out, in_=in_, func=func, scale=scale, bias=bias):
            kw = {}
            if scale is not None:
                kw["scale"] = scale
            if bias is not None:
                kw["bias"] = bias
            return nc.scalar.activation(out, in_, func, **kw)
        return S.emit("scalar", fn, waits, inc=inc)

    def vcopy(out, in_, waits=(), inc="sDVE", eng="vector"):
        def fn(e, out=out, in_=in_, eng=eng):
            return getattr(nc, eng).tensor_copy(out, in_)
        return S.emit(eng, fn, waits, inc=inc)

    def vtt(out, a, b, op, waits=(), inc="sDVE"):
        def fn(e, out=out, a=a, b=b, op=op):
            return nc.vector.tensor_tensor(out, a, b, op)
        return S.emit("vector", fn, waits, inc=inc)

    def vts(out, in_, s1, s2, op0, op1=None, waits=(), inc="sDVE"):
        def fn(e, out=out, in_=in_, s1=s1, s2=s2, op0=op0, op1=op1):
            if op1 is None:
                return nc.vector.tensor_scalar(out, in_, s1, None, op0)
            return nc.vector.tensor_scalar(out, in_, s1, s2, op0, op1)
        return S.emit("vector", fn, waits, inc=inc)

    def vstt(out, in0, scal, in1, op0, op1, waits=(), inc="sDVE"):
        def fn(e, out=out, in0=in0, scal=scal, in1=in1, op0=op0, op1=op1):
            return nc.vector.scalar_tensor_tensor(out, in0, scal, in1, op0, op1)
        return S.emit("vector", fn, waits, inc=inc)

    tc0 = dma("sync", IDN[:], didn[:])
    dma("sync", IDNF[:], didnf[:])
    dma("sync", ONESC[:], dconc[:])
    dma("sync", ONESRF[:], dconrf[:])
    dma("sync", DIF[:], ddif[:])
    dma("sync", REV[:], drev[:])
    dma("sync", IOT[:], diot[:])
    dma("sync", C128[:], dc128[:])
    dma("sync", TD8[:], dtd.rearrange("(c h2) d -> (h2 d) c", c=8, h2=2), slow=True)
    dma("sync", UC[:], dtf.rearrange("(c h2) d -> (h2 d) c", c=8, h2=2), slow=True)
    dma("sync", TDR[:], dtd[:, 0:1].rearrange("h one -> one h"), slow=True)
    dma("sync", B1R[:], db1.rearrange("(c p) -> c p", p=128))
    tcon = dma("sync", B2R[:], db2.rearrange("(c p) -> c p", p=128))

    regs = [(0, 0), (1, 0), (2, 0), (3, 0), (4, 0), (5, 0)]
    t_pre = dma("sync", OUT[:, 0:1024].rearrange("p (pc t) -> p pc t", t=CH),
                dxpre.rearrange("(pc p) t -> p pc t", p=128))
    t_pxb = act(YF[:, 0:1024], OUT[:, 0:1024], AF.Identity, waits=(t_pre,))
    t_x0a = dma("sync", XT[:, 0:1024].rearrange("p (pc t) -> p pc t", t=T),
                dxT[0:512, 0:T].rearrange("(pc p) t -> p pc t", p=128))
    t_xb0a = act(XBF[:, 0:1024], XT[:, 0:1024], AF.Identity, waits=(t_x0a,))
    t_x0b = dma("sync", XT[:, 1024:2048].rearrange("p (pc t) -> p pc t", t=T),
                dxT[512:1024, 0:T].rearrange("(pc p) t -> p pc t", p=128))
    t_xb0b = act(XBF[:, 1024:2048], XT[:, 1024:2048], AF.Identity,
                 waits=(t_x0b,))
    msrc0 = dmsk[:, 0:T].rearrange("(pc p) t -> p pc t", p=128)
    t_m0a = dma("sync", MSK[:].rearrange("p (pc t) -> p pc t", t=T),
                msrc0[:, 0:4, :], sem="SM")

    HQF = HQ.bitcast(F32)
    stage3 = [OUT[:, 0:1024], HQF[:, 0:1024], HQF[:, 1024:2048]]
    q_free = {0: t_pxb}
    w_toks = []
    for wi, (dsrc, wdst) in enumerate([(dwk, WKs), (dwv, WVs), (dwr, WRs), (dwo, WOs)]):
        for kc in range(PC):
            sl = (wi * PC + kc) % 3
            t_ld = dma("sync", stage3[sl], dsrc[128 * kc:128 * (kc + 1), :],
                       waits=(q_free.get(sl),))
            t_c = vcopy(wdst[:, kc * 1024:(kc + 1) * 1024], stage3[sl], waits=(t_ld,))
            q_free[sl] = t_c
        w_toks.append(t_c)
    wk_tok = w_toks[1]

    t_lnw = act(LNW[:], TD8[:], AF.Ln, waits=(tcon,))
    t_lnr = act(STATS[0:1, 0:16], TDR[:], AF.Ln, waits=(tcon,))
    t_lb = mm(PB[6][:, 0:16], ONESRF[0:1, :], STATS[0:1, 0:16], True, True,
              waits=(t_lnr,), inc="sPE")
    t_lnwb = vcopy(LNWB[:], PB[6][:, 0:16], waits=(t_lb,))
    S.emit("vector", lambda e: nc.vector.memset(STATS[:, 0:2], 1.0), (t_lb,), inc="sDVE")
    vcopy(ONES2R[:], STATS[:, 0:2])
    tb_toks = []
    for h in range(H):
        tb_toks.append(act(TB[:, h * 128:(h + 1) * 128], DIF[:], AF.Exp,
                           scale=LNWB[:, h:h + 1], waits=(t_lnwb,)))
    tk_toks, trt_toks, wc_toks = [], [], []
    for p in range(PC):
        tk_toks.append(act(TK[:, p * 128:(p + 1) * 128], REV[:], AF.Exp,
                           scale=LNW[:, p:p + 1], waits=(t_lnw,)))
        trt_toks.append(act(TRt[:, p * 128:(p + 1) * 128], IOT[:], AF.Exp,
                            scale=LNW[:, p:p + 1], waits=(t_lnw,)))
        wc_toks.append(act(WC[:, p:p + 1], C128[:], AF.Exp,
                           scale=LNW[:, p:p + 1], waits=(t_lnw,)))

    t_tkt = None
    for p in range(PC):
        t_tkt = tr(PB7B[:, p * 128:(p + 1) * 128], TK[:, p * 128:(p + 1) * 128],
                   IDN[:], waits=(tk_toks[p], tc0),
                   inc="sPE" if p == PC - 1 else None)
    t_tkt = vcopy(TKT[:], PB7B[:, 0:1024], waits=(t_tkt,))

    t_b1t = tr(PB[6][0:128, 16:48], B1R[:], IDNF[0:32, 0:32], waits=(tcon, t_lnwb), inc="sPE")
    t_b1e = vcopy(B1C[:], PB[6][:, 16:48], waits=(t_b1t,))
    t_e = vcopy(B1Q[:], B1C[:], waits=(t_b1e,))
    t_b1c = vcopy(B1C[:], B1Q[:], waits=(t_e,))
    t_b2t = tr(PB[6][0:128, 48:56], B2R[:], IDNF[0:8, 0:8], waits=(t_b1e,), inc="sPE")
    t_e = vcopy(B2C[:], PB[6][:, 48:56], waits=(t_b2t,))
    t_e = vcopy(B2Q[:], B2C[:], waits=(t_e,))
    t_b2c = vcopy(B2C[:], B2Q[:], waits=(t_e,))

    fi = 0
    w1_done, w2_done = [], []
    for kc in range(PC):
        for hh in range(4):
            sl = fi % 3
            t_ld = dma("sync", stage3[sl],
                       dw1[128 * kc:128 * (kc + 1), 1024 * hh:1024 * (hh + 1)],
                       waits=(q_free.get(sl),))
            dst = W1S[:, kc * 4096 + 1024 * hh: kc * 4096 + 1024 * (hh + 1)]
            t_c = vcopy(dst, stage3[sl], waits=(t_ld,), inc="sPOOL",
                        eng="gpsimd")
            q_free[sl] = t_c
            w1_done.append(t_c)
            fi += 1
    t_m0b = dma("sync", MSK2[:].rearrange("p (pc t) -> p pc t", t=T),
                msrc0[:, 4:8, :], waits=(t_tkt,), sem="SM")
    for j in range(H):
        srcj = dw2[256 * j:256 * (j + 1), :].rearrange("(j2 p) o -> p j2 o", p=128)
        for hh in range(2):
            sl = fi % 3
            t_ld = dma("sync",
                       stage3[sl].rearrange("p (j2 o) -> p j2 o", o=1024),
                       srcj[:, hh:hh + 1, :],
                       waits=(q_free.get(sl),))
            dst = W2S[:, j * 2048 + 1024 * hh: j * 2048 + 1024 * (hh + 1)]
            t_c = vcopy(dst, stage3[sl], waits=(t_ld,), inc="sPOOL",
                        eng="gpsimd")
            q_free[sl] = t_c
            w2_done.append(t_c)
            fi += 1
    stage_done = dict(q_free)

    reg_free = {}
    pre_kp = []
    for mc in range(PC):
        b, off = regs[mc % 6]
        tok = None
        for kc in range(PC):
            tok = mm(PB[b][:, off:off + 128],
                     WKs[:, kc * 1024 + mc * 128: kc * 1024 + mc * 128 + 128],
                     YF[:, kc * 128:(kc + 1) * 128],
                     start=(kc == 0), stop=(kc == PC - 1),
                     waits=(t_pxb, wk_tok, reg_free.get(mc % 6)),
                     inc="sPE" if kc == PC - 1 else None)
        t_ev = act(KP[:, mc * 128:(mc + 1) * 128], PB[b][:, off:off + 128],
                   AF.Identity, scale=UC[:, mc:mc + 1], waits=(tok,))
        reg_free[mc % 6] = t_ev
        pre_kp.append(t_ev)
    pre_v = []
    for eh in range(2):
        tok = None
        for kc in range(PC):
            tok = mm(PB[eh][:, 0:512],
                     YF[:, kc * 128:(kc + 1) * 128],
                     WVs[:, kc * 1024 + eh * 512: kc * 1024 + (eh + 1) * 512],
                     start=(kc == 0), stop=(kc == PC - 1),
                     waits=(reg_free.get(eh),),
                     inc="sPE" if kc == PC - 1 else None)
        pre_v.append(vcopy(VT[:, eh * 512:(eh + 1) * 512], PB[eh][:, 0:512],
                           waits=(tok,)))
    reg_free[0] = pre_v[0]
    reg_free[1] = pre_v[1]
    t_tp = None
    for p in range(PC):
        t_tp = tr(PB7B[:, p * 128:(p + 1) * 128], KP[:, p * 128:(p + 1) * 128],
                  IDN[:], waits=(pre_kp[p], t_tkt if p == 0 else None),
                  inc="sPE" if p == PC - 1 else None)
    t_kht = vtt(KHT[:], PB7B[:, 0:1024], TKT[:], ALU.mult, waits=(t_tp,))
    pre_m = []
    m_last = None
    for p in range(PC):
        mm(PB[4][0:64, p * 64:(p + 1) * 64], KHT[:, p * 128:p * 128 + 64],
           VT[:, 128 * p: 128 * p + 64], True, True,
           waits=(t_kht, pre_v[0], pre_v[1]) + ((t_b2c,) if p == 0 else ()))
        m_last = mm(PB[4][64:128, p * 64:(p + 1) * 64],
                    KHT[:, p * 128 + 64:p * 128 + 128],
                    VT[:, 128 * p + 64: 128 * p + 128], True, True,
                    inc="sPE" if p == PC - 1 else None)
    for p in range(PC):
        pre_m.append(vcopy(St[:, p * 64:(p + 1) * 64],
                           PB[4][:, p * 64:(p + 1) * 64], waits=(m_last,)))

    tok_xfree1 = None
    tok_xfree2 = None
    tok_vt_free = m_last
    tok_kp_done = t_tp
    tok_rs_done = None
    tok_xbf_done = pre_v[1]
    tok_kht_free = None
    tok_rt_free = None
    tok_x2q_done = None
    tok_s_upd = pre_m[-1]
    tok_pb7_free = t_kht
    tok_pb6_free = t_b2c
    tok_abr_free = None
    tok_tmp1_free = None
    tok_stat_prev = None
    tok_kht_ln = None
    at_free = None
    tok_hq_free = None
    prev_t_x2q = None
    prev_store = (stage_done.get(0), stage_done.get(1), stage_done.get(2))
    g_free = {}
    y_free = {4: pre_m[-1]}
    last_v_ev = pre_v[1]
    gk = 0

    def emit_front_K(i, pre=None):
        nonlocal tok_kp_done, tok_xbf_done, tok_xfree1, tok_xfree2
        t0 = i * T
        if pre is not None:
            t_xb1, t_xb2 = pre
        else:
            srcA = dxT[0:512, t0:t0 + T].rearrange("(pc p) t -> p pc t", p=128)
            srcB = dxT[512:1024, t0:t0 + T].rearrange("(pc p) t -> p pc t",
                                                      p=128)
            t_x1 = dma("sync", XT[:, 0:1024].rearrange("p (pc t) -> p pc t",
                                                       t=T),
                       srcA, waits=(tok_xfree1,)
                       if not isinstance(tok_xfree1, tuple)
                       else tok_xfree1, sem="SX")
            t_xb1 = act(XBF[:, 0:1024], XT[:, 0:1024], AF.Identity,
                        waits=(t_x1, tok_xbf_done))
            t_x2 = dma("sync", XT[:, 1024:2048].rearrange("p (pc t) -> p pc t",
                                                          t=T),
                       srcB, waits=(tok_xfree2,)
                       if not isinstance(tok_xfree2, tuple)
                       else tok_xfree2, sem="SX")
            t_xb2 = act(XBF[:, 1024:2048], XT[:, 1024:2048], AF.Identity,
                        waits=(t_x2,))
        t_xbf = (t_xb1, t_xb2)
        kp_ev = []
        for mc in range(PC):
            b, off = regs[mc % 6]
            tok = None
            for kc in range(PC):
                tok = mm(PB[b][:, off:off + 256],
                         WKs[:, kc * 1024 + mc * 128: kc * 1024 + mc * 128 + 128],
                         XBF[:, kc * 256:(kc + 1) * 256],
                         start=(kc == 0), stop=(kc == PC - 1),
                         waits=(t_xb1 if kc < 4 else t_xb2,
                                reg_free.get(mc % 6),
                                (g_free.get(b), y_free.get(b)) if kc == 0 else None),
                         inc="sPE" if kc == PC - 1 else None)
            t_ev = act(KP[:, mc * 256:(mc + 1) * 256], PB[b][:, off:off + 256],
                       AF.Identity, scale=UC[:, mc:mc + 1],
                       waits=(tok, tok_kp_done if mc == 0 else None))
            reg_free[mc % 6] = t_ev
            kp_ev.append(t_ev)
        return t_xbf, kp_ev

    def emit_front_R(i, t_xbf, extra=None):
        nonlocal tok_rs_done
        rs_ev = []
        for mc in range(PC):
            b, off = regs[mc % 6]
            tok = None
            for kc in range(PC):
                tok = mm(PB[b][:, off:off + 256],
                         WRs[:, kc * 1024 + mc * 128: kc * 1024 + mc * 128 + 128],
                         XBF[:, kc * 256:(kc + 1) * 256],
                         start=(kc == 0), stop=(kc == PC - 1),
                         waits=(t_xbf[0 if kc < 4 else 1],
                                reg_free.get(mc % 6),
                                extra if mc == 0 and kc == 0 else None),
                         inc="sPE" if kc == PC - 1 else None)
            t_ev = act(RS[:, mc * 256:(mc + 1) * 256], PB[b][:, off:off + 256],
                       AF.Sigmoid, waits=(tok, tok_rs_done if mc == 0 else None))
            reg_free[mc % 6] = t_ev
            rs_ev.append(t_ev)
        return rs_ev

    def emit_front(i, pre=None, extra_r=None):
        t_xbf, kp_ev = emit_front_K(i, pre=pre)
        rs_ev = emit_front_R(i, t_xbf, extra=extra_r)
        return t_xbf, kp_ev, rs_ev

    def emit_mask(i, w1, w2):
        msrc = dmsk[:, i * T:(i + 1) * T].rearrange("(pc p) t -> p pc t", p=128)
        ta = dma("sync", MSK[:].rearrange("p (pc t) -> p pc t", t=T),
                 msrc[:, 0:4, :], waits=(w1,), sem="SM")
        tb = dma("sync", MSK2[:].rearrange("p (pc t) -> p pc t", t=T),
                 msrc[:, 4:8, :], waits=(w2,), sem="SM")
        return ta, tb

    def emit_mixer_V(c):
        nonlocal tok_vt_free, last_v_ev
        v_ev_c = None
        for eh in range(2):
            tok = None
            for kc in range(PC):
                tok = mm(PB[eh][:, 0:512],
                         XBF[:, kc * 256 + c * 128: kc * 256 + c * 128 + 128],
                         WVs[:, kc * 1024 + eh * 512: kc * 1024 + (eh + 1) * 512],
                         start=(kc == 0), stop=(kc == PC - 1),
                         waits=(reg_free.get(eh),
                                (g_free.get(eh), y_free.get(eh)) if kc == 0 else None),
                         inc="sPE" if kc == PC - 1 else None)
            t_ev = act(VT[:, eh * 512:(eh + 1) * 512], PB[eh][:, 0:512],
                       AF.Identity,
                       waits=(tok,) + ((tok_vt_free,) if eh == 0 else ()))
            reg_free[eh] = t_ev
            if eh == 1:
                v_ev_c = t_ev
        last_v_ev = v_ev_c
        return v_ev_c

    def emit_mixer_T(c, kp_ev):
        nonlocal tok_pb7_free, tok_kht_free, tok_kp_done
        t_tp = None
        for p in range(PC):
            t_tp = tr(PB7B[:, p * 128:(p + 1) * 128],
                      KP[:, p * 256 + c * 128: p * 256 + c * 128 + 128], IDN[:],
                      waits=(kp_ev[p], tok_pb7_free if p == 0 else None),
                      inc="sPE" if p == PC - 1 else None)
        t_kht = vtt(KHT[:], PB7B[:, 0:1024], TKT[:], ALU.mult,
                    waits=(t_tp, tok_kht_free, tok_kht_ln))
        tok_pb7_free = t_kht
        if c == 1:
            tok_kp_done = t_kht
        return t_kht

    def emit_mixer_A(c, kp_ev):
        v_ev_c = emit_mixer_V(c)
        t_kht = emit_mixer_T(c, kp_ev)
        return v_ev_c, t_kht

    def emit_rt(c, rs_ev):
        nonlocal tok_rt_free, tok_rs_done
        rt_toks = []
        for p in range(PC):
            rt_toks.append(vtt(RT[:, p * 128:(p + 1) * 128],
                               RS[:, p * 256 + c * 128: p * 256 + c * 128 + 128],
                               TRt[:, p * 128:(p + 1) * 128], ALU.mult,
                               waits=(rs_ev[p], trt_toks[p],
                                      tok_rt_free if p == 0 else None)))
        if c == 1:
            tok_rs_done = rt_toks[-1]
        return rt_toks

    def emit_mixer_B(c, kp_ev, rs_ev, v_ev_c, t_kht, rt_toks=None):
        nonlocal tok_s_upd, tok_kht_free, tok_rt_free, tok_vt_free, at_free, gk
        scur = gk % 2
        snxt = 1 - scur
        if rt_toks is None:
            rt_toks = emit_rt(c, rs_ev)
        g_toks = {}
        bank_last = {}
        for wv in range(2):
            for pidx in range(4):
                p = wv * 4 + pidx
                for hh in range(2):
                    bank = (2 + hh) if wv == 0 else hh
                    off = pidx * 128
                    w = [kp_ev[p], rs_ev[p]]
                    if pidx == 0:
                        w += [g_free.get(bank), reg_free.get(bank),
                              y_free.get(bank)]
                    g_toks[(p, hh)] = mm(
                        PB[bank][:, off: off + 128],
                        KP[64 * hh:64 * hh + 64,
                           p * 256 + c * 128: p * 256 + c * 128 + 128],
                        RS[64 * hh:64 * hh + 64,
                           p * 256 + c * 128: p * 256 + c * 128 + 128],
                        True, True, waits=tuple(w), inc="sPE")
                    bank_last[bank] = g_toks[(p, hh)]
        for wv in range(2):
            ybank = 4 + wv
            for pidx in range(4):
                p = wv * 4 + pidx
                off = pidx * 128
                for hh in range(2):
                    mm(PB[ybank][64 * hh:64 * hh + 64, off:off + 128],
                       St[64 * hh:64 * hh + 64,
                          scur * 512 + p * 64: scur * 512 + (p + 1) * 64],
                       RT[64 * hh:64 * hh + 64, p * 128:(p + 1) * 128],
                       start=(pidx == 0), stop=False,
                       waits=(tok_s_upd, rt_toks[p],
                              (y_free.get(ybank), g_free.get(ybank),
                               reg_free.get(ybank))
                              if pidx == 0 and hh == 0 else None),
                       skip=True)
        at_toks = {}
        for wv in range(2):
            for pidx in range(4):
                p = wv * 4 + pidx
                for hh in range(2):
                    bank = (2 + hh) if wv == 0 else hh
                    off = pidx * 128
                    h = 2 * p + hh
                    sl = wv * 8 + 2 * pidx + hh
                    at = vtt(AT[:, sl * 128:(sl + 1) * 128],
                             PB[bank][:, off: off + 128],
                             TB[:, h * 128:(h + 1) * 128], ALU.mult,
                             waits=(bank_last[bank], tb_toks[h],
                                    at_free if sl == 0 else None))
                    at_toks[h] = at
                    g_free[bank] = at
        y_ev = []
        yev_w1 = None
        for wv in range(2):
            ybank = 4 + wv
            ybank_last = None
            for pidx in range(4):
                p = wv * 4 + pidx
                off = pidx * 128
                for hh in range(2):
                    h = 2 * p + hh
                    sl = wv * 8 + 2 * pidx + hh
                    ybank_last = mm(
                        PB[ybank][64 * hh:64 * hh + 64, off:off + 128],
                        VT[:, p * 128 + 64 * hh: p * 128 + 64 * hh + 64],
                        AT[:, sl * 128:(sl + 1) * 128],
                        start=False, stop=True,
                        waits=(v_ev_c, at_toks[h]), inc="sPE", skip=True)
            for pidx in range(4):
                p = wv * 4 + pidx
                off = pidx * 128
                ev = act(YF[:, p * 256 + c * 128: p * 256 + c * 128 + 128],
                         PB[ybank][:, off:off + 128], AF.Identity,
                         waits=(ybank_last,))
                y_free[ybank] = ev
                y_ev.append(ev)
            at_free = ybank_last
            if wv == 0:
                yev_w1 = y_ev[3]
        m_last = None
        for p in range(PC):
            mm(PB[4][0:64, p * 64:(p + 1) * 64], KHT[:, p * 128:p * 128 + 64],
               VT[:, 128 * p: 128 * p + 64], True, True,
               waits=(t_kht, v_ev_c) + ((yev_w1,) if p == 0 else ()))
            m_last = mm(PB[4][64:128, p * 64:(p + 1) * 64],
                        KHT[:, p * 128 + 64:p * 128 + 128],
                        VT[:, 128 * p + 64: 128 * p + 128],
                        True, True, inc="sPE" if p == PC - 1 else None)
        s_upd = []
        for p in range(PC):
            s_upd.append(vstt(St[:, snxt * 512 + p * 64: snxt * 512 + (p + 1) * 64],
                              St[:, scur * 512 + p * 64: scur * 512 + (p + 1) * 64],
                              WC[:, p:p + 1], PB[4][:, p * 64:(p + 1) * 64],
                              ALU.mult, ALU.add, waits=(m_last, wc_toks[p])))
        tok_s_upd = s_upd[-1]
        tok_kht_free = s_upd[-1]
        tok_rt_free = y_ev[-1]
        y_free[4] = s_upd[-1]
        tok_vt_free = m_last
        gk += 1
        return y_ev

    def emit_mixer_chunk(c, kp_ev, rs_ev):
        v_ev_c, t_kht = emit_mixer_A(c, kp_ev)
        return emit_mixer_B(c, kp_ev, rs_ev, v_ev_c, t_kht)

    def emit_attn_mms(y_ev_all, lo, hi, extra=None):
        toks = []
        for mc in range(lo, hi):
            b, off = regs[mc % 6]
            tok = None
            for kc in range(PC):
                tok = mm(PB[b][:, off:off + 256],
                         WOs[:, kc * 1024 + mc * 128: kc * 1024 + mc * 128 + 128],
                         YF[:, kc * 256:(kc + 1) * 256],
                         start=(kc == 0), stop=(kc == PC - 1),
                         waits=(y_ev_all[kc], y_ev_all[8 + kc],
                                reg_free.get(mc % 6),
                                extra if mc == lo and kc == 0 else None,
                                (g_free.get(b), y_free.get(b)) if kc == 0 else None),
                         inc="sPE" if kc == PC - 1 else None)
            toks.append((mc, tok))
        return toks

    def emit_attn_evacs(i, mm_toks):
        nonlocal tok_xfree1, tok_xfree2
        t_xbf_i = fronts[i][0]
        xr_toks = []
        for mc, tok in mm_toks:
            b, off = regs[mc % 6]
            t_ev = vtt(XR[:, mc * 256:(mc + 1) * 256],
                       XT[:, mc * 256:(mc + 1) * 256],
                       PB[b][:, off:off + 256], ALU.add, waits=(tok,))
            reg_free[mc % 6] = t_ev
            xr_toks.append(t_ev)
            if mc == 3:
                tok_xfree1 = (t_ev, t_xbf_i[0])
            elif mc == PC - 1:
                tok_xfree2 = (t_ev, t_xbf_i[1])
        return xr_toks

    def emit_attn(i, y_ev_all, extra=None):
        xr_toks = []
        for mc in range(PC):
            xr_toks += emit_attn_evacs(
                i, emit_attn_mms(y_ev_all, mc, mc + 1,
                                 extra=extra if mc == 0 else None))
        return xr_toks

    def ln_stats_make():
        nonlocal tok_pb6_free, tok_stat_prev, tok_kht_ln
        st = {"sq_free": [tok_stat_prev, None], "t": None}
        pb6w = (tok_pb6_free, tok_stat_prev)
        def piece(p, xr_tok):
            first = (p == 0)
            for th in range(2):
                mm(PB[6][:, 2 * th:2 * th + 2],
                   XR[:, p * 256 + th * 128: p * 256 + th * 128 + 128],
                   ONES2R[:, 0:2], start=first and th == 0, stop=(p == PC - 1),
                   waits=(xr_tok,) + (pb6w if first and th == 0 else ()),
                   skip=True)
            sq = XSQ[:, (p % 2) * 256:(p % 2) * 256 + 256]
            t_sq = vtt(sq, XR[:, p * 256:(p + 1) * 256],
                       XR[:, p * 256:(p + 1) * 256], ALU.mult,
                       waits=(xr_tok, st["sq_free"][p % 2],
                              wc_toks[-1] if p < 2 else None))
            t_stat = None
            for th in range(2):
                t_stat = mm(PB[6][:, 4 + 2 * th:6 + 2 * th],
                            sq[:, th * 128:(th + 1) * 128],
                            ONES2R[:, 0:2],
                            start=(p == 0 and th == 0), stop=(p == PC - 1),
                            waits=(t_sq,), inc="sPE", skip=True)
            st["sq_free"][p % 2] = t_stat
            st["t"] = t_stat
            return t_stat
        def fin():
            nonlocal tok_stat_prev, tok_kht_ln
            tok_stat_prev = st["t"]
            tok_kht_ln = st["t"]
            return st["t"]
        return piece, fin

    def ln_stats(xr_toks):
        nonlocal tok_pb6_free, tok_stat_prev, tok_kht_ln
        first = True
        for p in range(PC):
            for th in range(2):
                mm(PB[6][:, 2 * th:2 * th + 2],
                   XR[:, p * 256 + th * 128: p * 256 + th * 128 + 128],
                   ONES2R[:, 0:2], start=first, stop=(p == PC - 1),
                   waits=(xr_toks[p],) + ((tok_pb6_free, tok_stat_prev) if first else ()),
                   skip=True)
                first = False
        t_stat = None
        sq_free = [tok_stat_prev, None]
        for p in range(PC):
            sq = XSQ[:, (p % 2) * 256:(p % 2) * 256 + 256]
            t_sq = vtt(sq, XR[:, p * 256:(p + 1) * 256],
                       XR[:, p * 256:(p + 1) * 256], ALU.mult,
                       waits=(xr_toks[p], sq_free[p % 2],
                              wc_toks[-1] if p < 2 else None))
            for th in range(2):
                t_stat = mm(PB[6][:, 4 + 2 * th:6 + 2 * th],
                            sq[:, th * 128:(th + 1) * 128],
                            ONES2R[:, 0:2],
                            start=(p == 0 and th == 0), stop=(p == PC - 1),
                            waits=(t_sq,), inc="sPE", skip=True)
            sq_free[p % 2] = t_stat
        tok_stat_prev = t_stat
        tok_kht_ln = t_stat
        return t_stat

    def ln_chain(t_stat):
        vts(STATS[:, 0:4], PB[6][:, 0:4], 1.0 / D, None, ALU.mult, waits=(t_stat,))
        vts(STATS[:, 4:8], PB[6][:, 4:8], 1.0 / D, None, ALU.mult)
        vtt(STATS[:, 8:12], STATS[:, 0:4], STATS[:, 0:4], ALU.mult)
        vtt(STATS[:, 8:12], STATS[:, 4:8], STATS[:, 8:12], ALU.subtract)
        vts(STATS[:, 8:12], STATS[:, 8:12], EPS, None, ALU.add)
        vcopy(STATS[:, 12:16], STATS[:, 8:12].bitcast(I32))
        vts(STATS[:, 12:16], STATS[:, 12:16], -0.5, MAGIC, ALU.mult, ALU.add)
        vcopy(STATI[:, 0:4], STATS[:, 12:16])
        Y0 = STATI.bitcast(F32)
        for _ in range(2):
            vtt(STATS[:, 12:16], Y0[:, 0:4], Y0[:, 0:4], ALU.mult)
            vtt(STATS[:, 12:16], STATS[:, 12:16], STATS[:, 8:12], ALU.mult)
            vts(STATS[:, 12:16], STATS[:, 12:16], -0.5, 1.5, ALU.mult, ALU.add)
            vtt(Y0[:, 0:4], Y0[:, 0:4], STATS[:, 12:16], ALU.mult)
        vtt(STATS[:, 4:8], STATS[:, 0:4], Y0[:, 0:4], ALU.mult)
        t_pk = vts(STATS[:, 0:4], STATS[:, 4:8], -1.0, None, ALU.mult)
        return t_pk

    def ln_tr(t_pk):
        t_tr = None
        Y0F = STATI.bitcast(F32)
        srcs = [Y0F[:, 0:1], Y0F[:, 2:3], STATS[:, 0:1], STATS[:, 2:3]]
        for r in range(4):
            t_tr = tr(PB[6][0:1, r * 128:(r + 1) * 128],
                      srcs[r], IDNF[:],
                      waits=(t_pk,), inc="sPE" if r == 3 else None)
        return t_tr

    def ln_bc(t_tr):
        nonlocal tok_abr_free
        t_abr = vcopy(ABRROW[0:1, :], PB[6][0:1, 0:512],
                      waits=(t_tr, tok_abr_free))
        mm(PB[6][:, 256:512], ONESRF[0:1, :], ABRROW[0:1, 0:256], True, True,
           waits=(t_abr,), skip=True)
        t_bc = mm(PB[6][:, 0:256], ONESRF[0:1, :], ABRROW[0:1, 256:512],
                  True, True, inc="sPE", skip=True)
        tok_abr_free = t_bc
        return t_bc

    def ln_bcast(t_pk):
        return ln_bc(ln_tr(t_pk))

    def ln_apply(t_bc, out_writer):
        nonlocal tok_pb6_free, tok_tmp1_free
        out_toks = []
        for p in range(PC):
            vtt(TMP1[:, 0:256], XR[:, p * 256:(p + 1) * 256],
                PB[6][:, 256:512], ALU.mult,
                waits=(t_bc, tok_tmp1_free if p == 0 else None))
            tok_o = out_writer(p)
            tok_tmp1_free = tok_o
            out_toks.append(tok_o)
        tok_pb6_free = out_toks[-1]
        return out_toks

    def ln_bcast_apply(t_pk, out_writer):
        return ln_apply(ln_bcast(t_pk), out_writer)

    fronts = {0: emit_front(0, pre=(t_xb0a, t_xb0b), extra_r=w_toks[2])}
    t_m, t_m2 = t_m0a, t_m0b
    y_all = emit_mixer_chunk(0, fronts[0][1], fronts[0][2])
    y_all += emit_mixer_chunk(1, fronts[0][1], fronts[0][2])
    xr_toks = emit_attn(0, y_all, extra=w_toks[3])

    for i in range(nt):
        t0 = i * T
        nxt = i + 1 < nt

        t_stat1 = ln_stats(xr_toks)
        if nxt:
            t_xbf_n, kp_n = emit_front_K(i + 1)
        t_pk1 = ln_chain(t_stat1)
        t_tr1 = ln_tr(t_pk1)
        if nxt:
            v0 = emit_mixer_V(0)
        t_bc1 = ln_bc(t_tr1)
        if nxt:
            kht0 = emit_mixer_T(0, kp_n)

        def ln1_writer(p):
            return vtt(X2B[:, p * 256:(p + 1) * 256], TMP1[:, 0:256],
                       PB[6][:, 0:256], ALU.add,
                       waits=((prev_t_x2q,) if p == 0 and prev_t_x2q else ()))
        x2_toks = ln_apply(t_bc1, ln1_writer)
        t_x2q = None
        for p in range(PC):
            t_x2q = act(X2Q[:, p * 256:(p + 1) * 256],
                        X2B[:, p * 256:(p + 1) * 256], AF.Identity,
                        waits=(x2_toks[p],) + ((tok_x2q_done,) if p == 0 else ()))
        prev_t_x2q = t_x2q
        if nxt:
            rs_n = emit_front_R(i + 1, t_xbf_n)
            y_next = emit_mixer_B(0, kp_n, rs_n, v0, kht0)

        hq_toks = []
        h16_free = {0: None, 1: None}
        for mc in range(FC):
            b, off = regs[mc % 6]
            tok = None
            for kc in range(PC):
                tok = mm(PB[b][:, off:off + 256],
                         W1S[:, kc * 4096 + mc * 128: kc * 4096 + mc * 128 + 128],
                         X2Q[:, kc * 256:(kc + 1) * 256],
                         start=(kc == 0), stop=(kc == PC - 1),
                         waits=(t_x2q, reg_free.get(mc % 6),
                                (g_free.get(b), y_free.get(b)) if kc == 0 else None)
                         + (tuple(w1_done) if i == 0 and mc == 0 else ()),
                         inc="sPE" if kc == PC - 1 else None)
            hb = (H16 if mc % 2 == 0 else H16B)[:, 0:256]
            t_h = act(hb, PB[b][:, off:off + 256], AF.Identity,
                      bias=B1C[:, mc:mc + 1],
                      waits=(tok, h16_free[mc % 2],
                             t_b1c if i == 0 and mc == 0 else None))
            t_q = vts(HQ[:, mc * 256:(mc + 1) * 256], hb, 0.0, None, ALU.max,
                      waits=(t_h, tok_hq_free if mc == 0 else None)
                      + (tuple(w2_done) if i == 0 and mc == 0 else ()))
            h16_free[mc % 2] = t_q
            reg_free[mc % 6] = t_q
            hq_toks.append(t_q)

        if nxt:
            rt1 = emit_rt(1, rs_n)

        r2_toks = []
        for mc in range(PC):
            b, off = regs[mc % 6]
            tok = None
            for kc in range(FC):
                tok = mm(PB[b][:, off:off + 256],
                         W2S[:, kc * 1024 + mc * 128: kc * 1024 + mc * 128 + 128],
                         HQ[:, kc * 256:(kc + 1) * 256],
                         start=(kc == 0), stop=(kc == FC - 1),
                         waits=(hq_toks[kc], reg_free.get(mc % 6),
                                (g_free.get(b), y_free.get(b)) if kc == 0 else None),
                         inc="sPE" if kc == FC - 1 else None)
            hb = (H16 if mc % 2 == 0 else H16B)[:, 0:256]
            t_h = act(hb, PB[b][:, off:off + 256], AF.Identity,
                      bias=B2C[:, mc:mc + 1], waits=(tok, h16_free[mc % 2]))
            mbuf = MSK if mc < 4 else MSK2
            t_mul = vtt(TMP1[:, 0:256], hb,
                        mbuf[:, (mc % 4) * 256:((mc % 4) + 1) * 256],
                        ALU.mult, waits=(t_h, t_m if mc < 4 else t_m2,
                                         tok_tmp1_free))
            t_r2 = vtt(XR[:, mc * 256:(mc + 1) * 256], TMP1[:, 0:256],
                       X2B[:, mc * 256:(mc + 1) * 256], ALU.add,
                       waits=(t_mul,))
            tok_tmp1_free = t_r2
            h16_free[mc % 2] = t_mul
            reg_free[mc % 6] = t_r2
            r2_toks.append(t_r2)
            if mc == 3 and nxt:
                msrc_n = dmsk[:, (i + 1) * T:(i + 2) * T].rearrange(
                    "(pc p) t -> p pc t", p=128)
                t_mA = dma("sync", MSK[:].rearrange("p (pc t) -> p pc t", t=T),
                           msrc_n[:, 0:4, :], waits=(t_mul,), sem="SM")
            if mc == PC - 1 and nxt:
                msrc_n = dmsk[:, (i + 1) * T:(i + 2) * T].rearrange(
                    "(pc p) t -> p pc t", p=128)
                t_mB = dma("sync", MSK2[:].rearrange("p (pc t) -> p pc t", t=T),
                           msrc_n[:, 4:8, :], waits=(t_mul,), sem="SM")
        tok_hq_free = r2_toks[-1]
        tok_x2q_done = r2_toks[-1]
        if nxt:
            t_m, t_m2 = t_mA, t_mB

        t_stat2 = ln_stats(r2_toks)
        t_pk2 = ln_chain(t_stat2)
        if nxt:
            mA1 = emit_mixer_A(1, kp_n)
            y_next = y_next + emit_mixer_B(1, kp_n, rs_n, *mA1, rt_toks=rt1)
        t_tr2 = ln_tr(t_pk2)
        if nxt:
            fronts[i + 1] = (t_xbf_n, kp_n, rs_n)
            attn_p1 = emit_attn_mms(y_next, 0, 2)
        t_bc2 = ln_bc(t_tr2)

        ln2_state = {}

        def ln2_writer(p):
            nonlocal prev_store
            if p == 4:
                dst1 = dout[0:512, t0:t0 + T].rearrange("(pc p) t -> p pc t", p=128)
                ln2_state["st1"] = dma("gpsimd", dst1,
                                       OUT[:].rearrange("p (pc t) -> p pc t", t=T),
                                       waits=(ln2_state[3],), sem="SO")
            w = ()
            if p == 0:
                w = (prev_store,)
            elif p == 4:
                w = (ln2_state["st1"],)
            tok = vtt(OUT[:, (p % 4) * 256:((p % 4) + 1) * 256], TMP1[:, 0:256],
                      PB[6][:, 0:256], ALU.add, waits=w)
            ln2_state[p] = tok
            return tok
        o_toks = ln_apply(t_bc2, ln2_writer)
        if nxt:
            xr_toks = emit_attn_evacs(i + 1, attn_p1)
            xr_toks = xr_toks + emit_attn_evacs(
                i + 1, emit_attn_mms(y_next, 2, PC))
        dst2 = dout[512:1024, t0:t0 + T].rearrange("(pc p) t -> p pc t", p=128)
        prev_store = dma("gpsimd", dst2,
                         OUT[:].rearrange("p (pc t) -> p pc t", t=T),
                         waits=(o_toks[-1],), sem="SO")

    return nc, S


def realize(nc, S):
    import contextlib
    semh = {}
    ctx = contextlib.ExitStack()
    for s in SEMS:
        semh[s] = ctx.enter_context(nc.semaphore(s))
    block = ctx.enter_context(nc.Block())
    so_total = S.cnt.get("SO", 0)

    def make(engname):
        def body(engobj):
            for fn, waits, inc, amt in S.q[engname]:
                for (s, v) in waits:
                    engobj.wait_ge(semh[s], v)
                inst = fn(engobj)
                if inc is not None and inst is not None:
                    inst.then_inc(semh[inc], amt)
            if engname == "gpsimd":
                engobj.wait_ge(semh["SO"], so_total)
        return body

    for e in ENGINES:
        getattr(block, e)(make(e))
    ctx.close()
    return nc


def host_constants(ntok):
    import ml_dtypes
    bfd = ml_dtypes.bfloat16
    f8 = ml_dtypes.float8_e4m3
    s = np.arange(128)
    t = np.arange(128)
    dif = (t[None, :] - s[:, None]).astype(np.float32)
    dif[t[None, :] < s[:, None]] = 1e5
    rev = np.broadcast_to((127.0 - t).astype(np.float32), (128, 128)).copy()
    iot = np.broadcast_to((t + 1.0).astype(np.float32), (128, 128)).copy()
    return {
        "idn": np.eye(128, dtype=bfd),
        "idnf": np.eye(128, dtype=np.float32),
        "onesc": np.ones((128, 1), dtype=bfd),
        "onesrf": np.ones((1, 128), dtype=np.float32),
        "dif": dif,
        "rev": rev,
        "iot": iot,
        "c128": np.full((128, 1), 128.0, dtype=np.float32),
    }


NCORE = 8
NTOK = 4096


def kernel(L0_attn_x, L0_attn_Wk, L0_attn_Wv, L0_attn_Wr, L0_attn_Wo,
           L0_attn_time_decay, L0_attn_time_first,
           L0_ffn_w1, L0_ffn_b1, L0_ffn_w2, L0_ffn_b2):
    import ml_dtypes
    import jax
    from concourse.bass_utils import run_bass_kernel_spmd
    f8d = ml_dtypes.float8_e4m3

    x = np.asarray(L0_attn_x, dtype=np.float32)
    B, L, Dm = x.shape
    assert (B, L, Dm) == (4, 8192, 1024)

    cpu = jax.devices("cpu")[0]
    with jax.default_device(cpu):
        keep = np.asarray(jax.random.bernoulli(
            jax.random.key(42, impl="threefry2x32"), 0.8, (B, L, Dm)))
    maskv = np.where(keep, np.float32(1.25), np.float32(0.0))

    wkT = np.ascontiguousarray(np.asarray(L0_attn_Wk, np.float32).T)
    wvT = np.ascontiguousarray(np.asarray(L0_attn_Wv, np.float32).T)
    wrT = np.ascontiguousarray(np.asarray(L0_attn_Wr, np.float32).T)
    woT = np.ascontiguousarray(np.asarray(L0_attn_Wo, np.float32).T)
    w1T = np.ascontiguousarray(np.asarray(L0_ffn_w1, np.float32).T)
    w2T = np.ascontiguousarray(np.asarray(L0_ffn_w2, np.float32).T)
    td = np.asarray(L0_attn_time_decay, np.float32)
    tf = np.asarray(L0_attn_time_first, np.float32)
    b1 = np.asarray(L0_ffn_b1, np.float32)
    b2 = np.asarray(L0_ffn_b2, np.float32)
    con = host_constants(NTOK)
    zpre = np.zeros((Dm, CH), dtype=np.float32)

    in_maps = []
    for core in range(NCORE):
        b, hf = core // 2, core % 2
        sl = x[b, hf * NTOK:(hf + 1) * NTOK, :]
        xT = np.ascontiguousarray(sl.T)
        xpre = np.ascontiguousarray(x[b, NTOK - CH:NTOK, :].T) if hf == 1 else zpre
        mT = np.ascontiguousarray(maskv[b, hf * NTOK:(hf + 1) * NTOK, :].T).astype(f8d)
        in_maps.append(dict(
            xT=xT, xpre=xpre, wkT=wkT, wvT=wvT, wrT=wrT, woT=woT,
            w1T=w1T, w2T=w2T, td=td, tf=tf, b1=b1, b2=b2, maskT=mT, **con))

    nc, S = build_nc(NTOK)
    realize(nc, S)
    res = run_bass_kernel_spmd(nc, in_maps, core_ids=list(range(NCORE)))
    out = np.empty((B, L, Dm), dtype=np.float32)
    for core in range(NCORE):
        b, hf = core // 2, core % 2
        out[b, hf * NTOK:(hf + 1) * NTOK, :] = res.results[core]["outT"].T
    return out
